# version 15
# speedup vs baseline: 1.1095x; 1.0037x over previous
import numpy as np
import ml_dtypes
import concourse.bass as bass
import concourse.mybir as mybir
from concourse.bass_utils import run_bass_kernel_spmd
from contextlib import ExitStack

F32 = mybir.dt.float32
BF16 = mybir.dt.bfloat16
U32 = mybir.dt.uint32
AF = mybir.ActivationFunctionType
ALU = mybir.AluOpType
AX = mybir.AxisListType

NCORES = 8
B, L, D = 16, 2048, 1024
TPC = (B // NCORES) * L
NT = TPC // 128
EPS = 1e-5


class Count:
    def __init__(self, sem):
        self.sem = sem
        self.val = 0

    def inc(self, inst, k=1):
        inst.then_inc(self.sem, k)
        self.val += k
        return self.val


class Ctx:
    def __init__(self, nc, ctx):
        self.nc = nc
        self.ctx = ctx
        self.te = nc.tensor
        self.ve = nc.vector
        self.ae = nc.scalar
        self.ge = nc.gpsimd
        self.se = nc.sync
        self.dW = Count(ctx.enter_context(nc.semaphore("dW")))
        self.dX = Count(ctx.enter_context(nc.semaphore("dX")))
        self.dS = Count(ctx.enter_context(nc.semaphore("dS")))
        self.tE = Count(ctx.enter_context(nc.semaphore("tE")))
        self.vE = Count(ctx.enter_context(nc.semaphore("vE")))
        self.aE = Count(ctx.enter_context(nc.semaphore("aE")))
        self.gE = Count(ctx.enter_context(nc.semaphore("gE")))
        self.cc = Count(ctx.enter_context(nc.semaphore("cc")))

    def bump(self):
        if not hasattr(self, "_uniq"):
            self._uniq = 0
        self._uniq += 1
        return self._uniq

    def cnt(self, name):
        if not hasattr(self, "_cnts"):
            self._cnts = {}
        if name not in self._cnts:
            self._cnts[name] = Count(self.ctx.enter_context(self.nc.semaphore(name)))
        return self._cnts[name]

    def sbuf(self, name, shape, dtype):
        return self.ctx.enter_context(self.nc.sbuf_tensor(name, shape, dtype))

    def psum(self, name, shape, dtype=F32):
        return self.ctx.enter_context(self.nc.psum_tensor(name, shape, dtype))


def wait(eng, cnt: Count, tick=None):
    eng.wait_ge(cnt.sem, cnt.val if tick is None else tick)



def vse(c, inst):
    t = c.vE.inc(inst)
    c.ve.wait_ge(c.vE.sem, t)
    return t


def ase(c, inst):
    t = c.aE.inc(inst)
    c.ae.wait_ge(c.aE.sem, t)
    return t


def emit_ln_tile(c: Ctx, sx, sxc, sscr, sstat, sxln_bf, dn=D):
    ve, ae = c.ve, c.ae
    t = c.vE.inc(ve.reduce_sum(sstat[:, 0:1], sx[:], AX.X))
    ae.wait_ge(c.vE.sem, t)
    t2 = c.aE.inc(ae.activation(sstat[:, 1:2], sstat[:, 0:1], AF.Copy, scale=-1.0 / dn))
    ve.wait_ge(c.aE.sem, t2)
    t = c.vE.inc(ve.tensor_scalar(sxc[:], sx[:], sstat[:, 1:2], None, ALU.add))
    ae.wait_ge(c.vE.sem, t)
    t2 = c.aE.inc(ae.activation(sscr[:], sxc[:], AF.Square, accum_out=sstat[:, 2:3]))
    ve.wait_ge(c.aE.sem, t2)
    t = c.vE.inc(ve.tensor_scalar(sstat[:, 3:4], sstat[:, 2:3], 1.0 / dn, EPS, ALU.mult, ALU.add))
    ae.wait_ge(c.vE.sem, t)
    t2 = c.aE.inc(ae.activation(sstat[:, 0:1], sstat[:, 3:4], AF.Sqrt))
    ve.wait_ge(c.aE.sem, t2)
    t = c.vE.inc(ve.reciprocal(sstat[:, 1:2], sstat[:, 0:1]))
    ae.wait_ge(c.vE.sem, t)
    t2 = c.aE.inc(ae.mul(sxln_bf[:], sxc[:], sstat[:, 1:2]))
    return t2



def emit_l0ffn(c: Ctx, x_in, x_out, W, dbg_skip=()):
    nc, te, ve, ae, ge, se = c.nc, c.te, c.ve, c.ae, c.ge, c.se
    E = 16
    NQ = 4
    TQ = TPC // NQ
    NTQ = TQ // 128

    with ExitStack() as ps:
        _u = c.bump()
        sbuf = lambda n, s, d: ps.enter_context(nc.sbuf_tensor(f"{n}_{_u}", s, d))
        psum = lambda n, s, d=F32: ps.enter_context(nc.psum_tensor(f"{n}_{_u}", s, d))

        idb = sbuf("ff_idb", [128, 128], BF16)
        ones1 = sbuf("ff_ones1", [1, 128], F32)
        gw_sb = sbuf("ff_gw", [128, 8 * 16], BF16)
        ebias = sbuf("ff_ebias", [128, 16], F32)
        xlnT = sbuf("ff_xlnT", [128, 8 * TQ], BF16)
        comb_q = sbuf("ff_comb", [128, 8 * 16], F32)
        yacc = sbuf("ff_yacc", [128, NTQ * 1024], F32)
        sx = [sbuf(f"ff_sx{i}", [128, 1024], F32) for i in range(2)]
        sxc = sbuf("ff_sxc", [128, 1024], F32)
        sscr = sbuf("ff_sscr", [128, 1024], F32)
        sstat = sbuf("ff_sstat", [128, 4], F32)
        sxln = sbuf("ff_sxln", [128, 1024], BF16)
        sroute = sbuf("ff_sroute", [128, 64], F32)
        h_sb = sbuf("ff_h", [128, 8 * 512], BF16)
        sg_sb = [sbuf(f"ff_sg{i}", [128, 512], BF16) for i in range(2)]
        tmp_sb = sbuf("ff_tmp", [128, 512], BF16)
        sout = [sbuf(f"ff_sout{i}", [128, 1024], F32) for i in range(2)]
        wg_sb = [sbuf(f"ff_wg{i}", [128, 8 * 1024], BF16) for i in range(2)]
        wu_sb = [sbuf(f"ff_wu{i}", [128, 8 * 1024], BF16) for i in range(2)]
        wd_sb = [sbuf(f"ff_wd{i}", [128, 8 * 1024], BF16) for i in range(2)]

        ptr = [psum("ff_ptr0", [128, 1024], BF16)]
        ptr = ptr + ptr
        pg = psum("ff_pg", [128, 512], F32)
        phg = [psum(f"ff_phg{i}", [128, 512], F32) for i in range(2)]
        phu = [psum(f"ff_phu{i}", [128, 512], F32) for i in range(2)]
        py = [psum(f"ff_py{i}", [128, 512], F32) for i in range(2)]

        c.dW.inc(ge.dma_start(out=idb[:], in_=W["id_bf16"]), 16)
        c.dW.inc(ge.dma_start(out=ones1[:], in_=W["ones1"]), 16)
        c.dW.inc(ge.dma_start(out=gw_sb[:], in_=W["gwT"].rearrange("(k p) e -> p k e", p=128)), 16)
        c.dW.inc(ge.dma_start(out=ebias[:], in_=W["ebias"]), 16)
        const_tick = c.dW.val

        def load_expert(e, b, war_tick):
            if war_tick:
                ge.wait_ge(c.tE.sem, war_tick)
            if e < 0:
                g, u, d_ = W["sgT"], W["suT"], W["sdT"]
            else:
                g, u, d_ = W["wgT"][e], W["wuT"][e], W["wdT"][e]
            cw = c.cnt(f"ffW{b}")
            cw.inc(ge.dma_start(out=wg_sb[b][:], in_=g.rearrange("(k p) i -> p k i", p=128)), 16)
            cw.inc(ge.dma_start(out=wu_sb[b][:], in_=u.rearrange("(k p) i -> p k i", p=128)), 16)
            cw.inc(ge.dma_start(out=wd_sb[b][:], in_=d_.rearrange("(k p) i -> p k i", p=128)), 16)
            return cw.val

        expert_list = [-1] + list(range(E))
        silu_tick = {0: 0, 1: 0}
        tt1_tick = {0: 0, 1: 0}
        hwar_tick = 0
        wticks = {0: 0, 1: 0}
        consumed = {0: 0, 1: 0}
        tr_evac = {0: 0, 1: 0}
        ln_read = {0: 0, 1: 0}
        out_done = {0: 0, 1: 0}
        yev = {0: 0, 1: 0}

        te.wait_ge(c.dW.sem, const_tick)

        for q in range(NQ):
            q0 = q * TQ
            te.wait_ge(c.vE.sem, c.vE.val)
            for i in range(NTQ):
                bb = i % 2
                buf = sx[bb]
                cx = c.cnt(f"ffX{bb}")
                if ln_read[bb]:
                    se.wait_ge(c.vE.sem, ln_read[bb])
                tk = cx.inc(se.dma_start(out=buf[:], in_=x_in[q0 + i * 128: q0 + (i + 1) * 128, :]), 16)
                ve.wait_ge(cx.sem, tk)
                ve.wait_ge(c.tE.sem, c.tE.val)
                at_ln = emit_ln_tile(c, buf, sxc, sscr, sstat, sxln)
                ln_read[bb] = c.vE.val
                te.wait_ge(c.aE.sem, at_ln)
                for k in range(8):
                    if tr_evac[0]:
                        te.wait_ge(c.aE.sem, tr_evac[0])
                    tt = c.tE.inc(te.transpose(ptr[0][:, 0:128], sxln[:, k * 128:(k + 1) * 128], idb[:]))
                    ae.wait_ge(c.tE.sem, tt)
                    tr_evac[0] = c.aE.inc(
                        ae.copy(xlnT[:, k * TQ + i * 128: k * TQ + (i + 1) * 128], ptr[0][:, 0:128]))
            lvl = int(dbg_skip[-1]) if dbg_skip and str(dbg_skip[-1]).isdigit() else 99
            for i in range(NTQ if "a1b" not in dbg_skip else 0):
                te.wait_ge(c.aE.sem, c.aE.val)
                te.wait_ge(c.vE.sem, c.vE.val)
                if lvl < 1:
                    c.vE.inc(ve.memset(comb_q[:, i * 16:(i + 1) * 16], 0.1))
                    continue
                for k in range(8):
                    mm = te.matmul(pg[:, 0:16], xlnT[:, k * TQ + i * 128: k * TQ + (i + 1) * 128],
                                   gw_sb[:, k * 16:(k + 1) * 16], start=(k == 0), stop=(k == 7))
                tt = c.tE.inc(mm)
                ae.wait_ge(c.tE.sem, tt)
                at = c.aE.inc(ae.activation(sroute[:, 0:16], pg[:, 0:16], AF.Sigmoid))
                ve.wait_ge(c.aE.sem, at)
                vse(c, ve.tensor_tensor(sroute[:, 16:32], sroute[:, 0:16], ebias[:], ALU.add))
                vse(c, ve.max(sroute[:, 32:40], sroute[:, 16:32]))
                vse(c, ve.tensor_scalar(sroute[:, 40:56], sroute[:, 16:32], sroute[:, 39:40], None, ALU.is_ge))
                vse(c, ve.tensor_tensor(sroute[:, 40:56], sroute[:, 40:56], sroute[:, 0:16], ALU.mult))
                vse(c, ve.reduce_sum(sroute[:, 56:57], sroute[:, 40:56], AX.X))
                vse(c, ve.reciprocal(sroute[:, 57:58], sroute[:, 56:57]))
                c.vE.inc(ve.tensor_scalar(comb_q[:, i * 16:(i + 1) * 16], sroute[:, 40:56], sroute[:, 57:58], 2.826, ALU.mult, ALU.mult))

            for ei, e in enumerate(expert_list if "a2" not in dbg_skip else []):
                b = ei % 2
                if ei == 0:
                    wticks[0] = load_expert(expert_list[0], 0, consumed[0])
                if ei + 1 < len(expert_list):
                    nb = (ei + 1) % 2
                    wticks[nb] = load_expert(expert_list[ei + 1], nb, consumed[nb])
                te.wait_ge(c.cnt(f"ffW{b}").sem, wticks[b])
                for tb in range(2):
                    tb0 = tb * 512
                    for ic in range(8):
                        par = ic % 2
                        if silu_tick[par]:
                            te.wait_ge(c.aE.sem, silu_tick[par])
                        if tt1_tick[par]:
                            te.wait_ge(c.vE.sem, tt1_tick[par])
                        for k in range(8):
                            mmg = te.matmul(phg[par][:], wg_sb[b][:, k * 1024 + ic * 128: k * 1024 + (ic + 1) * 128],
                                            xlnT[:, k * TQ + tb0: k * TQ + tb0 + 512], start=(k == 0), stop=(k == 7))
                        tg = c.tE.inc(mmg)
                        for k in range(8):
                            mmu = te.matmul(phu[par][:], wu_sb[b][:, k * 1024 + ic * 128: k * 1024 + (ic + 1) * 128],
                                            xlnT[:, k * TQ + tb0: k * TQ + tb0 + 512], start=(k == 0), stop=(k == 7))
                        tt = c.tE.inc(mmu)
                        ae.wait_ge(c.tE.sem, tg)
                        if tt1_tick[par]:
                            ae.wait_ge(c.vE.sem, tt1_tick[par])
                        at = c.aE.inc(ae.activation(sg_sb[par][:], phg[par][:], AF.Silu))
                        silu_tick[par] = at
                        ve.wait_ge(c.aE.sem, at)
                        ve.wait_ge(c.tE.sem, tt)
                        if ic < 2 and hwar_tick:
                            ve.wait_ge(c.tE.sem, hwar_tick)
                        vt = c.vE.inc(ve.tensor_tensor(h_sb[:, ic * 512:(ic + 1) * 512], sg_sb[par][:], phu[par][:], ALU.mult))
                        tt1_tick[par] = vt
                    te.wait_ge(c.vE.sem, tt1_tick[0])
                    te.wait_ge(c.vE.sem, tt1_tick[1])
                    for tt_i in range(4):
                        pb = tt_i % 2
                        for dh in range(2):
                            if yev[pb]:
                                te.wait_ge(c.vE.sem, yev[pb])
                            for ic in range(8):
                                mmd = te.matmul(py[pb][:],
                                                h_sb[:, ic * 512 + tt_i * 128: ic * 512 + tt_i * 128 + 128],
                                                wd_sb[b][:, ic * 1024 + dh * 512: ic * 1024 + (dh + 1) * 512],
                                                start=(ic == 0), stop=(ic == 7))
                            tk = c.tE.inc(mmd)
                            ve.wait_ge(c.tE.sem, tk)
                            col = (tb * 4 + tt_i) * 1024 + dh * 512
                            if ei == 0:
                                yev[pb] = c.vE.inc(ve.tensor_copy(yacc[:, col:col + 512], py[pb][:]))
                            else:
                                cb_col = comb_q[:, (tb * 4 + tt_i) * 16 + e: (tb * 4 + tt_i) * 16 + e + 1]
                                yev[pb] = c.vE.inc(ve.scalar_tensor_tensor(
                                    yacc[:, col:col + 512], py[pb][:], cb_col, yacc[:, col:col + 512], ALU.mult, ALU.add))
                    consumed[b] = c.tE.val
            if "dump" in dbg_skip and q == 0:
                dmp = W["dbg"]
                se.wait_ge(c.vE.sem, c.vE.val)
                se.wait_ge(c.aE.sem, c.aE.val)
                c.dS.inc(ge.dma_start(out=dmp[0:128, :], in_=xlnT[:, 0:4096]), 16)
                c.dS.inc(se.dma_start(out=dmp[128:256, 0:128], in_=comb_q[:]), 16)
                c.dS.inc(se.dma_start(out=dmp[256:384, :], in_=yacc[:, 0:4096]), 16)
                c.dS.inc(ge.dma_start(out=dmp[384:512, :], in_=h_sb[:]), 16)
            se.wait_ge(c.vE.sem, c.vE.val)
            for i in range(NTQ):
                bb = i % 2
                buf, ob = sx[bb], sout[bb]
                cx = c.cnt(f"ffX{bb}")
                cs = c.cnt(f"ffS{bb}")
                if ln_read[bb]:
                    se.wait_ge(c.vE.sem, ln_read[bb])
                tk = cx.inc(se.dma_start(out=buf[:], in_=x_in[q0 + i * 128: q0 + (i + 1) * 128, :]), 16)
                if out_done[bb]:
                    ve.wait_ge(cs.sem, out_done[bb])
                ve.wait_ge(cx.sem, tk)
                vt = c.vE.inc(ve.scalar_tensor_tensor(ob[:], yacc[:, i * 1024:(i + 1) * 1024], 0.5, buf[:], ALU.mult, ALU.add))
                ln_read[bb] = vt
                se.wait_ge(c.vE.sem, vt)
                out_done[bb] = cs.inc(se.dma_start(out=x_out[q0 + i * 128: q0 + (i + 1) * 128, :], in_=ob[:]), 16)
        ge.wait_ge(c.dS.sem, c.dS.val)



def _prep_l0ffn_weights(inputs):
    f = np.float32
    W = {}
    W["gwT"] = np.ascontiguousarray(inputs["L0_ffn_gate_w"].T.astype(f))
    W["ebias"] = np.ascontiguousarray(np.broadcast_to(inputs["L0_ffn_expert_bias"].astype(f), (128, 16)))
    W["wgT"] = np.ascontiguousarray(np.transpose(np.asarray(inputs["L0_ffn_w_gate"], f), (0, 2, 1)))
    W["wuT"] = np.ascontiguousarray(np.transpose(np.asarray(inputs["L0_ffn_w_up"], f), (0, 2, 1)))
    W["wdT"] = np.ascontiguousarray(np.transpose(np.asarray(inputs["L0_ffn_w_down"], f), (0, 2, 1)))
    W["sgT"] = np.ascontiguousarray(inputs["L0_ffn_shared_gate_w"].T.astype(f))
    W["suT"] = np.ascontiguousarray(inputs["L0_ffn_shared_up_w"].T.astype(f))
    W["sdT"] = np.ascontiguousarray(inputs["L0_ffn_shared_down_w"].T.astype(f))
    W["id_bf16"] = np.eye(128, dtype=ml_dtypes.bfloat16)
    W["ones1"] = np.ones((1, 128), dtype=f)
    return W


L0FFN_W_KEYS = ["gwT", "ebias", "wgT", "wuT", "wdT", "sgT", "suT", "sdT", "id_bf16", "ones1"]


def build_l0ffn_program(dbg_skip=()):
    nc = bass.Bass("TRN2", target_bir_lowering=False)
    x_in = nc.dram_tensor("x", [TPC, D], F32, kind="ExternalInput").ap()
    x_out = nc.dram_tensor("xo", [TPC, D], F32, kind="ExternalOutput").ap()
    Wd = {
        "dbg": nc.dram_tensor("dbg", [512, 4096], F32, kind="ExternalOutput").ap(),
        "gwT": nc.dram_tensor("gwT", [D, 16], F32, kind="ExternalInput").ap(),
        "ebias": nc.dram_tensor("ebias", [128, 16], F32, kind="ExternalInput").ap(),
        "wgT": nc.dram_tensor("wgT", [16, D, D], F32, kind="ExternalInput").ap(),
        "wuT": nc.dram_tensor("wuT", [16, D, D], F32, kind="ExternalInput").ap(),
        "wdT": nc.dram_tensor("wdT", [16, D, D], F32, kind="ExternalInput").ap(),
        "sgT": nc.dram_tensor("sgT", [D, D], F32, kind="ExternalInput").ap(),
        "suT": nc.dram_tensor("suT", [D, D], F32, kind="ExternalInput").ap(),
        "sdT": nc.dram_tensor("sdT", [D, D], F32, kind="ExternalInput").ap(),
        "id_bf16": nc.dram_tensor("id_bf16", [128, 128], BF16, kind="ExternalInput").ap(),
        "ones1": nc.dram_tensor("ones1", [1, 128], F32, kind="ExternalInput").ap(),
    }
    with ExitStack() as ctx:
        c = Ctx(nc, ctx)
        block = ctx.enter_context(nc.Block())

        @block.sync
        def _(sync):
            emit_l0ffn(c, x_in, x_out, Wd, dbg_skip=dbg_skip)
    return nc


def run_l0ffn(x_full, W, nc):
    ims = []
    for core in range(NCORES):
        xs = np.ascontiguousarray(x_full[2 * core:2 * core + 2].reshape(TPC, D))
        im = {"x": xs}
        for k in L0FFN_W_KEYS:
            im[k] = W[k]
        ims.append(im)
    res = run_bass_kernel_spmd(nc, ims, core_ids=list(range(NCORES)))
    out = np.empty((B, L, D), np.float32)
    for core in range(NCORES):
        out[2 * core:2 * core + 2] = np.asarray(res.results[core]["xo"]).reshape(2, L, D)
    return out




NHEAD = 32
DSSD = 2048
CONV_DIM = 2304
CHUNK = 256


def emit_mamba(c: Ctx, x_in, x_out, W):
    nc, te, ve, ae, ge, se = c.nc, c.te, c.ve, c.ae, c.ge, c.se
    NOC = CONV_DIM // 128

    with ExitStack() as ps:
        _u = c.bump()
        sbuf = lambda n, s, d: ps.enter_context(nc.sbuf_tensor(f"{n}_{_u}", s, d))
        psum = lambda n, s, d=F32: ps.enter_context(nc.psum_tensor(f"{n}_{_u}", s, d))

        idb = sbuf("mb_idb", [128, 128], BF16)
        idf = sbuf("mb_idf", [128, 128], F32)
        ones1 = sbuf("mb_ones1", [1, 128], F32)
        ones32 = sbuf("mb_ones32", [32, 256], F32)
        convw = sbuf("mb_convw", [128, NOC * 4], F32)
        convb = sbuf("mb_convb", [128, NOC], F32)
        dtb = sbuf("mb_dtb", [128, 32], F32)
        alog = sbuf("mb_alog", [128, 32], F32)
        drep = sbuf("mb_drep", [128, 32], F32)
        negA = sbuf("mb_negA", [128, 32], F32)
        normw = sbuf("mb_normw", [128, DSSD], BF16)
        trmask = sbuf("mb_trmask", [128, 2 * 256], BF16)

        wbuf = [sbuf(f"mb_w{i}", [128, 8 * 2336], BF16) for i in range(2)]

        sx = [sbuf(f"mb_sx{i}", [128, 1024], F32) for i in range(2)]
        sxc = sbuf("mb_sxc", [128, 1024], F32)
        sstat = sbuf("mb_sstat", [128, 8], F32)
        sxln = sbuf("mb_sxln", [128, 1024], BF16)
        xlnT = sbuf("mb_xlnT", [128, 8 * 256], BF16)
        z_sb = sbuf("mb_z", [128, 2 * 2048], BF16)
        xbc = sbuf("mb_xbc", [128, NOC * 259], BF16)
        xbcs = sbuf("mb_xbcs", [128, NOC * 256], BF16)
        x_tm = sbuf("mb_xtm", [128, 2 * 2048], BF16)
        xdt = sbuf("mb_xdt", [128, 2 * 2048], BF16)
        b_tm = sbuf("mb_btm", [128, 2 * 128], BF16)
        dt_tm = sbuf("mb_dttm", [128, 2 * 32], F32)
        dts_tm = sbuf("mb_dtstm", [128, 2 * 32], F32)
        a_tm = sbuf("mb_atm", [128, 2 * 32], F32)
        acs_tm = sbuf("mb_acstm", [128, 2 * 32], F32)
        eacs_tm = sbuf("mb_eacstm", [128, 2 * 32], F32)
        w_tm = sbuf("mb_wtm", [128, 2 * 32], F32)
        sp32 = sbuf("mb_sp32", [128, 4 * 32], F32)
        afm = sbuf("mb_afm", [32, 256], F32)
        acsfm = sbuf("mb_acsfm", [32, 256], F32)
        dtfm = sbuf("mb_dtfm", [32, 256], F32)
        efm = sbuf("mb_efm", [32, 256], F32)
        wfm = sbuf("mb_wfm", [32, 256], F32)
        acs_hg = sbuf("mb_acshg", [1, 2048], F32)
        acs_last = sbuf("mb_acslast", [1, 32], F32)
        cdrep = sbuf("mb_cdrep", [128, 32], F32)
        bct = sbuf("mb_bct", [128, 2 * 256], F32)
        dD = sbuf("mb_dD", [128, 512], F32)
        e_s = sbuf("mb_es", [128, 512], BF16)
        s1_s = sbuf("mb_s1", [128, 512], BF16)
        S_sb = sbuf("mb_S", [128, 2 * 2048], BF16)
        carry = sbuf("mb_carry", [128, 2048], F32)
        carry_bf = sbuf("mb_carrybf", [128, 2048], BF16)
        y_sb = sbuf("mb_y", [128, 2 * 2048], F32)
        ytmp = sbuf("mb_ytmp", [128, 2048], F32)
        dD2 = [dD[:], ytmp[:, 0:512]]
        e_s2 = [e_s[:], sxln[:, 0:512]]
        s1_s2 = [s1_s[:], sxln[:, 512:1024]]

        acs_dram = W["acs_dram"]

        p0 = psum("mb_p0", [128, 512], F32)
        p1 = psum("mb_p1", [128, 512], F32)
        p2 = psum("mb_p2", [128, 512], F32)
        p3 = psum("mb_p3", [128, 512], F32)
        ptb = [psum(f"mb_ptb{i}", [128, 1024], BF16) for i in range(2)]
        yps = [psum("mb_yps0", [128, 512], F32), psum("mb_yps1", [128, 512], F32)]

        for dst, key in [(idb, "id_bf16"), (idf, "id_f32"), (ones1, "ones1"),
                         (dtb, "dtb_rep"), (alog, "alog_rep"), (drep, "d_rep"),
                         (normw, "normw_rep"), (trmask, "trmask"), (convb, "convb_r")]:
            c.dW.inc(ge.dma_start(out=dst[:], in_=W[key]), 16)
        c.dW.inc(ge.dma_start(out=convw[:], in_=W["convw"].rearrange("(o p) j -> p o j", p=128)), 16)
        const_tick = c.dW.val
        ve.wait_ge(c.dW.sem, const_tick)
        vse(c, ve.memset(ones32[:], 1.0))
        t = c.vE.inc(ve.tensor_copy(sp32[:, 0:32], alog[:]))
        ae.wait_ge(c.vE.sem, t)
        t2 = c.aE.inc(ae.activation(sp32[:, 32:64], sp32[:, 0:32], AF.Exp))
        ve.wait_ge(c.aE.sem, t2)
        vse(c, ve.tensor_scalar(negA[:], sp32[:, 32:64], -1.0, None, ALU.mult))
        te.wait_ge(c.dW.sem, const_tick)

        wticks = {0: 0, 1: 0}
        wconsumed = {0: 0, 1: 0}
        ln_read = {0: 0, 1: 0}
        tr_evac = {0: 0, 1: 0}
        ps_ev = {}
        out_done = 0
        acs_row_used = 0

        def load_w(part, b):
            cw = c.cnt(f"mbW{b}")
            if wconsumed[b]:
                ge.wait_ge(c.tE.sem, wconsumed[b])
            if part == 0:
                src = W["inprojT"][:, 0:2048]
                cw.inc(ge.dma_start(out=bass.AP(wbuf[b][:].tensor, 0, [[wbuf[b][:].ap[0][0], 128], [2048, 8], [1, 2048]]),
                                     in_=src.rearrange("(k p) i -> p k i", p=128)), 16)
            elif part == 1:
                src = W["inprojT"][:, 2048:4384]
                cw.inc(ge.dma_start(out=bass.AP(wbuf[b][:].tensor, 0, [[wbuf[b][:].ap[0][0], 128], [2336, 8], [1, 2336]]),
                                     in_=src.rearrange("(k p) i -> p k i", p=128)), 16)
            else:
                src = W["opT"]
                cw.inc(ge.dma_start(out=bass.AP(wbuf[b][:].tensor, 0, [[wbuf[b][:].ap[0][0], 128], [1024, 16], [1, 1024]]),
                                     in_=src.rearrange("(k p) i -> p k i", p=128)), 16)
            return cw.val

        def transpose_bf(src_ap, dst_ap, pb):
            nsrc_p, nsrc_f = src_ap.shape[0], src_ap.shape[1]
            if tr_evac[pb]:
                te.wait_ge(c.aE.sem, tr_evac[pb])
            tt = c.tE.inc(te.transpose(ptb[pb][0:nsrc_f, 0:nsrc_p], src_ap, idb[0:nsrc_p, 0:nsrc_p]))
            ae.wait_ge(c.tE.sem, tt)
            tr_evac[pb] = c.aE.inc(ae.copy(dst_ap, ptb[pb][0:nsrc_f, 0:nsrc_p]))
            return tr_evac[pb]

        ftr_evac = {2: 0, 3: 0}

        def transpose_f32(src_ap, dst_ap, which):
            pp = {2: p2, 3: p3}[which]
            nsrc_p, nsrc_f = src_ap.shape[0], src_ap.shape[1]
            if ftr_evac[which]:
                te.wait_ge(c.aE.sem, ftr_evac[which])
            te.wait_ge(c.vE.sem, c.vE.val)
            tt = c.tE.inc(te.transpose(pp[0:nsrc_f, 0:nsrc_p], src_ap, idf[0:nsrc_p, 0:nsrc_p]))
            ae.wait_ge(c.tE.sem, tt)
            ftr_evac[which] = c.aE.inc(ae.copy(dst_ap, pp[0:nsrc_f, 0:nsrc_p]))
            return ftr_evac[which]

        def mdump(row, ap):
            if "mdbg" not in W:
                return
            ge.wait_ge(c.vE.sem, c.vE.val)
            ge.wait_ge(c.aE.sem, c.aE.val)
            ge.wait_ge(c.tE.sem, c.tE.val)
            ncols = ap.shape[1] if len(ap.shape) > 1 else ap.shape[0]
            c.gE.inc(ge.dma_start(out=W["mdbg"][row * 128:(row) * 128 + ap.shape[0], 0:ncols], in_=ap), 16)
            ge.wait_ge(c.gE.sem, c.gE.val)
            ve.wait_ge(c.gE.sem, c.gE.val)
            ae.wait_ge(c.gE.sem, c.gE.val)
            te.wait_ge(c.gE.sem, c.gE.val)

        NCH = (TPC // CHUNK) if not W.get('dbg_nch') else W['dbg_nch']
        bz, bxd, bop = 0, 1, 0

        wticks[0] = load_w(0, 0)
        wticks[1] = load_w(1, 1)
        parts_loaded = [0, 1]

        for ch in range(NCH):
            t0 = ch * CHUNK
            seq_start = (ch % 8 == 0)

            for i in range(2):
                bb = i % 2
                cx = c.cnt(f"mbX{bb}")
                if ln_read[bb]:
                    se.wait_ge(c.vE.sem, ln_read[bb])
                tk = cx.inc(se.dma_start(out=sx[bb][:], in_=x_in[t0 + i * 128: t0 + (i + 1) * 128, :]), 16)
                ve.wait_ge(cx.sem, tk)
                ve.wait_ge(c.tE.sem, c.tE.val)
                if out_done:
                    ve.wait_ge(c.dS.sem, out_done)
                at_ln = emit_ln_tile(c, sx[bb], sxc, z_sb[:, 0:1024], sstat, sxln)
                ln_read[bb] = c.vE.val
                te.wait_ge(c.aE.sem, at_ln)
                for k in range(8):
                    transpose_bf(sxln[:, k * 128:(k + 1) * 128],
                                 xlnT[:, k * 256 + i * 128: k * 256 + (i + 1) * 128], k % 2)

            bz = parts_loaded.index(0)
            te.wait_ge(c.cnt(f"mbW{bz}").sem, wticks[bz])
            te.wait_ge(c.aE.sem, c.aE.val)
            for i in range(2):
                for oc in range(4):
                    pp = [p0, p1, p2, p3][oc]
                    te.wait_ge(c.aE.sem, c.aE.val)
                    for k in range(8):
                        mm = te.matmul(pp[:], xlnT[:, k * 256 + i * 128: k * 256 + (i + 1) * 128],
                                       wbuf[bz][:, k * 2048 + oc * 512: k * 2048 + oc * 512 + 512],
                                       start=(k == 0), stop=(k == 7))
                    tt = c.tE.inc(mm)
                    ae.wait_ge(c.tE.sem, tt)
                    c.aE.inc(ae.copy(z_sb[:, i * 2048 + oc * 512: i * 2048 + (oc + 1) * 512], pp[:]))
            wconsumed[bz] = c.tE.val
            nextp = (max(parts_loaded) + 1) % 3 if ch * 3 + 2 < NCH * 3 else None
            wticks[bz] = load_w(2, bz)
            parts_loaded[bz] = 2

            bxd = parts_loaded.index(1)
            te.wait_ge(c.cnt(f"mbW{bxd}").sem, wticks[bxd])
            if seq_start:
                hist = c.vE.inc(ve.memset(bass.AP(xbc[:].tensor, 0, [[xbc[:].ap[0][0], 128], [259, NOC], [1, 3]]), 0.0))
            else:
                hist = c.vE.inc(ve.tensor_copy(
                    bass.AP(xbc[:].tensor, 0, [[xbc[:].ap[0][0], 128], [259, NOC], [1, 3]]),
                    bass.AP(xbc[:].tensor, 256, [[xbc[:].ap[0][0], 128], [259, NOC], [1, 3]])))
            for oc in range(NOC):
                pp = [p0, p1][oc % 2]
                te.wait_ge(c.aE.sem, c.aE.val)
                for k in range(8):
                    mm = te.matmul(pp[:, 0:256], wbuf[bxd][:, k * 2336 + oc * 128: k * 2336 + oc * 128 + 128],
                                   xlnT[:, k * 256: (k + 1) * 256], start=(k == 0), stop=(k == 7))
                tt = c.tE.inc(mm)
                ae.wait_ge(c.tE.sem, tt)
                ae.wait_ge(c.vE.sem, hist)
                c.aE.inc(ae.copy(xbc[:, oc * 259 + 3: oc * 259 + 259], pp[:, 0:256]))
            for i in range(2):
                pp = [p2, p3][i]
                te.wait_ge(c.vE.sem, c.vE.val)
                for k in range(8):
                    mm = te.matmul(pp[:, 0:32], xlnT[:, k * 256 + i * 128: k * 256 + (i + 1) * 128],
                                   wbuf[bxd][:, k * 2336 + NOC * 128: k * 2336 + NOC * 128 + 32],
                                   start=(k == 0), stop=(k == 7))
                tt = c.tE.inc(mm)
                ve.wait_ge(c.tE.sem, tt)
                c.vE.inc(ve.tensor_tensor(dt_tm[:, i * 32:(i + 1) * 32], pp[:, 0:32], dtb[:], ALU.add))
            wconsumed[bxd] = c.tE.val
            if ch + 1 < NCH:
                wticks[bxd] = load_w(1, bxd)
            parts_loaded[bxd] = 1

            ve.wait_ge(c.aE.sem, c.aE.val)
            ve.wait_ge(c.tE.sem, c.tE.val)
            at = 0
            for oc in range(NOC):
                dst = ytmp[:, 0:256] if oc % 2 == 0 else ytmp[:, 256:512]
                src0 = xbc[:, oc * 259: oc * 259 + 256]
                ve.tensor_scalar(dst, src0, convw[:, oc * 4: oc * 4 + 1], convb[:, oc: oc + 1], ALU.mult, ALU.add)
                for j in range(1, 3):
                    srcj = xbc[:, oc * 259 + j: oc * 259 + j + 256]
                    ve.scalar_tensor_tensor(dst, srcj, convw[:, oc * 4 + j: oc * 4 + j + 1], dst, ALU.mult, ALU.add)
                cvt = c.vE.inc(ve.scalar_tensor_tensor(dst, xbc[:, oc * 259 + 3: oc * 259 + 3 + 256],
                                                       convw[:, oc * 4 + 3: oc * 4 + 4], dst, ALU.mult, ALU.add))
                ae.wait_ge(c.vE.sem, cvt)
                at = c.aE.inc(ae.activation(xbcs[:, oc * 256:(oc + 1) * 256], dst, AF.Silu))
                if oc >= 1:
                    ve.wait_ge(c.aE.sem, prev_at)
                prev_at = at

            ae.wait_ge(c.vE.sem, c.vE.val)
            t2 = c.aE.inc(ae.activation(sp32[:, 0:64], dt_tm[:], AF.Abs))
            ae.wait_ge(c.aE.sem, t2)
            t2 = c.aE.inc(ae.activation(sp32[:, 64:128], sp32[:, 0:64], AF.Exp, scale=-1.0))
            ae.wait_ge(c.aE.sem, t2)
            t2 = c.aE.inc(ae.activation(sp32[:, 0:64], sp32[:, 64:128], AF.Ln, bias=1.0))
            ve.wait_ge(c.aE.sem, t2)
            vse(c, ve.tensor_scalar(sp32[:, 64:128], dt_tm[:], 0.0, None, ALU.max))
            vse(c, ve.tensor_tensor(dts_tm[:], sp32[:, 64:128], sp32[:, 0:64], ALU.add))
            if ch == 0:
                mdump(0, dts_tm[:])
            vse(c, ve.tensor_tensor(a_tm[:], dts_tm[:],
                                    bass.AP(negA[:].tensor, 0, [[negA[:].ap[0][0], 128], [0, 2], [1, 32]]),
                                    ALU.mult))

            for i in range(2):
                transpose_f32(a_tm[:, i * 32:(i + 1) * 32], afm[:, i * 128:(i + 1) * 128], 2)
                transpose_f32(dts_tm[:, i * 32:(i + 1) * 32], dtfm[:, i * 128:(i + 1) * 128], 3)

            ve.wait_ge(c.aE.sem, c.aE.val)
            vse(c, ve.tensor_tensor_scan(acsfm[:], ones32[:], afm[:], 0.0, ALU.mult, ALU.add))

            se.wait_ge(c.vE.sem, c.vE.val)
            cw_ = c.cnt("mbAcsW")
            tk = cw_.inc(se.dma_start(out=acs_dram, in_=acsfm[:]), 16)
            se.wait_ge(cw_.sem, tk)
            if acs_row_used:
                se.wait_ge(c.tE.sem, acs_row_used)
            cl_ = c.cnt("mbAcsL")
            tk_row = cl_.inc(se.dma_start(out=acs_last[:],
                             in_=bass.AP(acs_dram.tensor, 255, [[8192, 1], [256, 32]]),
                             allow_slow_non_contiguous=True), 16)
            for i in range(2):
                transpose_f32(acsfm[:, i * 128:(i + 1) * 128], acs_tm[:, i * 32:(i + 1) * 32], 2)
            ae.wait_ge(c.aE.sem, c.aE.val)
            t2 = c.aE.inc(ae.activation(eacs_tm[:], acs_tm[:], AF.Exp))

            if ch == 0:
                mdump(1, acs_tm[:])
                mdump(2, xbcs[:, 16 * 256: 18 * 256])
            vse(c, ve.tensor_scalar(efm[:], acsfm[:], acsfm[:, 255:256], None, ALU.subtract))
            ae.wait_ge(c.vE.sem, c.vE.val)
            t2 = c.aE.inc(ae.activation(efm[:], efm[:], AF.Exp, scale=-1.0))
            ve.wait_ge(c.aE.sem, t2)
            vse(c, ve.tensor_tensor(wfm[:], efm[:], dtfm[:], ALU.mult))
            for i in range(2):
                transpose_f32(wfm[:, i * 128:(i + 1) * 128], w_tm[:, i * 32:(i + 1) * 32], 3)

            te.wait_ge(c.cnt("mbAcsL").sem, tk_row)
            te.wait_ge(c.aE.sem, c.aE.val)
            te.wait_ge(c.vE.sem, c.vE.val)
            mm = te.matmul(p0[:, 0:32], ones1[:], acs_last[:], start=True, stop=True)
            tt = c.tE.inc(mm)
            ae.wait_ge(c.tE.sem, tt)
            t2 = c.aE.inc(ae.activation(cdrep[:], p0[:, 0:32], AF.Exp))

            te.wait_ge(c.aE.sem, at)
            for i in range(2):
                for oc in range(16):
                    transpose_bf(xbcs[:, oc * 256 + i * 128: oc * 256 + (i + 1) * 128],
                                 x_tm[:, i * 2048 + oc * 128: i * 2048 + (oc + 1) * 128], oc % 2)
                transpose_bf(xbcs[:, 16 * 256 + i * 128: 16 * 256 + (i + 1) * 128],
                             b_tm[:, i * 128:(i + 1) * 128], i % 2)

            if ch == 0:
                mdump(3, x_tm[:, 0:2048])
                mdump(9, w_tm[:])
            ve.wait_ge(c.aE.sem, c.aE.val)
            for i in range(2):
                vse(c, ve.tensor_tensor(xdt[:, i * 2048:(i + 1) * 2048],
                                        x_tm[:, i * 2048:(i + 1) * 2048],
                                        w_tm[:, i * 32:(i + 1) * 32].broadcast_to((128, 32, 64)), ALU.mult))

            for jt in range(2):
                pp = [p2, p3][jt]
                te.wait_ge(c.vE.sem, c.vE.val)
                te.wait_ge(c.aE.sem, c.aE.val)
                mm = te.matmul(pp[:, 0:256], xbcs[:, 16 * 256 + jt * 128: 16 * 256 + (jt + 1) * 128],
                               xbcs[:, 17 * 256: 18 * 256], start=True, stop=True)
                tt = c.tE.inc(mm)
                ve.wait_ge(c.tE.sem, tt)
                c.vE.inc(ve.tensor_tensor(bct[:, jt * 256:(jt + 1) * 256], pp[:, 0:256],
                                          trmask[:, jt * 256:(jt + 1) * 256], ALU.mult))

            if seq_start:
                vse(c, ve.memset(carry[:], 0.0))
                vse(c, ve.memset(carry_bf[:], 0.0))
            te.wait_ge(c.vE.sem, c.vE.val)
            te.wait_ge(c.aE.sem, c.aE.val)
            yiev = {}
            for i in range(2):
                for ob in range(4):
                    pp = [p0, p1, p2, p3][ob]
                    if i == 1:
                        te.wait_ge(c.vE.sem, yiev[ob])
                    mm = te.matmul(pp[:], xbcs[:, 17 * 256 + i * 128: 17 * 256 + (i + 1) * 128],
                                   carry_bf[:, ob * 512:(ob + 1) * 512], start=True, stop=True)
                    tt = c.tE.inc(mm)
                    ve.wait_ge(c.tE.sem, tt)
                    yiev[ob] = c.vE.inc(ve.tensor_tensor(
                        y_sb[:, i * 2048 + ob * 512: i * 2048 + (ob + 1) * 512], pp[:],
                        eacs_tm[:, i * 32 + ob * 8: i * 32 + (ob + 1) * 8].broadcast_to((128, 8, 64)),
                        ALU.mult))

            if ch == 0:
                mdump(4, y_sb[:, 0:2048])
                mdump(7, bct[:])
            hg_bcast = 0
            for hg in range(4):
                h0 = hg * 8
                ch_ = c.cnt("mbAcsH")
                if hg == 0 and acs_row_used:
                    se.wait_ge(c.tE.sem, acs_row_used)
                if hg >= 1:
                    se.wait_ge(c.tE.sem, hg_bcast)
                tk_hg = ch_.inc(se.dma_start(
                    out=acs_hg[:], in_=bass.AP(acs_dram.tensor, h0 * 256, [[8192, 1], [1, 2048]])), 16)
                te.wait_ge(ch_.sem, tk_hg)
                dsub = {0: c.vE.val, 1: c.vE.val}
                dmin = {0: 0, 1: 0}
                aexp = {0: 0, 1: 0}
                for u, (jt, hp) in enumerate([(j_, h_) for j_ in range(2) for h_ in range(4)]):
                    h = h0 + hp * 2
                    par = u % 2
                    pp = [p0, p1][par]
                    te.wait_ge(c.vE.sem, dsub[par])
                    mm = te.matmul(pp[:], ones1[:], acs_hg[0:1, hp * 512:(hp + 1) * 512], start=True, stop=True)
                    tt = c.tE.inc(mm)
                    hg_bcast = tt
                    ve.wait_ge(c.tE.sem, tt)
                    if aexp[par]:
                        ve.wait_ge(c.aE.sem, aexp[par])
                    ve.tensor_tensor(dD2[par], pp[:],
                                     acs_tm[:, jt * 32 + h: jt * 32 + h + 2].broadcast_to((128, 2, 256)),
                                     ALU.subtract)
                    dsub[par] = c.vE.val + 1
                    dmin[par] = c.vE.inc(ve.tensor_scalar(dD2[par], dD2[par], 0.0, None, ALU.min))
                    ae.wait_ge(c.vE.sem, dmin[par])
                    aexp[par] = c.aE.inc(ae.activation(e_s2[par], dD2[par], AF.Exp))
                    ve.wait_ge(c.aE.sem, aexp[par])
                    ve.tensor_tensor(s1_s2[par], e_s2[par],
                                     dts_tm[:, jt * 32 + h: jt * 32 + h + 2].broadcast_to((128, 2, 256)),
                                     ALU.mult)
                    c.vE.inc(ve.tensor_tensor(
                        S_sb[:, jt * 2048 + hp * 512: jt * 2048 + (hp + 1) * 512], s1_s2[par],
                        bass.AP(bct[:].tensor, jt * 256, [[bct[:].ap[0][0], 128], [0, 2], [1, 256]]),
                        ALU.mult))
                acs_row_used = c.tE.val
                te.wait_ge(c.vE.sem, c.vE.val)
                for it in range(2):
                    for hl in range(8):
                        h = h0 + hl
                        for jt in range(2):
                            mm = te.matmul(yps[it][:, hl * 64:(hl + 1) * 64],
                                           S_sb[:, jt * 2048 + hl * 256 + it * 128: jt * 2048 + hl * 256 + it * 128 + 128],
                                           x_tm[:, jt * 2048 + h * 64: jt * 2048 + (h + 1) * 64],
                                           start=(jt == 0), stop=(jt == 1))
                    tt = c.tE.inc(mm)
                    ve.wait_ge(c.tE.sem, tt)
                    c.vE.inc(ve.tensor_tensor(y_sb[:, it * 2048 + h0 * 64: it * 2048 + h0 * 64 + 512],
                                              y_sb[:, it * 2048 + h0 * 64: it * 2048 + h0 * 64 + 512],
                                              yps[it][:], ALU.add))

            if ch == 0:
                mdump(5, y_sb[:, 0:2048])
            te.wait_ge(c.vE.sem, c.vE.val)
            for h in range(NHEAD):
                pp = [p2, p3][h % 2]
                if h >= 2:
                    te.wait_ge(c.vE.sem, st_ev[h % 2])
                for jt in range(2):
                    mm = te.matmul(pp[:, 0:64], b_tm[:, jt * 128:(jt + 1) * 128],
                                   xdt[:, jt * 2048 + h * 64: jt * 2048 + (h + 1) * 64],
                                   start=(jt == 0), stop=(jt == 1))
                tt = c.tE.inc(mm)
                ve.wait_ge(c.tE.sem, tt)
                ev = c.vE.inc(ve.scalar_tensor_tensor(carry[:, h * 64:(h + 1) * 64],
                                                      carry[:, h * 64:(h + 1) * 64],
                                                      cdrep[:, h: h + 1], pp[:, 0:64],
                                                      ALU.mult, ALU.add))
                if h == 0:
                    st_ev = {}
                st_ev[h % 2] = ev
            ve.wait_ge(c.vE.sem, c.vE.val)
            vse(c, ve.tensor_copy(carry_bf[:], carry[:]))

            if ch == 0:
                mdump(6, carry[:])
            yn = S_sb
            for it in range(2):
                vse(c, ve.tensor_tensor(ytmp[:], x_tm[:, it * 2048:(it + 1) * 2048],
                                drep[:].broadcast_to((128, 32, 64)), ALU.mult))
                vse(c, ve.tensor_tensor(y_sb[:, it * 2048:(it + 1) * 2048],
                                        y_sb[:, it * 2048:(it + 1) * 2048], ytmp[:], ALU.add))
                ae.wait_ge(c.tE.sem, c.tE.val)
                at2 = c.aE.inc(ae.activation(xdt[:, it * 2048:(it + 1) * 2048],
                                             z_sb[:, it * 2048:(it + 1) * 2048], AF.Silu))
                ve.wait_ge(c.aE.sem, at2)
                vse(c, ve.tensor_tensor(ytmp[:], y_sb[:, it * 2048:(it + 1) * 2048],
                                        xdt[:, it * 2048:(it + 1) * 2048], ALU.mult))
                ae.wait_ge(c.vE.sem, c.vE.val)
                t2 = c.aE.inc(ae.activation(y_sb[:, it * 2048:(it + 1) * 2048], ytmp[:],
                                            AF.Square, accum_out=sstat[:, 0:1]))
                ve.wait_ge(c.aE.sem, t2)
                vse(c, ve.tensor_scalar(sstat[:, 1:2], sstat[:, 0:1], 1.0 / DSSD, EPS, ALU.mult, ALU.add))
                ae.wait_ge(c.vE.sem, c.vE.val)
                t2 = c.aE.inc(ae.activation(sstat[:, 2:3], sstat[:, 1:2], AF.Sqrt))
                ve.wait_ge(c.aE.sem, t2)
                vt = c.vE.inc(ve.reciprocal(sstat[:, 3:4], sstat[:, 2:3]))
                ae.wait_ge(c.vE.sem, vt)
                t2 = c.aE.inc(ae.mul(y_sb[:, it * 2048:(it + 1) * 2048], ytmp[:], sstat[:, 3:4]))
                ve.wait_ge(c.aE.sem, t2)
                vse(c, ve.tensor_tensor(yn[:, it * 2048:(it + 1) * 2048],
                                        y_sb[:, it * 2048:(it + 1) * 2048], normw[:], ALU.mult))
                if ch == 0 and it == 0:
                    mdump(8, yn[:, 0:2048])
                ynT = xlnT
                te.wait_ge(c.vE.sem, c.vE.val)
                for k in range(16):
                    transpose_bf(yn[:, it * 2048 + k * 128: it * 2048 + (k + 1) * 128],
                                 ynT[:, k * 128:(k + 1) * 128], k % 2)
                bop = parts_loaded.index(2)
                te.wait_ge(c.cnt(f"mbW{bop}").sem, wticks[bop])
                te.wait_ge(c.aE.sem, c.aE.val)
                bb = it % 2
                cx = c.cnt(f"mbX{bb}")
                if ln_read[bb]:
                    se.wait_ge(c.vE.sem, ln_read[bb])
                tk = cx.inc(se.dma_start(out=sx[bb][:], in_=x_in[t0 + it * 128: t0 + (it + 1) * 128, :]), 16)
                for oc in range(2):
                    pp = [p0, p1][oc]
                    te.wait_ge(c.vE.sem, c.vE.val)
                    for k in range(16):
                        mm = te.matmul(pp[:], ynT[:, k * 128:(k + 1) * 128],
                                       wbuf[bop][:, k * 1024 + oc * 512: k * 1024 + oc * 512 + 512],
                                       start=(k == 0), stop=(k == 15))
                    tt = c.tE.inc(mm)
                    ve.wait_ge(c.tE.sem, tt)
                    ve.wait_ge(cx.sem, tk)
                    if out_done:
                        ve.wait_ge(c.dS.sem, out_done)
                    c.vE.inc(ve.tensor_tensor(sxc[:, oc * 512:(oc + 1) * 512], pp[:],
                                              sx[bb][:, oc * 512:(oc + 1) * 512], ALU.add))
                ln_read[bb] = c.vE.val
                se.wait_ge(c.vE.sem, c.vE.val)
                out_done = c.dS.inc(se.dma_start(out=x_out[t0 + it * 128: t0 + (it + 1) * 128, :], in_=sxc[:]), 16)
                if it == 1:
                    wconsumed[bop] = c.tE.val
                    if ch + 1 < NCH:
                        wticks[bop] = load_w(0, bop)
                        parts_loaded[bop] = 0
            acs_row_used = c.tE.val
        ge.wait_ge(c.dS.sem, c.dS.val)


def _prep_mamba_weights(inputs):
    f = np.float32
    W = {}
    W["inprojT"] = np.ascontiguousarray(np.asarray(inputs["L0_attn_in_proj_w"], f).T)
    W["opT"] = np.ascontiguousarray(np.asarray(inputs["L0_attn_out_proj_w"], f).T)
    W["convw"] = np.ascontiguousarray(np.asarray(inputs["L0_attn_conv_w"], f))
    W["convb_r"] = np.ascontiguousarray(np.asarray(inputs["L0_attn_conv_b"], f).reshape(18, 128).T)
    W["dtb_rep"] = np.ascontiguousarray(np.broadcast_to(np.asarray(inputs["L0_attn_dt_bias"], f), (128, 32)))
    W["alog_rep"] = np.ascontiguousarray(np.broadcast_to(np.asarray(inputs["L0_attn_A_log"], f), (128, 32)))
    W["d_rep"] = np.ascontiguousarray(np.broadcast_to(np.asarray(inputs["L0_attn_D"], f), (128, 32)))
    W["normw_rep"] = np.ascontiguousarray(
        np.broadcast_to(np.asarray(inputs["L0_attn_norm_w"], f), (128, 2048))).astype(ml_dtypes.bfloat16)
    i_idx = np.arange(256)
    m = np.zeros((128, 512), f)
    for jt in range(2):
        m[:, jt * 256:(jt + 1) * 256] = (i_idx[None, :] >= (jt * 128 + np.arange(128))[:, None]).astype(f)
    W["trmask"] = m.astype(ml_dtypes.bfloat16)
    W["id_bf16"] = np.eye(128, dtype=ml_dtypes.bfloat16)
    W["id_f32"] = np.eye(128, dtype=f)
    W["ones1"] = np.ones((1, 128), dtype=f)
    return W


MAMBA_W_KEYS = ["inprojT", "opT", "convw", "convb_r", "dtb_rep", "alog_rep", "d_rep",
                "normw_rep", "trmask", "id_bf16", "id_f32", "ones1"]


def build_mamba_program(dbg_nch=None):
    nc = bass.Bass("TRN2", target_bir_lowering=False)
    x_in = nc.dram_tensor("x", [TPC, D], F32, kind="ExternalInput").ap()
    x_out = nc.dram_tensor("xo", [TPC, D], F32, kind="ExternalOutput").ap()
    Wd = {
        "inprojT": nc.dram_tensor("inprojT", [D, 4384], F32, kind="ExternalInput").ap(),
        "opT": nc.dram_tensor("opT", [2048, 1024], F32, kind="ExternalInput").ap(),
        "convw": nc.dram_tensor("convw", [2304, 4], F32, kind="ExternalInput").ap(),
        "convb_r": nc.dram_tensor("convb_r", [128, 18], F32, kind="ExternalInput").ap(),
        "dtb_rep": nc.dram_tensor("dtb_rep", [128, 32], F32, kind="ExternalInput").ap(),
        "alog_rep": nc.dram_tensor("alog_rep", [128, 32], F32, kind="ExternalInput").ap(),
        "d_rep": nc.dram_tensor("d_rep", [128, 32], F32, kind="ExternalInput").ap(),
        "normw_rep": nc.dram_tensor("normw_rep", [128, 2048], BF16, kind="ExternalInput").ap(),
        "trmask": nc.dram_tensor("trmask", [128, 512], BF16, kind="ExternalInput").ap(),
        "id_bf16": nc.dram_tensor("id_bf16", [128, 128], BF16, kind="ExternalInput").ap(),
        "id_f32": nc.dram_tensor("id_f32", [128, 128], F32, kind="ExternalInput").ap(),
        "ones1": nc.dram_tensor("ones1", [1, 128], F32, kind="ExternalInput").ap(),
        "acs_dram": nc.dram_tensor("acs_dram", [32, 256], F32).ap(),
    }
    if dbg_nch:
        Wd["dbg_nch"] = dbg_nch
        Wd["mdbg"] = nc.dram_tensor("mdbg", [10 * 128, 2048], F32, kind="ExternalOutput").ap()
    with ExitStack() as ctx:
        c = Ctx(nc, ctx)
        block = ctx.enter_context(nc.Block())

        @block.sync
        def _(sync):
            emit_mamba(c, x_in, x_out, Wd)
    return nc


def run_mamba(x_full, W, nc):
    ims = []
    for core in range(NCORES):
        xs = np.ascontiguousarray(x_full[2 * core:2 * core + 2].reshape(TPC, D))
        im = {"x": xs}
        for k in MAMBA_W_KEYS:
            im[k] = W[k]
        ims.append(im)
    res = run_bass_kernel_spmd(nc, ims, core_ids=list(range(NCORES)))
    out = np.empty((B, L, D), np.float32)
    for core in range(NCORES):
        out[2 * core:2 * core + 2] = np.asarray(res.results[core]["xo"]).reshape(2, L, D)
    return out





def emit_l1attn(c: Ctx, x_in, x_out, W):
    nc, te, ve, ae, ge, se = c.nc, c.te, c.ve, c.ae, c.ge, c.se
    H, DH = 16, 64
    NT_SEQ = L // 128
    SHIFT = -8.0

    with ExitStack() as ps:
        _u = c.bump()
        sbuf = lambda n, s, d: ps.enter_context(nc.sbuf_tensor(f"{n}_{_u}", s, d))
        psum = lambda n, s, d=F32: ps.enter_context(nc.psum_tensor(f"{n}_{_u}", s, d))

        idb = sbuf("at_idb", [128, 128], BF16)
        ones1 = sbuf("at_ones1", [1, 128], F32)
        cosh = sbuf("at_cosh", [128, NT_SEQ * 32], F32)
        sinh = sbuf("at_sinh", [128, NT_SEQ * 32], F32)
        wqkv = sbuf("at_wqkv", [128, 8 * 3072], BF16)
        wop = sbuf("at_wop", [128, 8 * 1024], BF16)
        sx = [sbuf(f"at_sx{i}", [128, 1024], F32) for i in range(2)]
        sxc = sbuf("at_sxc", [128, 1024], F32)
        sstat = sbuf("at_sstat", [128, 8], F32)
        sxln = sbuf("at_sxln", [128, 1024], BF16)
        xlnT = sbuf("at_xlnT", [128, 8 * 128], BF16)
        rtmp1 = sbuf("at_rtmp1", [128, 512], F32)
        rtmp2 = sbuf("at_rtmp2", [128, 512], F32)
        q_tm = sbuf("at_qtm", [128, NT_SEQ * 1024], BF16)
        k_rope = sbuf("at_krope", [128, 1024], BF16)
        k_fm = sbuf("at_kfm", [128, 8 * L], BF16)
        v_sb = sbuf("at_vsb", [128, NT_SEQ * (H * 65 + 8)], BF16)
        qfib = sbuf("at_qfib", [128, 8 * 512], BF16)
        ofib = sbuf("at_ofib", [128, 8 * 512], BF16)
        srow2 = [sbuf(f"at_srow{i}", [128, 1024], BF16) for i in range(2)]
        o_ev = rtmp1
        rd = rtmp2

        psA = psum("at_psA", [128, 1024], F32)
        psB = psum("at_psB", [128, 1024], F32)
        pO = psum("at_pO", [128, 512], F32)
        pB_ = psum("at_pB", [128, 512], F32)
        pS2 = [psA, psB]
        pq = [psA[:, 0:512], psA[:, 512:1024]]
        pk = [psB[:, 0:512], psB[:, 512:1024]]
        pv = [pO[:], pB_[:]]
        ptb = [psum(f"at_ptb{i}", [128, 1024], BF16) for i in range(2)]

        for dst, key in [(idb, "id_bf16"), (ones1, "ones1"), (cosh, "cosh"), (sinh, "sinh")]:
            c.dW.inc(ge.dma_start(out=dst[:], in_=W[key]), 16)
        c.dW.inc(ge.dma_start(out=wqkv[:], in_=W["qkvT"].rearrange("(k p) o -> p k o", p=128)), 16)
        c.dW.inc(ge.dma_start(out=wop[:], in_=W["outT"].rearrange("(k p) o -> p k o", p=128)), 16)
        const_tick = c.dW.val
        te.wait_ge(c.dW.sem, const_tick)
        ve.wait_ge(c.dW.sem, const_tick)
        vse(c, ve.memset(sstat[:, 7:8], SHIFT))

        ln_read = {0: 0, 1: 0}
        tr_evac = {0: 0, 1: 0}
        out_done = {0: 0, 1: 0}

        def transpose_bf(src_ap, dst_ap, pb):
            nsrc_p, nsrc_f = src_ap.shape[0], src_ap.shape[1]
            if tr_evac[pb]:
                te.wait_ge(c.aE.sem, tr_evac[pb])
            tt = c.tE.inc(te.transpose(ptb[pb][0:nsrc_f, 0:nsrc_p], src_ap, idb[0:nsrc_p, 0:nsrc_p]))
            ae.wait_ge(c.tE.sem, tt)
            tr_evac[pb] = c.aE.inc(ae.copy(dst_ap, ptb[pb][0:nsrc_f, 0:nsrc_p]))
            return tr_evac[pb]

        def rope(dst_tm, src_psums, i):
            cos8 = bass.AP(cosh[:].tensor, i * 32, [[cosh[:].ap[0][0], 128], [0, 8], [1, 32]])
            sin8 = bass.AP(sinh[:].tensor, i * 32, [[sinh[:].ap[0][0], 128], [0, 8], [1, 32]])
            for half in range(2):
                pp = src_psums[half]
                q1 = bass.AP(pp.tensor, pp.offset, [[pp.ap[0][0], 128], [64, 8], [1, 32]])
                q2 = bass.AP(pp.tensor, pp.offset + 32, [[pp.ap[0][0], 128], [64, 8], [1, 32]])
                vse(c, ve.tensor_tensor(rtmp1[:, 0:256].rearrange("p (h j) -> p h j", h=8), q1, cos8, ALU.mult))
                vse(c, ve.tensor_tensor(rtmp2[:, 0:256].rearrange("p (h j) -> p h j", h=8), q2, sin8, ALU.mult))
                d1 = bass.AP(dst_tm.tensor, dst_tm.offset + half * 512, [[dst_tm.ap[0][0], 128], [64, 8], [1, 32]])
                vse(c, ve.tensor_tensor(d1, rtmp1[:, 0:256].rearrange("p (h j) -> p h j", h=8),
                                        rtmp2[:, 0:256].rearrange("p (h j) -> p h j", h=8), ALU.subtract))
                vse(c, ve.tensor_tensor(rtmp1[:, 0:256].rearrange("p (h j) -> p h j", h=8), q2, cos8, ALU.mult))
                vse(c, ve.tensor_tensor(rtmp2[:, 0:256].rearrange("p (h j) -> p h j", h=8), q1, sin8, ALU.mult))
                d2 = bass.AP(dst_tm.tensor, dst_tm.offset + half * 512 + 32, [[dst_tm.ap[0][0], 128], [64, 8], [1, 32]])
                vse(c, ve.tensor_tensor(d2, rtmp1[:, 0:256].rearrange("p (h j) -> p h j", h=8),
                                        rtmp2[:, 0:256].rearrange("p (h j) -> p h j", h=8), ALU.add))

        for s_ in range(2):
            soff = s_ * L
            vse(c, ve.memset(bass.AP(v_sb[:].tensor, 64, [[v_sb[:].ap[0][0], 128], [1048, NT_SEQ], [65, 16]]), 1.0))
            for i in range(NT_SEQ):
                bb = i % 2
                cx = c.cnt(f"atX{bb}")
                cs0 = c.cnt(f"atS{bb}")
                if ln_read[bb]:
                    se.wait_ge(c.vE.sem, ln_read[bb])
                if out_done[bb]:
                    se.wait_ge(cs0.sem, out_done[bb])
                tk = cx.inc(se.dma_start(out=sx[bb][:], in_=x_in[soff + i * 128: soff + (i + 1) * 128, :]), 16)
                ve.wait_ge(cx.sem, tk)
                ve.wait_ge(c.tE.sem, c.tE.val)
                at_ln = emit_ln_tile(c, sx[bb], sxc, k_rope, sstat, sxln)
                ln_read[bb] = c.vE.val
                te.wait_ge(c.aE.sem, at_ln)
                for k in range(8):
                    transpose_bf(sxln[:, k * 128:(k + 1) * 128], xlnT[:, k * 128:(k + 1) * 128], k % 2)
                te.wait_ge(c.aE.sem, c.aE.val)
                te.wait_ge(c.vE.sem, c.vE.val)
                for half in range(2):
                    for pp, obase in [(pq[half], half * 512), (pk[half], 1024 + half * 512), (pv[half], 2048 + half * 512)]:
                        for k in range(8):
                            mm = te.matmul(pp, xlnT[:, k * 128:(k + 1) * 128],
                                           wqkv[:, k * 3072 + obase: k * 3072 + obase + 512],
                                           start=(k == 0), stop=(k == 7))
                        c.tE.inc(mm)
                tq = c.tE.val
                ve.wait_ge(c.tE.sem, tq)
                ae.wait_ge(c.tE.sem, tq)
                ae.wait_ge(c.vE.sem, c.vE.val)
                for half in range(2):
                    c.aE.inc(ae.copy(
                        bass.AP(v_sb[:].tensor, i * 1048 + half * 8 * 65,
                                [[v_sb[:].ap[0][0], 128], [65, 8], [1, 64]]),
                        pv[half].rearrange("p (h j) -> p h j", h=8)))
                rope(q_tm[:, i * 1024:(i + 1) * 1024], pq, i)
                rope(k_rope[:], pk, i)
                ve.wait_ge(c.vE.sem, c.vE.val)
                te.wait_ge(c.vE.sem, c.vE.val)
                for k in range(8):
                    transpose_bf(k_rope[:, k * 128:(k + 1) * 128], k_fm[:, k * L + i * 128: k * L + (i + 1) * 128], k % 2)
            oev_tick = 0
            for ib in range(4):
                ib0 = ib * 512
                te.wait_ge(c.aE.sem, c.aE.val)
                for ti in range(4):
                    for k in range(8):
                        transpose_bf(q_tm[:, (ib * 4 + ti) * 1024 + k * 128: (ib * 4 + ti) * 1024 + (k + 1) * 128],
                                     qfib[:, k * 512 + ti * 128: k * 512 + (ti + 1) * 128], k % 2)
                te.wait_ge(c.aE.sem, c.aE.val)
                for h in range(H):
                    kc, ko = h // 2, (h % 2) * 64
                    NP = NT_SEQ // 2
                    exp_tick = {0: 0, 1: 0}
                    pv_tick = {0: 0, 1: 0}
                    if h == 0:
                        te.wait_ge(c.vE.sem, c.vE.val)
                        te.wait_ge(c.aE.sem, c.aE.val)
                    def emit_pv(pt):
                        par = pt % 2
                        te.wait_ge(c.aE.sem, exp_tick[par])
                        for half in range(2):
                            mt = pt * 2 + half
                            mm2 = te.matmul(pO[0:65, 0:512], v_sb[:, mt * 1048 + h * 65: mt * 1048 + (h + 1) * 65],
                                            srow2[par][:, half * 512:(half + 1) * 512],
                                            start=(mt == 0), stop=(mt == NT_SEQ - 1))
                        pv_tick[par] = c.tE.inc(mm2)
                    for pt in range(NP):
                        par = pt % 2
                        pp = pS2[par]
                        if exp_tick[par]:
                            te.wait_ge(c.aE.sem, exp_tick[par])
                        if pt == 0:
                            te.wait_ge(c.aE.sem, oev_tick if (oev_tick and h > 0) else 0)
                        for half in range(2):
                            mt = pt * 2 + half
                            mm = te.matmul(pp[:, half * 512:(half + 1) * 512],
                                           k_fm[ko:ko + 64, kc * L + mt * 128: kc * L + (mt + 1) * 128],
                                           qfib[ko:ko + 64, kc * 512:(kc + 1) * 512], start=True, stop=True)
                        ts_ = c.tE.inc(mm)
                        ae.wait_ge(c.tE.sem, ts_)
                        if pv_tick[par]:
                            ae.wait_ge(c.tE.sem, pv_tick[par])
                        at2 = c.aE.inc(ae.activation(srow2[par][:], pp[:], AF.Exp, bias=sstat[:, 7:8], scale=0.125))
                        exp_tick[par] = at2
                        if pt >= 1:
                            emit_pv(pt - 1)
                    emit_pv(NP - 1)
                    pv_used = pv_tick[(NP - 1) % 2]
                    ve.wait_ge(c.tE.sem, pv_used)
                    vse(c, ve.reciprocal(rd[0:1, 0:512], pO[64:65, 0:512]))
                    ae.wait_ge(c.tE.sem, pv_used)
                    at3 = c.aE.inc(ae.copy(o_ev[0:64, :], pO[0:64, 0:512]))
                    oev_tick = at3
                    te.wait_ge(c.vE.sem, c.vE.val)
                    mm3 = te.matmul(pB_[0:64, 0:512], ones1[0:1, 0:64], rd[0:1, 0:512], start=True, stop=True)
                    tb3 = c.tE.inc(mm3)
                    ve.wait_ge(c.tE.sem, tb3)
                    ve.wait_ge(c.aE.sem, at3)
                    c.vE.inc(ve.tensor_tensor(ofib[ko:ko + 64, kc * 512:(kc + 1) * 512], o_ev[0:64, :], pB_[0:64, 0:512], ALU.mult))
                te.wait_ge(c.vE.sem, c.vE.val)
                for ti in range(4):
                    gt = ib * 4 + ti
                    bb = gt % 2
                    cx = c.cnt(f"atX{bb}")
                    cs = c.cnt(f"atS{bb}")
                    if ln_read[bb]:
                        se.wait_ge(c.vE.sem, ln_read[bb])
                    if out_done[bb]:
                        se.wait_ge(cs.sem, out_done[bb])
                    tk = cx.inc(se.dma_start(out=sx[bb][:], in_=x_in[soff + gt * 128: soff + (gt + 1) * 128, :]), 16)
                    for oc in range(2):
                        pp = [pO, pB_][oc]
                        te.wait_ge(c.vE.sem, c.vE.val)
                        for k in range(8):
                            mm = te.matmul(pp[:], ofib[:, k * 512 + ti * 128: k * 512 + (ti + 1) * 128],
                                           wop[:, k * 1024 + oc * 512: k * 1024 + (oc + 1) * 512],
                                           start=(k == 0), stop=(k == 7))
                        tt = c.tE.inc(mm)
                        ve.wait_ge(c.tE.sem, tt)
                        ve.wait_ge(cx.sem, tk)
                        if out_done[bb]:
                            ve.wait_ge(cs.sem, out_done[bb])
                        c.vE.inc(ve.tensor_tensor(sx[bb][:, oc * 512:(oc + 1) * 512], pp[:],
                                                  sx[bb][:, oc * 512:(oc + 1) * 512], ALU.add))
                    ln_read[bb] = c.vE.val
                    se.wait_ge(c.vE.sem, c.vE.val)
                    out_done[bb] = cs.inc(se.dma_start(out=x_out[soff + gt * 128: soff + (gt + 1) * 128, :], in_=sx[bb][:]), 16)
        ge.wait_ge(c.cnt("atS0").sem, c.cnt("atS0").val)
        ge.wait_ge(c.cnt("atS1").sem, c.cnt("atS1").val)


def _prep_l1attn_weights(inputs):
    f = np.float32
    W = {}
    W["qkvT"] = np.ascontiguousarray(np.asarray(inputs["L1_attn_w_qkv"], f).T)
    W["outT"] = np.ascontiguousarray(np.asarray(inputs["L1_attn_w_out"], f).T)
    half = 32
    inv_freq = 1.0 / (10000.0 ** (np.arange(0, half, dtype=np.float64) / half))
    ang = np.arange(L, dtype=np.float64)[:, None] * inv_freq[None, :]
    cos = np.cos(ang).astype(f)
    sin = np.sin(ang).astype(f)
    W["cosh"] = np.ascontiguousarray(cos.reshape(16, 128, 32).transpose(1, 0, 2).reshape(128, 512))
    W["sinh"] = np.ascontiguousarray(sin.reshape(16, 128, 32).transpose(1, 0, 2).reshape(128, 512))
    W["id_bf16"] = np.eye(128, dtype=ml_dtypes.bfloat16)
    W["ones1"] = np.ones((1, 128), dtype=f)
    return W


L1ATTN_W_KEYS = ["qkvT", "outT", "cosh", "sinh", "id_bf16", "ones1"]


def build_l1attn_program():
    nc = bass.Bass("TRN2", target_bir_lowering=False)
    x_in = nc.dram_tensor("x", [TPC, D], F32, kind="ExternalInput").ap()
    x_out = nc.dram_tensor("xo", [TPC, D], F32, kind="ExternalOutput").ap()
    Wd = {
        "qkvT": nc.dram_tensor("qkvT", [D, 3072], F32, kind="ExternalInput").ap(),
        "outT": nc.dram_tensor("outT", [D, D], F32, kind="ExternalInput").ap(),
        "cosh": nc.dram_tensor("cosh", [128, 512], F32, kind="ExternalInput").ap(),
        "sinh": nc.dram_tensor("sinh", [128, 512], F32, kind="ExternalInput").ap(),
        "id_bf16": nc.dram_tensor("id_bf16", [128, 128], BF16, kind="ExternalInput").ap(),
        "ones1": nc.dram_tensor("ones1", [1, 128], F32, kind="ExternalInput").ap(),
    }
    with ExitStack() as ctx:
        c = Ctx(nc, ctx)
        block = ctx.enter_context(nc.Block())

        @block.sync
        def _(sync):
            emit_l1attn(c, x_in, x_out, Wd)
    return nc


def run_l1attn(x_full, W, nc):
    ims = []
    for core in range(NCORES):
        xs = np.ascontiguousarray(x_full[2 * core:2 * core + 2].reshape(TPC, D))
        im = {"x": xs}
        for k in L1ATTN_W_KEYS:
            im[k] = W[k]
        ims.append(im)
    res = run_bass_kernel_spmd(nc, ims, core_ids=list(range(NCORES)))
    out = np.empty((B, L, D), np.float32)
    for core in range(NCORES):
        out[2 * core:2 * core + 2] = np.asarray(res.results[core]["xo"]).reshape(2, L, D)
    return out





def emit_ln_gb(c: Ctx, src, sxc, sscr, sstat, g_rep, b_rep, out_bf, dn=D):
    ve, ae = c.ve, c.ae
    t = c.vE.inc(ve.reduce_sum(sstat[:, 0:1], src[:], AX.X))
    ae.wait_ge(c.vE.sem, t)
    t2 = c.aE.inc(ae.activation(sstat[:, 1:2], sstat[:, 0:1], AF.Copy, scale=-1.0 / dn))
    ve.wait_ge(c.aE.sem, t2)
    t = c.vE.inc(ve.tensor_scalar(sxc[:], src[:], sstat[:, 1:2], None, ALU.add))
    ae.wait_ge(c.vE.sem, t)
    t2 = c.aE.inc(ae.activation(sscr[:], sxc[:], AF.Square, accum_out=sstat[:, 2:3]))
    ve.wait_ge(c.aE.sem, t2)
    t = c.vE.inc(ve.tensor_scalar(sstat[:, 3:4], sstat[:, 2:3], 1.0 / dn, EPS, ALU.mult, ALU.add))
    ae.wait_ge(c.vE.sem, t)
    t2 = c.aE.inc(ae.activation(sstat[:, 0:1], sstat[:, 3:4], AF.Sqrt))
    ve.wait_ge(c.aE.sem, t2)
    t = c.vE.inc(ve.reciprocal(sstat[:, 1:2], sstat[:, 0:1]))
    ae.wait_ge(c.vE.sem, t)
    t2 = c.aE.inc(ae.mul(sscr[:], sxc[:], sstat[:, 1:2]))
    ve.wait_ge(c.aE.sem, t2)
    vse(c, ve.tensor_tensor(sxc[:], sscr[:], g_rep[:], ALU.mult))
    t = c.vE.inc(ve.tensor_tensor(out_bf, sxc[:], b_rep[:], ALU.add))
    return t


def emit_l1ffn(c: Ctx, x_in, x_out, W):
    nc, te, ve, ae, ge, se = c.nc, c.te, c.ve, c.ae, c.ge, c.se
    HI = 4
    RG = [list(range(NCORES))]

    with ExitStack() as ps:
        _u = c.bump()
        sbuf = lambda n, s, d: ps.enter_context(nc.sbuf_tensor(f"{n}_{_u}", s, d))
        psum = lambda n, s, d=F32: ps.enter_context(nc.psum_tensor(f"{n}_{_u}", s, d))

        idb = sbuf("lf_idb", [128, 128], BF16)
        idf = sbuf("lf_idf", [128, 128], F32)
        ones1 = sbuf("lf_ones1", [1, 128], F32)
        onescol_bf = sbuf("lf_onescol", [128, 1], BF16)
        ones81 = sbuf("lf_ones81", [8, 1], F32)
        blockmask = sbuf("lf_bmask", [128, 128], BF16)
        eramp = sbuf("lf_eramp", [128, 8], F32)
        selmat = sbuf("lf_selmat", [64, 8], F32)
        premask = sbuf("lf_premask", [64, 8], F32)
        p2idx = sbuf("lf_p2idx", [128, 32], U32)
        invsel = sbuf("lf_invsel", [128, 32], U32)
        p6idx = sbuf("lf_p6idx", [128, 32], U32)
        idsb = sbuf("lf_idsb", [128, 128], F32)
        inb_rep = sbuf("lf_inb", [128, 3072], BF16)
        outb_rep = sbuf("lf_outb", [128, 1024], BF16)
        ln1g = sbuf("lf_ln1g", [128, 1024], F32)
        ln1b = sbuf("lf_ln1b", [128, 1024], F32)
        ln2g = sbuf("lf_ln2g", [128, 1024], F32)
        ln2b = sbuf("lf_ln2b", [128, 1024], F32)
        b1r = sbuf("lf_b1r", [128, 32], F32)
        b2rep = sbuf("lf_b2rep", [128, 1024], BF16)
        wg_sb = sbuf("lf_wg", [128, 8 * 8], BF16)
        wA = sbuf("lf_wA", [128, 8 * 3072], BF16)
        wB = sbuf("lf_wB", [128, 8 * 1024], BF16)
        sx = [sbuf(f"lf_sx{i}", [128, 1024], F32) for i in range(2)]
        sxc = sbuf("lf_sxc", [128, 1024], F32)
        sscr = sbuf("lf_sscr", [128, 1024], F32)
        sstat = sbuf("lf_sstat", [128, 16], F32)
        sbf = sbuf("lf_sbf", [128, 1024], BF16)
        xgt = sbuf("lf_xgt", [128, 1024], BF16)
        xgT = sbuf("lf_xgT", [128, 8 * 128], BF16)
        qkv_sb = sbuf("lf_qkv", [128, 3072], BF16)
        qfm = sbuf("lf_qfm", [128, 1024], BF16)
        kfm = sbuf("lf_kfm", [128, 1024], BF16)
        es = sbuf("lf_es", [128, 512], BF16)
        esm = sbuf("lf_esm", [128, 512], BF16)
        rd1 = sbuf("lf_rd1", [1, 512], F32)
        aout = sbuf("lf_aout", [128, 8 * 128], BF16)
        y1pre = sbuf("lf_y1pre", [128, 1024], F32)
        oev = y1pre
        g_sb = sbuf("lf_gsb", [128, 8], F32)
        eg = sbuf("lf_eg", [128, 8], F32)
        route = sbuf("lf_route", [128, 3 * 32], F32)
        slu = sbuf("lf_slu", [128, 32], U32)
        sl2u = sbuf("lf_sl2u", [128, 32], U32)
        rscr = sbuf("lf_rscr", [128, 16], F32)
        ohT = sbuf("lf_ohT", [8, TPC], F32)
        pfxT = sbuf("lf_pfxT", [8, TPC], F32)
        allcnt = sbuf("lf_allcnt", [64, 16], F32)
        offlm = sbuf("lf_offlm", [16, 8], F32)
        off_tile = sbuf("lf_offtile", [128, 8], F32)
        ptile = sbuf("lf_ptile", [128, 8], F32)
        invu = sbuf("lf_invu", [128, 256], U32)
        dispfm = sbuf("lf_dispfm", [128, 8 * 2048], BF16)
        hstg = [sbuf(f"lf_hstg{i}", [128, 512], BF16) for i in range(2)]
        gbuf = sbuf("lf_gbuf", [128, 1024], BF16)
        y1t = sbuf("lf_y1t", [128, 1024], BF16)
        zero_sb = sbuf("lf_zero", [128, 256], F32)

        p0 = psum("lf_p0", [128, 512], F32)
        p1 = psum("lf_p1", [128, 512], F32)
        p2 = psum("lf_p2", [128, 512], F32)
        p3 = psum("lf_p3", [128, 512], F32)
        p4 = psum("lf_p4", [128, 512], F32)
        p5 = psum("lf_p5", [128, 512], F32)
        ptb = [psum(f"lf_ptb{i}", [128, 1024], BF16) for i in range(2)]

        DR = W

        for dst, key in [(idb, "id_bf16"), (idf, "id_f32"), (ones1, "ones1"),
                         (blockmask, "blockmask"), (eramp, "eramp_rep"),
                         (selmat, "selmat"), (premask, "premask"),
                         (p2idx, "p2idx"), (p6idx, "p6idx"), (idsb, "ids_base"),
                         (invsel, "invsel"),
                         (inb_rep, "inb_rep"), (outb_rep, "outb_rep"),
                         (ln1g, "ln1g_rep"), (ln1b, "ln1b_rep"),
                         (ln2g, "ln2g_rep"), (ln2b, "ln2b_rep"),
                         (b1r, "b1r"), (b2rep, "b2_rep")]:
            c.dW.inc(ge.dma_start(out=dst[:], in_=DR[key]), 16)
        c.dW.inc(ge.dma_start(out=wg_sb[:], in_=DR["wgT"].rearrange("(k p) e -> p k e", p=128)), 16)
        c.dW.inc(ge.dma_start(out=wA[:], in_=DR["inwT"].rearrange("(k p) o -> p k o", p=128)), 16)
        c.dW.inc(ge.dma_start(out=wB[:], in_=DR["outwT"].rearrange("(k p) o -> p k o", p=128)), 16)
        const_tick = c.dW.val
        te.wait_ge(c.dW.sem, const_tick)
        ve.wait_ge(c.dW.sem, const_tick)
        ae.wait_ge(c.dW.sem, const_tick)
        vse(c, ve.memset(onescol_bf[:], 1.0))
        vse(c, ve.memset(ones81[:], 1.0))
        vse(c, ve.memset(zero_sb[:], 0.0))
        vse(c, ve.memset(sstat[:, 7:8], -8.0))

        ln_read = {0: 0, 1: 0}
        tr_evac = {0: 0, 1: 0}
        if not hasattr(c, "r_bnd"):
            c.r_bnd = ge.alloc_register("lf_bnd")
            ge.reg_mov(c.r_bnd, 32767)
        r_bnd = c.r_bnd

        def transpose_bf(src_ap, dst_ap, pb):
            nsrc_p, nsrc_f = src_ap.shape[0], src_ap.shape[1]
            if tr_evac[pb]:
                te.wait_ge(c.aE.sem, tr_evac[pb])
            tt = c.tE.inc(te.transpose(ptb[pb][0:nsrc_f, 0:nsrc_p], src_ap, idb[0:nsrc_p, 0:nsrc_p]))
            ae.wait_ge(c.tE.sem, tt)
            tr_evac[pb] = c.aE.inc(ae.copy(dst_ap, ptb[pb][0:nsrc_f, 0:nsrc_p]))
            return tr_evac[pb]

        ftr_ev = {4: 0, 5: 0}

        def transpose_f32(src_ap, dst_ap, which):
            pp = {4: p4, 5: p5}[which]
            nsrc_p, nsrc_f = src_ap.shape[0], src_ap.shape[1]
            te.wait_ge(c.aE.sem, c.aE.val)
            te.wait_ge(c.vE.sem, c.vE.val)
            tt = c.tE.inc(te.transpose(pp[0:nsrc_f, 0:nsrc_p], src_ap, idf[0:nsrc_p, 0:nsrc_p]))
            ae.wait_ge(c.tE.sem, tt)
            ftr_ev[which] = c.aE.inc(ae.copy(dst_ap, pp[0:nsrc_f, 0:nsrc_p]))
            return ftr_ev[which]

        cs1 = c.cnt("lfP1S")
        sbf_done = {}
        for j in range(NT):
            bb = j % 2
            cx = c.cnt(f"lfX{bb}")
            if ln_read[bb]:
                se.wait_ge(c.vE.sem, ln_read[bb])
            tk = cx.inc(se.dma_start(out=sx[bb][:], in_=x_in[j * 128:(j + 1) * 128, :]), 16)
            ve.wait_ge(cx.sem, tk)
            if j >= 1:
                ve.wait_ge(cs1.sem, sbf_done[j - 1])
            at_ln = emit_ln_tile(c, sx[bb], sxc, sscr, sstat, sbf)
            ln_read[bb] = c.vE.val
            se.wait_ge(c.aE.sem, at_ln)
            sbf_done[j] = cs1.inc(se.dma_start(out=DR["my_xln"][j * 128:(j + 1) * 128, :], in_=sbf[:]), 16)
            if j in (7, 15, 23):
                q = j // 8
                ge.wait_ge(cs1.sem, sbf_done[j])
                if q == 0:
                    ccxga = c.cnt("ccXgA")
                ccxa_t = ccxga.inc(ge.collective_compute(
                    "AllGather", ALU.bypass, replica_groups=RG,
                    ins=[DR["my_xln"][q * 1024:(q + 1) * 1024, :]],
                    outs=[DR["xg_all"][q * 8192:(q + 1) * 8192, :]]))
        ge.wait_ge(cs1.sem, cs1.val)
        ccxa_t = ccxga.inc(ge.collective_compute(
            "AllGather", ALU.bypass, replica_groups=RG,
            ins=[DR["my_xln"][3072:4096, :]], outs=[DR["xg_all"][24576:32768, :]]))
        ge.wait_ge(ccxga.sem, ccxa_t)

        cy1 = c.cnt("lfY1S")
        cg2 = c.cnt("lfP2G")
        y1_done = {}
        for j in range(NT):
            ge.wait_ge(c.tE.sem, c.tE.val)
            ge.wait_ge(c.vE.sem, c.vE.val)
            gk = cg2.inc(ge.indirect_dma_start(
                xgt[:], None, DR["xg_all"][:], bass.IndirectOffsetOnAxis(p2idx[:, j:j + 1], 0),
                bounds_check=r_bnd, oob_is_err=False), 16)
            te.wait_ge(cg2.sem, gk)
            if "dbg5" in DR and j == 0:
                cdbg0 = c.cnt("lfDbg0")
                cdbg0.inc(ge.dma_start(out=DR["dbg5"][0:128, 0:1024], in_=xgt[:]), 16)
                ge.wait_ge(cdbg0.sem, cdbg0.val)
            te.wait_ge(c.aE.sem, c.aE.val)
            for k in range(8):
                transpose_bf(xgt[:, k * 128:(k + 1) * 128], xgT[:, k * 128:(k + 1) * 128], k % 2)
            te.wait_ge(c.aE.sem, c.aE.val)
            te.wait_ge(c.vE.sem, c.vE.val)
            for ob in range(6):
                pp = [p0, p1, p2, p3, p4, p5][ob]
                for k in range(8):
                    mm = te.matmul(pp[:], xgT[:, k * 128:(k + 1) * 128],
                                   wA[:, k * 3072 + ob * 512: k * 3072 + (ob + 1) * 512],
                                   start=(k == 0), stop=(k == 7))
                tq = c.tE.inc(mm)
                ve.wait_ge(c.tE.sem, tq)
                c.vE.inc(ve.tensor_tensor(qkv_sb[:, ob * 512:(ob + 1) * 512], pp[:],
                                          inb_rep[:, ob * 512:(ob + 1) * 512], ALU.add))
            if "dbg5" in DR and j == 0:
                ve.wait_ge(c.vE.sem, c.vE.val)
                cdbg0 = c.cnt("lfDbg0")
                ge.wait_ge(c.vE.sem, c.vE.val)
                cdbg0.inc(ge.dma_start(out=DR["dbg5"][128:256, 0:2048], in_=qkv_sb[:, 0:2048]), 16)
                ge.wait_ge(cdbg0.sem, cdbg0.val)
            te.wait_ge(c.vE.sem, c.vE.val)
            for h in range(HI):
                for cc_ in range(2):
                    transpose_bf(qkv_sb[:, h * 256 + cc_ * 128: h * 256 + (cc_ + 1) * 128],
                                 qfm[:, h * 256 + cc_ * 128: h * 256 + (cc_ + 1) * 128], 0)
                    transpose_bf(qkv_sb[:, 1024 + h * 256 + cc_ * 128: 1024 + h * 256 + (cc_ + 1) * 128],
                                 kfm[:, h * 256 + cc_ * 128: h * 256 + (cc_ + 1) * 128], 1)
            te.wait_ge(c.aE.sem, c.aE.val)
            te.wait_ge(c.vE.sem, c.vE.val)
            for h in range(HI):
                for cc_ in range(2):
                    mm = te.matmul(p0[:, h * 128:(h + 1) * 128],
                                   kfm[:, h * 256 + cc_ * 128: h * 256 + (cc_ + 1) * 128],
                                   qfm[:, h * 256 + cc_ * 128: h * 256 + (cc_ + 1) * 128],
                                   start=(cc_ == 0), stop=(cc_ == 1))
            ts_ = c.tE.inc(mm)
            ae.wait_ge(c.tE.sem, ts_)
            ae.wait_ge(c.vE.sem, c.vE.val)
            at2 = c.aE.inc(ae.activation(es[:], p0[:], AF.Exp, bias=sstat[:, 7:8], scale=1.0 / 16.0))
            ve.wait_ge(c.aE.sem, at2)
            vse(c, ve.tensor_tensor(
                esm[:].rearrange("p (h t) -> p h t", h=4),
                es[:].rearrange("p (h t) -> p h t", h=4),
                bass.AP(blockmask[:].tensor, 0, [[blockmask[:].ap[0][0], 128], [0, 4], [1, 128]]),
                ALU.mult))
            te.wait_ge(c.vE.sem, c.vE.val)
            for h in range(HI):
                pp = [p2, p4][h // 2]
                for hd in range(2):
                    mm = te.matmul(pp[:, (h % 2) * 256 + hd * 128: (h % 2) * 256 + (hd + 1) * 128],
                                   qkv_sb[:, 2048 + h * 256 + hd * 128: 2048 + h * 256 + (hd + 1) * 128],
                                   esm[:, h * 128:(h + 1) * 128], start=True, stop=True)
                mmd = te.matmul(p3[0:1, h * 128:(h + 1) * 128], onescol_bf[:],
                                esm[:, h * 128:(h + 1) * 128], start=True, stop=True)
            td = c.tE.inc(mmd)
            ve.wait_ge(c.tE.sem, td)
            vse(c, ve.reciprocal(rd1[0:1, 0:512], p3[0:1, 0:512]))
            te.wait_ge(c.vE.sem, c.vE.val)
            for b_ in range(4):
                mm = te.matmul(p3[:, b_ * 128:(b_ + 1) * 128], ones1[0:1, 0:128],
                               rd1[0:1, (b_ // 2) * 128:(b_ // 2 + 1) * 128], start=True, stop=True)
                mm = te.matmul(p5[:, b_ * 128:(b_ + 1) * 128], ones1[0:1, 0:128],
                               rd1[0:1, 256 + (b_ // 2) * 128: 256 + (b_ // 2 + 1) * 128], start=True, stop=True)
            tb3 = c.tE.inc(mm)
            ae.wait_ge(c.tE.sem, td)
            ae.wait_ge(c.vE.sem, c.vE.val)
            at3 = c.aE.inc(ae.copy(oev[:, 0:512], p2[:]))
            at3 = c.aE.inc(ae.copy(oev[:, 512:1024], p4[:]))
            ve.wait_ge(c.aE.sem, at3)
            ve.wait_ge(c.tE.sem, tb3)
            c.vE.inc(ve.tensor_tensor(aout[:, 0:512], oev[:, 0:512], p3[:], ALU.mult))
            c.vE.inc(ve.tensor_tensor(aout[:, 512:1024], oev[:, 512:1024], p5[:], ALU.mult))
            te.wait_ge(c.vE.sem, c.vE.val)
            for oc in range(2):
                pp = [p0, p1][oc]
                for k in range(8):
                    mm = te.matmul(pp[:], aout[:, k * 128:(k + 1) * 128],
                                   wB[:, k * 1024 + oc * 512: k * 1024 + (oc + 1) * 512],
                                   start=(k == 0), stop=(k == 7))
                to = c.tE.inc(mm)
                ve.wait_ge(c.tE.sem, to)
                vse(c, ve.tensor_tensor(y1pre[:, oc * 512:(oc + 1) * 512], pp[:],
                                        outb_rep[:, oc * 512:(oc + 1) * 512], ALU.add))
            vse(c, ve.tensor_tensor(y1pre[:], y1pre[:], xgt[:], ALU.add))
            if j >= 1:
                ve.wait_ge(cy1.sem, y1_done[j - 1])
            lt = emit_ln_gb(c, y1pre, sxc, sscr, sstat, ln1g, ln1b, sbf[:])
            se.wait_ge(c.vE.sem, lt)
            y1_done[j] = cy1.inc(se.dma_start(out=DR["my_y1"][j * 128:(j + 1) * 128, :], in_=sbf[:]), 16)
            if j in (7, 15, 23):
                q = j // 8
                ge.wait_ge(cy1.sem, y1_done[j])
                if q == 0:
                    ccy1a = c.cnt("ccY1a")
                ccy1a.inc(ge.collective_compute(
                    "AllGather", ALU.bypass, replica_groups=RG,
                    ins=[DR["my_y1"][q * 1024:(q + 1) * 1024, :]],
                    outs=[DR["y1g"][q * 8192:(q + 1) * 8192, :]]))
            te.wait_ge(c.vE.sem, lt)
            for k in range(8):
                transpose_bf(sbf[:, k * 128:(k + 1) * 128], xgT[:, k * 128:(k + 1) * 128], k % 2)
            te.wait_ge(c.aE.sem, c.aE.val)
            for k in range(8):
                mm = te.matmul(p0[:, 0:8], xgT[:, k * 128:(k + 1) * 128],
                               wg_sb[:, k * 8:(k + 1) * 8], start=(k == 0), stop=(k == 7))
            tg = c.tE.inc(mm)
            ae.wait_ge(c.tE.sem, tg)
            at4 = c.aE.inc(ae.copy(g_sb[:], p0[:, 0:8]))
            ve.wait_ge(c.aE.sem, at4)
            vse(c, ve.reduce_max(rscr[:, 0:1], g_sb[:], AX.X))
            vse(c, ve.tensor_scalar(rscr[:, 1:2], rscr[:, 0:1], -1.0, None, ALU.mult))
            ae.wait_ge(c.vE.sem, c.vE.val)
            at5 = c.aE.inc(ae.activation(eg[:], g_sb[:], AF.Exp, bias=rscr[:, 1:2], accum_out=rscr[:, 2:3]))
            ve.wait_ge(c.aE.sem, at5)
            vse(c, ve.reciprocal(route[:, 32 + j: 32 + j + 1], rscr[:, 2:3]))
            vse(c, ve.tensor_scalar(rscr[:, 3:11], g_sb[:], rscr[:, 0:1], None, ALU.is_ge))
            vse(c, ve.tensor_scalar(sstat[:, 8:16], rscr[:, 3:11], -99.0, 99.0, ALU.mult, ALU.add))
            vse(c, ve.tensor_tensor(rscr[:, 3:11], rscr[:, 3:11], eramp[:], ALU.mult))
            vse(c, ve.tensor_tensor(rscr[:, 3:11], rscr[:, 3:11], sstat[:, 8:16], ALU.add))
            vse(c, ve.tensor_reduce(route[:, j:j + 1], rscr[:, 3:11], AX.X, ALU.min))
            vse(c, ve.tensor_scalar(rscr[:, 3:11], eramp[:], route[:, j:j + 1], None, ALU.is_equal))
            transpose_f32(rscr[:, 3:11], ohT[:, j * 128:(j + 1) * 128], 4)

        ve.wait_ge(c.aE.sem, c.aE.val)
        for l in range(16):
            vse(c, ve.tensor_tensor_scan(
                bass.AP(pfxT[:].tensor, l, [[pfxT[:].ap[0][0], 8], [16, 256]]),
                bass.AP(ones81[:].tensor, 0, [[ones81[:].ap[0][0], 8], [0, 256]]),
                bass.AP(ohT[:].tensor, l, [[ohT[:].ap[0][0], 8], [16, 256]]),
                0.0, ALU.mult, ALU.add))
        cw2 = c.cnt("lfCnt")
        se.wait_ge(c.vE.sem, c.vE.val)
        ct = cw2.inc(se.dma_start(out=DR["cnt_d"][:], in_=pfxT[:, TPC - 16: TPC]), 16)
        ge.wait_ge(cw2.sem, ct)
        cc2 = c.cc.inc(ge.collective_compute(
            "AllGather", ALU.bypass, replica_groups=RG,
            ins=[DR["cnt_d"][:]], outs=[DR["cntg"][:]]))
        ge.wait_ge(c.cc.sem, cc2)
        ct2 = cw2.inc(ge.dma_start(out=allcnt[:], in_=DR["cntg"][:]), 16)
        te.wait_ge(cw2.sem, ct2)
        te.wait_ge(c.vE.sem, c.vE.val)
        te.wait_ge(c.aE.sem, c.aE.val)
        mm = te.matmul(p4[0:8, 0:16], selmat[:], allcnt[:], start=True, stop=True)
        c.tE.inc(mm)
        mm = te.matmul(p5[0:8, 0:16], premask[:], allcnt[:], start=True, stop=True)
        t5_ = c.tE.inc(mm)
        ve.wait_ge(c.tE.sem, t5_)
        vse(c, ve.tensor_copy(rscr[0:8, 0:16], p4[0:8, 0:16]))
        vse(c, ve.tensor_tensor_scan(
            sstat[0:8, 0:16] if False else sscr[0:8, 0:16],
            bass.AP(ones81[:].tensor, 0, [[ones81[:].ap[0][0], 8], [0, 16]]),
            rscr[0:8, 0:16], 0.0, ALU.mult, ALU.add))
        vse(c, ve.tensor_tensor(sscr[0:8, 0:16], sscr[0:8, 0:16], rscr[0:8, 0:16], ALU.subtract))
        vse(c, ve.tensor_tensor(sscr[0:8, 0:16], sscr[0:8, 0:16], p5[0:8, 0:16], ALU.add))
        vse(c, ve.tensor_scalar(sscr[0:8, 0:16], sscr[0:8, 0:16], -1.0, None, ALU.add))
        transpose_f32(sscr[0:8, 0:16], offlm[:], 5)
        cw3 = c.cnt("lfOff")
        se.wait_ge(c.aE.sem, c.aE.val)
        o1 = cw3.inc(se.dma_start(out=DR["off_d"][:], in_=offlm[:]), 16)
        se.wait_ge(cw3.sem, o1)
        for r8 in range(8):
            cw3.inc(se.dma_start(out=off_tile[r8 * 16:(r8 + 1) * 16, :], in_=DR["off_d"][:]), 16)
        ve.wait_ge(cw3.sem, cw3.val)

        cz = c.cnt("lfZ")
        vz = vse(c, ve.memset(sscr[:], 0.0))
        se.wait_ge(c.vE.sem, vz)
        zt = cz.inc(se.dma_start(out=DR["inv2_p"].rearrange("(p q) x -> p (q x)", p=128), in_=sscr[:]), 16)
        ge.wait_ge(cz.sem, zt)
        if "dbg3" in DR:
            cdbg3 = c.cnt("lfDbg3")
            cdbg3.inc(ge.dma_start(out=DR["dbg3"][128:256, 0:1024],
                                   in_=DR["inv2_p"].rearrange("(p q) x -> p (q x)", p=128)), 16)
            ge.wait_ge(cdbg3.sem, cdbg3.val)
        ge.wait_ge(cy1.sem, cy1.val)
        ccy1a_t = ccy1a.inc(ge.collective_compute(
            "AllGather", ALU.bypass, replica_groups=RG,
            ins=[DR["my_y1"][3072:4096, :]], outs=[DR["y1g"][24576:32768, :]]))
        for j in range(NT):
            transpose_f32(pfxT[:, j * 128:(j + 1) * 128], ptile[:], 4)
            ve.wait_ge(c.aE.sem, c.aE.val)
            vse(c, ve.tensor_tensor(ptile[:], ptile[:], off_tile[:], ALU.add))
            vse(c, ve.tensor_scalar(rscr[:, 3:11], eramp[:], route[:, j:j + 1], None, ALU.is_equal))
            vse(c, ve.tensor_tensor(rscr[:, 3:11], rscr[:, 3:11], ptile[:], ALU.mult))
            vse(c, ve.reduce_sum(rscr[:, 0:1], rscr[:, 3:11], AX.X))
            vse(c, ve.tensor_scalar(rscr[:, 1:2], rscr[:, 0:1], 4096.0, None, ALU.is_lt))
            vse(c, ve.tensor_scalar(rscr[:, 2:3], route[:, j:j + 1], 4096.0, None, ALU.mult))
            vse(c, ve.tensor_tensor(rscr[:, 2:3], rscr[:, 2:3], rscr[:, 0:1], ALU.add))
            vse(c, ve.tensor_scalar(rscr[:, 11:12], rscr[:, 1:2], -1e6, 1e6, ALU.mult, ALU.add))
            vse(c, ve.tensor_tensor(route[:, 64 + j: 64 + j + 1], rscr[:, 2:3], rscr[:, 11:12], ALU.add))
            vse(c, ve.tensor_copy(slu[:, j:j + 1], route[:, 64 + j: 64 + j + 1]))
            vse(c, ve.tensor_copy(sl2u[:, j:j + 1], rscr[:, 2:3]))
            vse(c, ve.tensor_copy(invu[:, 255:256], rscr[:, 11:12]))
            vse(c, ve.tensor_scalar(invu[:, 254:255], sl2u[:, j:j + 1], 127, None, ALU.bitwise_and))
            vse(c, ve.tensor_scalar(invu[:, 254:255], invu[:, 254:255], 8, None, ALU.logical_shift_left))
            vse(c, ve.tensor_scalar(sl2u[:, j:j + 1], sl2u[:, j:j + 1], 7, None, ALU.logical_shift_right))
            vse(c, ve.tensor_tensor(sl2u[:, j:j + 1], sl2u[:, j:j + 1], invu[:, 254:255], ALU.add))
            vse(c, ve.tensor_tensor(sl2u[:, j:j + 1], sl2u[:, j:j + 1], invu[:, 255:256], ALU.add))
            ge.wait_ge(c.vE.sem, c.vE.val)
            c.gE.inc(ge.indirect_dma_start(
                DR["inv2_p"][:], bass.IndirectOffsetOnAxis(sl2u[:, j:j + 1], 0),
                idsb[:, 4 * j: 4 * j + 4], None, bounds_check=r_bnd, oob_is_err=False), 16)
        ge.wait_ge(c.gE.sem, c.gE.val)
        if "dbg3" in DR:
            cdbg2 = c.cnt("lfDbg2")
            cdbg2.inc(ge.dma_start(out=DR["dbg3"][0:128, 0:1024],
                                   in_=DR["inv2_p"].rearrange("(p q) x -> p (q x)", p=128)), 16)
            ge.wait_ge(cdbg2.sem, cdbg2.val)
        cc4 = c.cc.inc(ge.collective_compute(
            "AllGather", ALU.bypass, replica_groups=RG,
            ins=[DR["inv2_p"].rearrange("(p q) x -> p (q x)", p=128)],
            outs=[DR["inv2_all"].rearrange("(p q) x -> p (q x)", p=1024)]))
        ge.wait_ge(c.cc.sem, cc4)
        if "dbg3" in DR:
            cdbg4 = c.cnt("lfDbg4")
            cdbg4.inc(ge.dma_start(out=DR["dbg3"][256:384, 0:1024],
                                   in_=DR["inv2_all"][0:32768, :].rearrange("(p q) x -> p (q x)", p=128)), 16)
            cdbg4.inc(ge.dma_start(out=DR["dbg3"][384:512, 0:1024],
                                   in_=DR["inv2_all"][3 * 32768:4 * 32768, :].rearrange("(p q) x -> p (q x)", p=128)), 16)
            ge.wait_ge(cdbg4.sem, cdbg4.val)
        cw4 = c.cnt("lfInv")
        invs, invs2 = y1pre, sxc
        se.wait_ge(c.cc.sem, cc4)
        i0 = cw4.inc(se.dma_start(out=invs[:], in_=DR["inv2_all"][0:32768, :].rearrange("(p q) x -> p (q x)", p=128)), 16)
        ve.wait_ge(cw4.sem, i0)
        for r in range(1, 8):
            ik = cw4.inc(se.dma_start(out=invs2[:], in_=DR["inv2_all"][r * 32768:(r + 1) * 32768, :].rearrange("(p q) x -> p (q x)", p=128)), 16)
            ve.wait_ge(cw4.sem, ik)
            vse(c, ve.tensor_tensor(invs[:], invs[:], invs2[:], ALU.add))
            if r < 7:
                se.wait_ge(c.vE.sem, c.vE.val)
        cw5 = c.cnt("lfInvW")
        se.wait_ge(c.vE.sem, c.vE.val)
        iw = cw5.inc(se.dma_start(out=DR["inv_sum_d"].rearrange("(p q) x -> p (q x)", p=128), in_=invs[:]), 16)
        ge.wait_ge(cw5.sem, iw)

        ge.wait_ge(ccy1a.sem, ccy1a_t)
        ge.wait_ge(c.vE.sem, c.vE.val)
        cgd = c.cnt("lfDisp")
        cwE = c.cnt("lfWE")
        chS = c.cnt("lfHS")
        wticks = {0: 0, 1: 0}
        wcons = {0: 0, 1: 0}
        hdone = {0: 0, 1: 0}

        def load_w1(oc, b):
            if wcons[b]:
                ge.wait_ge(c.tE.sem, wcons[b])
            cwb = c.cnt(f"lfW1{b}")
            cwb.inc(ge.dma_start(
                out=bass.AP(wB[:].tensor, b * 4096, [[wB[:].ap[0][0], 128], [512, 8], [1, 128]]),
                in_=DR["w1c"][:, oc * 128:(oc + 1) * 128].rearrange("(k p) f -> p k f", p=128)), 16)
            return cwb.val

        for half in range(2):
            for j in range(16):
                jj = half * 16 + j
                gi = cgd.inc(ge.indirect_dma_start(
                    oev[:, 0:4], None, DR["inv_sum_d"][:],
                    bass.IndirectOffsetOnAxis(invsel[:, jj:jj + 1], 0),
                    bounds_check=r_bnd, oob_is_err=False), 16)
                ve.wait_ge(cgd.sem, gi)
                vt_ = c.vE.inc(ve.tensor_copy(invu[:, jj:jj + 1], oev[:, 0:1]))
                ge.wait_ge(c.vE.sem, vt_)
                gk = cgd.inc(ge.indirect_dma_start(
                    gbuf[:], None, DR["y1g"][:],
                    bass.IndirectOffsetOnAxis(invu[:, jj:jj + 1], 0),
                    bounds_check=r_bnd, oob_is_err=False), 16)
                te.wait_ge(cgd.sem, gk)
                for k in range(8):
                    transpose_bf(gbuf[:, k * 128:(k + 1) * 128],
                                 dispfm[:, k * 2048 + j * 128: k * 2048 + (j + 1) * 128], k % 2)
                ge.wait_ge(c.tE.sem, c.tE.val)
            wticks[0] = load_w1(0, 0)
            wticks[1] = load_w1(1, 1)
            for oc in range(32):
                b = oc % 2
                te.wait_ge(c.cnt(f"lfW1{b}").sem, wticks[b])
                for tb in range(4):
                    pp = [p0, p1][tb % 2]
                    te.wait_ge(c.vE.sem, c.vE.val)
                    for k in range(8):
                        mm = te.matmul(pp[:], wB[:, b * 4096 + k * 512: b * 4096 + k * 512 + 128],
                                       dispfm[:, k * 2048 + tb * 512: k * 2048 + (tb + 1) * 512],
                                       start=(k == 0), stop=(k == 7))
                    th = c.tE.inc(mm)
                    ve.wait_ge(c.tE.sem, th)
                    hb_ = hstg[tb % 2]
                    if hdone[tb % 2]:
                        ve.wait_ge(chS.sem, hdone[tb % 2])
                    at6 = c.vE.inc(ve.tensor_scalar(hb_[:], pp[:], b1r[:, oc:oc + 1], None, ALU.add))
                    se.wait_ge(c.vE.sem, at6)
                    hdone[tb % 2] = chS.inc(se.dma_start(
                        out=DR["h_d"][oc, :, half * 2048 + tb * 512: half * 2048 + (tb + 1) * 512], in_=hb_[:]), 16)
                wcons[b] = c.tE.val
                if oc + 2 < 32:
                    wticks[b] = load_w1(oc + 2, b)
        se.wait_ge(chS.sem, chS.val)
        cyS = c.cnt("lfYS")
        chL = c.cnt("lfHL")
        for dh in range(2):
            ge.wait_ge(c.tE.sem, c.tE.val)
            wt = cwE.inc(ge.dma_start(
                out=bass.AP(wA[:].tensor, 0, [[wA[:].ap[0][0], 128], [512, 32], [1, 512]]),
                in_=DR["w2c"][:, dh * 512:(dh + 1) * 512].rearrange("(k p) d -> p k d", p=128)), 16)
            te.wait_ge(cwE.sem, wt)
            hload = {0: 0, 1: 0}
            hcons = {0: c.tE.val, 1: c.tE.val}
            for tt in range(32):
                hb2 = tt % 2
                hbuf_cols = hb2 * 4096
                if tt < 2:
                    se.wait_ge(c.tE.sem, hcons[hb2])
                    hload[hb2] = chL.inc(se.dma_start(
                        out=bass.AP(dispfm[:].tensor, hbuf_cols, [[dispfm[:].ap[0][0], 128], [128, 32], [1, 128]]),
                        in_=bass.AP(DR["h_d"].tensor, tt * 128, [[4096, 128], [128 * 4096, 32], [1, 128]])), 16)
                te.wait_ge(chL.sem, hload[hb2])
                if tt + 2 < 32:
                    nb = (tt + 2) % 2
                    se.wait_ge(c.tE.sem, hcons[nb] if hcons[nb] else 0)
                te.wait_ge(c.vE.sem, c.vE.val)
                pp = [p2, p3][tt % 2]
                for k in range(32):
                    mm = te.matmul(pp[:], dispfm[:, hbuf_cols + k * 128: hbuf_cols + (k + 1) * 128],
                                   wA[:, k * 512:(k + 1) * 512], start=(k == 0), stop=(k == 31))
                ty = c.tE.inc(mm)
                hcons[hb2] = ty
                if tt + 2 < 32:
                    nb2 = (tt + 2) % 2
                    se.wait_ge(c.tE.sem, hcons[nb2])
                    hload[nb2] = chL.inc(se.dma_start(
                        out=bass.AP(dispfm[:].tensor, nb2 * 4096, [[dispfm[:].ap[0][0], 128], [128, 32], [1, 128]]),
                        in_=bass.AP(DR["h_d"].tensor, (tt + 2) * 128, [[4096, 128], [128 * 4096, 32], [1, 128]])), 16)
                ve.wait_ge(c.tE.sem, ty)
                if tt >= 2:
                    ve.wait_ge(cyS.sem, ystore[tt % 2])
                vt = c.vE.inc(ve.tensor_tensor(hstg[tt % 2][:], pp[:], b2rep[:, dh * 512:(dh + 1) * 512], ALU.add))
                se.wait_ge(c.vE.sem, vt)
                if tt == 0 and dh == 0:
                    ystore = {0: 0, 1: 0}
                ystore[tt % 2] = cyS.inc(se.dma_start(
                    out=[DR["yexp_dA"], DR["yexp_dB"]][dh][tt * 128:(tt + 1) * 128, :], in_=hstg[tt % 2][:]), 16)
            if dh == 0:
                ge.wait_ge(cyS.sem, cyS.val)
                ccyea = c.cnt("ccYeA")
                ccye_t = ccyea.inc(ge.collective_compute(
                    "AllGather", ALU.bypass, replica_groups=RG,
                    ins=[DR["yexp_dA"][:]], outs=[DR["yexp_cA"][:]]))
        ge.wait_ge(cyS.sem, cyS.val)
        cc5 = c.cc.inc(ge.collective_compute(
            "AllGather", ALU.bypass, replica_groups=RG,
            ins=[DR["yexp_dB"][:]], outs=[DR["yexp_cB"][:]]))
        ge.wait_ge(c.cc.sem, cc5)
        ge.wait_ge(ccyea.sem, ccye_t)

        cg5 = c.cnt("lfP5G")
        cy2 = c.cnt("lfY2S")
        cyl = c.cnt("lfY1L")
        y2_done = {}
        for j in range(NT):
            ve.wait_ge(c.tE.sem, c.tE.val)
            vm = vse(c, ve.memset(gbuf[:], 0.0))
            ge.wait_ge(c.vE.sem, vm)
            cg5.inc(ge.indirect_dma_start(
                gbuf[:, 0:512], None, DR["yexp_cA"][:], bass.IndirectOffsetOnAxis(slu[:, j:j + 1], 0),
                bounds_check=r_bnd, oob_is_err=False), 16)
            gk = cg5.inc(ge.indirect_dma_start(
                gbuf[:, 512:1024], None, DR["yexp_cB"][:], bass.IndirectOffsetOnAxis(slu[:, j:j + 1], 0),
                bounds_check=r_bnd, oob_is_err=False), 16)
            yk = cyl.inc(se.dma_start(out=y1t[:], in_=DR["my_y1"][j * 128:(j + 1) * 128, :]), 16)
            ve.wait_ge(cg5.sem, gk)
            ve.wait_ge(cyl.sem, yk)
            vse(c, ve.scalar_tensor_tensor(y1pre[:], gbuf[:], route[:, 32 + j: 32 + j + 1], y1t[:], ALU.mult, ALU.add))
            if j >= 1:
                ve.wait_ge(cy2.sem, y2_done[j - 1])
            lt = emit_ln_gb(c, y1pre, sxc, sscr, sstat, ln2g, ln2b, sbf[:])
            se.wait_ge(c.vE.sem, lt)
            y2_done[j] = cy2.inc(se.dma_start(out=DR["y2_d"][j * 128:(j + 1) * 128, :], in_=sbf[:]), 16)
            if j in (7, 15, 23):
                q = j // 8
                ge.wait_ge(cy2.sem, y2_done[j])
                if q == 0:
                    ccy2a = c.cnt("ccY2a")
                    ccy2b = c.cnt("ccY2b")
                cq = ccy2a if q < 2 else ccy2b
                _t = cq.inc(ge.collective_compute(
                    "AllGather", ALU.bypass, replica_groups=RG,
                    ins=[DR["y2_d"][q * 1024:(q + 1) * 1024, :]],
                    outs=[DR["y2_all"][q * 8192:(q + 1) * 8192, :]]))
                if q == 1:
                    ccY1 = _t
        ge.wait_ge(cy2.sem, cy2.val)
        cc6 = ccy2b.inc(ge.collective_compute(
            "AllGather", ALU.bypass, replica_groups=RG,
            ins=[DR["y2_d"][3072:4096, :]], outs=[DR["y2_all"][24576:32768, :]]))

        if "dbg1" in DR:
            cdbg = c.cnt("lfDbg")
            ge.wait_ge(c.vE.sem, c.vE.val)
            ge.wait_ge(c.aE.sem, c.aE.val)
            cdbg.inc(ge.dma_start(out=DR["dbg1"][:], in_=DR["my_y1"][:]), 16)
            cdbg.inc(ge.dma_start(out=DR["dbg2"][0:128, 0:96], in_=route[:]), 16)
            cdbg.inc(ge.dma_start(out=DR["dbg2"][128:256, 0:8], in_=off_tile[:]), 16)
            cdbg.inc(ge.dma_start(out=DR["dbg2"][256:264, 0:2048], in_=pfxT[:, 0:2048]), 16)
            cdbg.inc(ge.dma_start(out=DR["dbg2"][264:392, 0:256], in_=invu[:]), 16)
            cdbg.inc(ge.dma_start(out=DR["dbg3"][512:4096 + 512 - 512, :], in_=DR["yexp_d"][0:3584, :]), 16)
            cdbg.inc(ge.dma_start(out=DR["dbg4"][:], in_=DR["y2_d"][:]), 16)
            ge.wait_ge(cdbg.sem, cdbg.val)
        cg6 = c.cnt("lfP6G")
        cs6 = c.cnt("lfP6S")
        x6_done = {}
        p6order = [j for j in range(NT) if (j % 16) % 2 == 0] + [j for j in range(NT) if (j % 16) % 2 == 1]
        ge.wait_ge(ccy2a.sem, ccY1)
        for i, j in enumerate(p6order):
            bb = i % 2
            gb6 = dispfm[:, bb * 1024:(bb + 1) * 1024]
            cx = c.cnt(f"lfX{bb}")
            if i == 16:
                ge.wait_ge(ccy2b.sem, cc6)
            if i >= 2:
                se.wait_ge(cs6.sem, x6_done[i - 2])
                ge.wait_ge(c.vE.sem, c.vE.val)
            tk = cx.inc(se.dma_start(out=sx[bb][:], in_=x_in[j * 128:(j + 1) * 128, :]), 16)
            gk = cg6.inc(ge.indirect_dma_start(
                gb6, None, DR["y2_all"][:], bass.IndirectOffsetOnAxis(p6idx[:, j:j + 1], 0),
                bounds_check=r_bnd, oob_is_err=False), 16)
            ve.wait_ge(cx.sem, tk)
            ve.wait_ge(cg6.sem, gk)
            vt = c.vE.inc(ve.scalar_tensor_tensor(sx[bb][:], gb6, 0.5, sx[bb][:], ALU.mult, ALU.add))
            se.wait_ge(c.vE.sem, vt)
            x6_done[i] = cs6.inc(se.dma_start(out=x_out[j * 128:(j + 1) * 128, :], in_=sx[bb][:]), 16)
        ge.wait_ge(cs6.sem, cs6.val)


def _prep_l1ffn_weights(inputs, core):
    f = np.float32
    bf = ml_dtypes.bfloat16
    W = {}
    W["inwT"] = np.ascontiguousarray(np.asarray(inputs["L1_ffn_in_proj_w"], f).T)
    W["outwT"] = np.ascontiguousarray(np.asarray(inputs["L1_ffn_out_proj_w"], f).T)
    W["wgT"] = np.ascontiguousarray(np.asarray(inputs["L1_ffn_wg"], f))
    W["w1c"] = np.ascontiguousarray(np.asarray(inputs["L1_ffn_w1"], f)[core])
    W["w2c"] = np.ascontiguousarray(np.asarray(inputs["L1_ffn_w2"], f)[core])
    b1 = np.asarray(inputs["L1_ffn_b1"], f)[core]
    W["b1r"] = np.ascontiguousarray(b1.reshape(32, 128).T)
    W["b2_rep"] = np.ascontiguousarray(np.broadcast_to(np.asarray(inputs["L1_ffn_b2"], f)[core], (128, 1024))).astype(bf)
    W["inb_rep"] = np.ascontiguousarray(np.broadcast_to(np.asarray(inputs["L1_ffn_in_proj_b"], f), (128, 3072))).astype(bf)
    W["outb_rep"] = np.ascontiguousarray(np.broadcast_to(np.asarray(inputs["L1_ffn_out_proj_b"], f), (128, 1024))).astype(bf)
    for nm, key in [("ln1g_rep", "L1_ffn_ln1_g"), ("ln1b_rep", "L1_ffn_ln1_b"),
                    ("ln2g_rep", "L1_ffn_ln2_g"), ("ln2b_rep", "L1_ffn_ln2_b")]:
        W[nm] = np.ascontiguousarray(np.broadcast_to(np.asarray(inputs[key], f), (128, 1024)))
    W["id_bf16"] = np.eye(128, dtype=bf)
    W["id_f32"] = np.eye(128, dtype=f)
    W["ones1"] = np.ones((1, 128), f)
    t_ = np.arange(128)
    W["blockmask"] = ((t_[:, None] // 16) == (t_[None, :] // 16)).astype(bf)
    W["eramp_rep"] = np.ascontiguousarray(np.broadcast_to(np.arange(8, dtype=f), (128, 8)))
    ce = np.arange(64)
    W["selmat"] = np.eye(8, dtype=f)[ce % 8]
    W["premask"] = (np.eye(8, dtype=f)[ce % 8]) * ((ce // 8) < core)[:, None]
    def glob(l, n):
        return (l // 2) * 4096 + (l % 2) * 2048 + n
    p2 = np.zeros((128, 32), np.uint32)
    for p in range(128):
        n_sub, l = p // 16, p % 16
        for j in range(32):
            p2[p, j] = glob(l, 256 * core + j * 8 + n_sub)
    p2 = ((p2 % 4096) // 1024) * 8192 + (p2 // 4096) * 1024 + (p2 % 1024)
    W["p2idx"] = p2
    p6 = np.zeros((128, 32), np.uint32)
    for j in range(32):
        b_loc = j // 16
        l = 2 * core + b_loc
        for p in range(128):
            n = (j % 16) * 128 + p
            r = n // 256
            p6[p, j] = r * 4096 + (n % 256) * 16 + l
    p6 = ((p6 % 4096) // 1024) * 8192 + (p6 // 4096) * 1024 + (p6 % 1024)
    W["p6idx"] = p6
    ids = np.zeros((128, 128), f)
    for j in range(32):
        loc = j * 128 + np.arange(128)
        for q in range(4):
            ids[:, 4 * j + q] = (loc // 1024) * 8192 + core * 1024 + (loc % 1024)
    W["ids_base"] = ids
    isel = np.zeros((128, 32), np.uint32)
    for j in range(32):
        isel[:, j] = np.arange(128) * 256 + core * 32 + j
    W["invsel"] = isel
    return W


L1FFN_W_KEYS = ["inwT", "outwT", "wgT", "w1c", "w2c", "b1r", "b2_rep", "inb_rep",
                "outb_rep", "ln1g_rep", "ln1b_rep", "ln2g_rep", "ln2b_rep",
                "id_bf16", "id_f32", "ones1", "blockmask", "eramp_rep",
                "selmat", "premask", "p2idx", "p6idx", "ids_base", "invsel"]


def build_l1ffn_program(dbg=False):
    nc = bass.Bass("TRN2", target_bir_lowering=False)
    x_in = nc.dram_tensor("x", [TPC, D], F32, kind="ExternalInput").ap()
    x_out = nc.dram_tensor("xo", [TPC, D], F32, kind="ExternalOutput").ap()
    Wd = {
        "inwT": nc.dram_tensor("inwT", [D, 3072], F32, kind="ExternalInput").ap(),
        "outwT": nc.dram_tensor("outwT", [D, D], F32, kind="ExternalInput").ap(),
        "wgT": nc.dram_tensor("wgT", [D, 8], F32, kind="ExternalInput").ap(),
        "w1c": nc.dram_tensor("w1c", [D, 4096], F32, kind="ExternalInput").ap(),
        "w2c": nc.dram_tensor("w2c", [4096, D], F32, kind="ExternalInput").ap(),
        "b1r": nc.dram_tensor("b1r", [128, 32], F32, kind="ExternalInput").ap(),
        "b2_rep": nc.dram_tensor("b2_rep", [128, 1024], BF16, kind="ExternalInput").ap(),
        "inb_rep": nc.dram_tensor("inb_rep", [128, 3072], BF16, kind="ExternalInput").ap(),
        "outb_rep": nc.dram_tensor("outb_rep", [128, 1024], BF16, kind="ExternalInput").ap(),
        "ln1g_rep": nc.dram_tensor("ln1g_rep", [128, 1024], F32, kind="ExternalInput").ap(),
        "ln1b_rep": nc.dram_tensor("ln1b_rep", [128, 1024], F32, kind="ExternalInput").ap(),
        "ln2g_rep": nc.dram_tensor("ln2g_rep", [128, 1024], F32, kind="ExternalInput").ap(),
        "ln2b_rep": nc.dram_tensor("ln2b_rep", [128, 1024], F32, kind="ExternalInput").ap(),
        "id_bf16": nc.dram_tensor("id_bf16", [128, 128], BF16, kind="ExternalInput").ap(),
        "id_f32": nc.dram_tensor("id_f32", [128, 128], F32, kind="ExternalInput").ap(),
        "ones1": nc.dram_tensor("ones1", [1, 128], F32, kind="ExternalInput").ap(),
        "blockmask": nc.dram_tensor("blockmask", [128, 128], BF16, kind="ExternalInput").ap(),
        "eramp_rep": nc.dram_tensor("eramp_rep", [128, 8], F32, kind="ExternalInput").ap(),
        "selmat": nc.dram_tensor("selmat", [64, 8], F32, kind="ExternalInput").ap(),
        "premask": nc.dram_tensor("premask", [64, 8], F32, kind="ExternalInput").ap(),
        "p2idx": nc.dram_tensor("p2idx", [128, 32], U32, kind="ExternalInput").ap(),
        "p6idx": nc.dram_tensor("p6idx", [128, 32], U32, kind="ExternalInput").ap(),
        "ids_base": nc.dram_tensor("ids_base", [128, 128], F32, kind="ExternalInput").ap(),
        "invsel": nc.dram_tensor("invsel", [128, 32], U32, kind="ExternalInput").ap(),
        "my_xln": nc.dram_tensor("my_xln", [TPC, D], BF16).ap(),
        "xg_all": nc.dram_tensor("xg_all", [8 * TPC, D], BF16).ap(),
        "my_y1": nc.dram_tensor("my_y1", [TPC, D], BF16).ap(),
        "y1g": nc.dram_tensor("y1g", [8 * TPC, D], BF16).ap(),
        "cnt_d": nc.dram_tensor("cnt_d", [8, 16], F32).ap(),
        "cntg": nc.dram_tensor("cntg", [64, 16], F32).ap(),
        "off_d": nc.dram_tensor("off_d", [16, 8], F32).ap(),
        "inv2_p": nc.dram_tensor("inv2_p", [32768, 4], F32).ap(),
        "inv2_all": nc.dram_tensor("inv2_all", [8 * 32768, 4], F32).ap(),
        "inv_sum_d": nc.dram_tensor("inv_sum_d", [32768, 4], F32).ap(),
        "h_d": nc.dram_tensor("h_d", [32, 128, 4096], BF16).ap(),
        "yexp_dA": nc.dram_tensor("yexp_dA", [4096, 512], BF16).ap(),
        "yexp_dB": nc.dram_tensor("yexp_dB", [4096, 512], BF16).ap(),
        "yexp_cA": nc.dram_tensor("yexp_cA", [8 * 4096, 512], BF16).ap(),
        "yexp_cB": nc.dram_tensor("yexp_cB", [8 * 4096, 512], BF16).ap(),
        "y2_d": nc.dram_tensor("y2_d", [TPC, D], BF16).ap(),
        "y2_all": nc.dram_tensor("y2_all", [8 * TPC, D], BF16).ap(),
    }
    if dbg:
        Wd["dbg1"] = nc.dram_tensor("dbg1", [TPC, D], F32, kind="ExternalOutput").ap()
        Wd["dbg2"] = nc.dram_tensor("dbg2", [392, 2048], F32, kind="ExternalOutput").ap()
        Wd["dbg3"] = nc.dram_tensor("dbg3", [4096, D], F32, kind="ExternalOutput").ap()
        Wd["dbg4"] = nc.dram_tensor("dbg4", [TPC, D], F32, kind="ExternalOutput").ap()
        Wd["dbg5"] = nc.dram_tensor("dbg5", [256, 2048], F32, kind="ExternalOutput").ap()
    with ExitStack() as ctx:
        c = Ctx(nc, ctx)
        block = ctx.enter_context(nc.Block())

        @block.sync
        def _(sync):
            emit_l1ffn(c, x_in, x_out, Wd)
    return nc


def run_l1ffn(x_full, Ws, nc):
    ims = []
    for core in range(NCORES):
        xs = np.ascontiguousarray(x_full[2 * core:2 * core + 2].reshape(TPC, D))
        im = {"x": xs}
        for k in L1FFN_W_KEYS:
            im[k] = Ws[core][k]
        ims.append(im)
    res = run_bass_kernel_spmd(nc, ims, core_ids=list(range(NCORES)))
    out = np.empty((B, L, D), np.float32)
    for core in range(NCORES):
        out[2 * core:2 * core + 2] = np.asarray(res.results[core]["xo"]).reshape(2, L, D)
    return out





def phase_barrier(c):
    cnts = [c.dW, c.dX, c.dS, c.tE, c.vE, c.aE, c.gE, c.cc]
    if hasattr(c, "_cnts"):
        cnts += list(c._cnts.values())
    for cn in cnts:
        c.ge.wait_ge(cn.sem, cn.val)
    t = c.gE.inc(c.ge.nop())
    for eng in (c.te, c.ve, c.ae, c.se):
        eng.wait_ge(c.gE.sem, t)


def build_fused_program():
    nc = bass.Bass("TRN2", target_bir_lowering=False)
    x0 = nc.dram_tensor("x", [TPC, D], F32, kind="ExternalInput").ap()
    xo = nc.dram_tensor("xo", [TPC, D], F32, kind="ExternalOutput").ap()
    xA = nc.dram_tensor("xA", [TPC, D], F32).ap()
    xB = nc.dram_tensor("xB", [TPC, D], F32).ap()
    xC = nc.dram_tensor("xC", [TPC, D], F32).ap()
    xD = nc.dram_tensor("xD", [TPC, D], F32).ap()
    xE = nc.dram_tensor("xE", [TPC, D], F32).ap()

    WA = {
        "gwT": nc.dram_tensor("gwT", [D, 16], F32, kind="ExternalInput").ap(),
        "ebias": nc.dram_tensor("ebias", [128, 16], F32, kind="ExternalInput").ap(),
        "wgT": nc.dram_tensor("wgT", [16, D, D], F32, kind="ExternalInput").ap(),
        "wuT": nc.dram_tensor("wuT", [16, D, D], F32, kind="ExternalInput").ap(),
        "wdT": nc.dram_tensor("wdT", [16, D, D], F32, kind="ExternalInput").ap(),
        "sgT": nc.dram_tensor("sgT", [D, D], F32, kind="ExternalInput").ap(),
        "suT": nc.dram_tensor("suT", [D, D], F32, kind="ExternalInput").ap(),
        "sdT": nc.dram_tensor("sdT", [D, D], F32, kind="ExternalInput").ap(),
        "id_bf16": nc.dram_tensor("id_bf16", [128, 128], BF16, kind="ExternalInput").ap(),
        "ones1": nc.dram_tensor("ones1", [1, 128], F32, kind="ExternalInput").ap(),
    }
    WB = {
        "inprojT": nc.dram_tensor("inprojT", [D, 4384], F32, kind="ExternalInput").ap(),
        "opT": nc.dram_tensor("opT", [2048, 1024], F32, kind="ExternalInput").ap(),
        "convw": nc.dram_tensor("convw", [2304, 4], F32, kind="ExternalInput").ap(),
        "convb_r": nc.dram_tensor("convb_r", [128, 18], F32, kind="ExternalInput").ap(),
        "dtb_rep": nc.dram_tensor("dtb_rep", [128, 32], F32, kind="ExternalInput").ap(),
        "alog_rep": nc.dram_tensor("alog_rep", [128, 32], F32, kind="ExternalInput").ap(),
        "d_rep": nc.dram_tensor("d_rep", [128, 32], F32, kind="ExternalInput").ap(),
        "normw_rep": nc.dram_tensor("normw_rep", [128, 2048], BF16, kind="ExternalInput").ap(),
        "trmask": nc.dram_tensor("trmask", [128, 512], BF16, kind="ExternalInput").ap(),
        "id_bf16": WA["id_bf16"], "id_f32": nc.dram_tensor("id_f32", [128, 128], F32, kind="ExternalInput").ap(),
        "ones1": WA["ones1"],
        "acs_dram": nc.dram_tensor("acs_dram", [32, 256], F32).ap(),
    }
    WC = {
        "qkvT": nc.dram_tensor("qkvT", [D, 3072], F32, kind="ExternalInput").ap(),
        "outT": nc.dram_tensor("outT", [D, D], F32, kind="ExternalInput").ap(),
        "cosh": nc.dram_tensor("cosh", [128, 512], F32, kind="ExternalInput").ap(),
        "sinh": nc.dram_tensor("sinh", [128, 512], F32, kind="ExternalInput").ap(),
        "id_bf16": WA["id_bf16"], "ones1": WA["ones1"],
    }
    WD = {
        "inwT": nc.dram_tensor("inwT", [D, 3072], F32, kind="ExternalInput").ap(),
        "outwT": nc.dram_tensor("outwT", [D, D], F32, kind="ExternalInput").ap(),
        "wgTl1": nc.dram_tensor("wgTl1", [D, 8], F32, kind="ExternalInput").ap(),
        "w1c": nc.dram_tensor("w1c", [D, 4096], F32, kind="ExternalInput").ap(),
        "w2c": nc.dram_tensor("w2c", [4096, D], F32, kind="ExternalInput").ap(),
        "b1r": nc.dram_tensor("b1r", [128, 32], F32, kind="ExternalInput").ap(),
        "b2_rep": nc.dram_tensor("b2_rep", [128, 1024], BF16, kind="ExternalInput").ap(),
        "inb_rep": nc.dram_tensor("inb_rep", [128, 3072], BF16, kind="ExternalInput").ap(),
        "outb_rep": nc.dram_tensor("outb_rep", [128, 1024], BF16, kind="ExternalInput").ap(),
        "ln1g_rep": nc.dram_tensor("ln1g_rep", [128, 1024], F32, kind="ExternalInput").ap(),
        "ln1b_rep": nc.dram_tensor("ln1b_rep", [128, 1024], F32, kind="ExternalInput").ap(),
        "ln2g_rep": nc.dram_tensor("ln2g_rep", [128, 1024], F32, kind="ExternalInput").ap(),
        "ln2b_rep": nc.dram_tensor("ln2b_rep", [128, 1024], F32, kind="ExternalInput").ap(),
        "id_bf16": WA["id_bf16"], "id_f32": WB["id_f32"], "ones1": WA["ones1"],
        "blockmask": nc.dram_tensor("blockmask", [128, 128], BF16, kind="ExternalInput").ap(),
        "eramp_rep": nc.dram_tensor("eramp_rep", [128, 8], F32, kind="ExternalInput").ap(),
        "selmat": nc.dram_tensor("selmat", [64, 8], F32, kind="ExternalInput").ap(),
        "premask": nc.dram_tensor("premask", [64, 8], F32, kind="ExternalInput").ap(),
        "p2idx": nc.dram_tensor("p2idx", [128, 32], U32, kind="ExternalInput").ap(),
        "p6idx": nc.dram_tensor("p6idx", [128, 32], U32, kind="ExternalInput").ap(),
        "ids_base": nc.dram_tensor("ids_base", [128, 128], F32, kind="ExternalInput").ap(),
        "invsel": nc.dram_tensor("invsel", [128, 32], U32, kind="ExternalInput").ap(),
        "my_xln": nc.dram_tensor("my_xln", [TPC, D], BF16).ap(),
        "xg_all": nc.dram_tensor("xg_all", [8 * TPC, D], BF16).ap(),
        "my_y1": nc.dram_tensor("my_y1", [TPC, D], BF16).ap(),
        "y1g": nc.dram_tensor("y1g", [8 * TPC, D], BF16).ap(),
        "cnt_d": nc.dram_tensor("cnt_d", [8, 16], F32).ap(),
        "cntg": nc.dram_tensor("cntg", [64, 16], F32).ap(),
        "off_d": nc.dram_tensor("off_d", [16, 8], F32).ap(),
        "inv2_p": nc.dram_tensor("inv2_p", [32768, 4], F32).ap(),
        "inv2_all": nc.dram_tensor("inv2_all", [8 * 32768, 4], F32).ap(),
        "inv_sum_d": nc.dram_tensor("inv_sum_d", [32768, 4], F32).ap(),
        "h_d": nc.dram_tensor("h_d", [32, 128, 4096], BF16).ap(),
        "yexp_dA": nc.dram_tensor("yexp_dA", [4096, 512], BF16).ap(),
        "yexp_dB": nc.dram_tensor("yexp_dB", [4096, 512], BF16).ap(),
        "yexp_cA": nc.dram_tensor("yexp_cA", [8 * 4096, 512], BF16).ap(),
        "yexp_cB": nc.dram_tensor("yexp_cB", [8 * 4096, 512], BF16).ap(),
        "y2_d": nc.dram_tensor("y2_d", [TPC, D], BF16).ap(),
        "y2_all": nc.dram_tensor("y2_all", [8 * TPC, D], BF16).ap(),
    }
    WD["wgT"] = WD["wgTl1"]

    with ExitStack() as ctx:
        c = Ctx(nc, ctx)
        block = ctx.enter_context(nc.Block())

        @block.sync
        def _(sync):
            emit_l0ffn(c, x0, xA, WA)
            phase_barrier(c)
            emit_mamba(c, xA, xB, WB)
            phase_barrier(c)
            emit_l0ffn(c, xB, xC, WA)
            phase_barrier(c)
            emit_l1ffn(c, xC, xD, WD)
            phase_barrier(c)
            emit_l1attn(c, xD, xE, WC)
            phase_barrier(c)
            emit_l1ffn(c, xE, xo, WD)
    return nc


FUSED_IN_KEYS_A = L0FFN_W_KEYS
FUSED_IN_KEYS_B = [k for k in MAMBA_W_KEYS if k not in ("id_bf16", "ones1")]
FUSED_IN_KEYS_C = [k for k in L1ATTN_W_KEYS if k not in ("id_bf16", "ones1")]
FUSED_IN_KEYS_D = [k for k in L1FFN_W_KEYS if k not in ("id_bf16", "id_f32", "ones1", "wgT")]


def kernel(**inputs):
    x = np.ascontiguousarray(np.asarray(inputs["L0_attn_u"], np.float32))
    Wff0 = _prep_l0ffn_weights(inputs)
    Wmb = _prep_mamba_weights(inputs)
    Wat = _prep_l1attn_weights(inputs)
    Wlf = [_prep_l1ffn_weights(inputs, core) for core in range(NCORES)]
    nc = build_fused_program()
    ims = []
    for core in range(NCORES):
        im = {"x": np.ascontiguousarray(x[2 * core:2 * core + 2].reshape(TPC, D))}
        for k in FUSED_IN_KEYS_A:
            im[k] = Wff0[k]
        for k in FUSED_IN_KEYS_B:
            im[k] = Wmb[k]
        for k in FUSED_IN_KEYS_C:
            im[k] = Wat[k]
        for k in FUSED_IN_KEYS_D:
            im[k] = Wlf[core][k]
        im["wgTl1"] = Wlf[core]["wgT"]
        ims.append(im)
    res = run_bass_kernel_spmd(nc, ims, core_ids=list(range(NCORES)))
    out = np.empty((B, L, D), np.float32)
    for core in range(NCORES):
        out[2 * core:2 * core + 2] = np.asarray(res.results[core]["xo"]).reshape(2, L, D)
    return out


# version 16
# speedup vs baseline: 1.1177x; 1.0074x over previous
import numpy as np
import ml_dtypes
import concourse.bass as bass
import concourse.mybir as mybir
from concourse.bass_utils import run_bass_kernel_spmd
from contextlib import ExitStack

F32 = mybir.dt.float32
BF16 = mybir.dt.bfloat16
U32 = mybir.dt.uint32
AF = mybir.ActivationFunctionType
ALU = mybir.AluOpType
AX = mybir.AxisListType

NCORES = 8
B, L, D = 16, 2048, 1024
TPC = (B // NCORES) * L
NT = TPC // 128
EPS = 1e-5


class Count:
    def __init__(self, sem):
        self.sem = sem
        self.val = 0

    def inc(self, inst, k=1):
        inst.then_inc(self.sem, k)
        self.val += k
        return self.val


class Ctx:
    def __init__(self, nc, ctx):
        self.nc = nc
        self.ctx = ctx
        self.te = nc.tensor
        self.ve = nc.vector
        self.ae = nc.scalar
        self.ge = nc.gpsimd
        self.se = nc.sync
        self.dW = Count(ctx.enter_context(nc.semaphore("dW")))
        self.dX = Count(ctx.enter_context(nc.semaphore("dX")))
        self.dS = Count(ctx.enter_context(nc.semaphore("dS")))
        self.tE = Count(ctx.enter_context(nc.semaphore("tE")))
        self.vE = Count(ctx.enter_context(nc.semaphore("vE")))
        self.aE = Count(ctx.enter_context(nc.semaphore("aE")))
        self.gE = Count(ctx.enter_context(nc.semaphore("gE")))
        self.cc = Count(ctx.enter_context(nc.semaphore("cc")))

    def bump(self):
        if not hasattr(self, "_uniq"):
            self._uniq = 0
        self._uniq += 1
        return self._uniq

    def cnt(self, name):
        if not hasattr(self, "_cnts"):
            self._cnts = {}
        if name not in self._cnts:
            self._cnts[name] = Count(self.ctx.enter_context(self.nc.semaphore(name)))
        return self._cnts[name]

    def sbuf(self, name, shape, dtype):
        return self.ctx.enter_context(self.nc.sbuf_tensor(name, shape, dtype))

    def psum(self, name, shape, dtype=F32):
        return self.ctx.enter_context(self.nc.psum_tensor(name, shape, dtype))


def wait(eng, cnt: Count, tick=None):
    eng.wait_ge(cnt.sem, cnt.val if tick is None else tick)



def vse(c, inst):
    t = c.vE.inc(inst)
    c.ve.wait_ge(c.vE.sem, t)
    return t


def ase(c, inst):
    t = c.aE.inc(inst)
    c.ae.wait_ge(c.aE.sem, t)
    return t


def emit_ln_tile(c: Ctx, sx, sxc, sscr, sstat, sxln_bf, dn=D):
    ve, ae = c.ve, c.ae
    t = c.vE.inc(ve.reduce_sum(sstat[:, 0:1], sx[:], AX.X))
    ae.wait_ge(c.vE.sem, t)
    t2 = c.aE.inc(ae.activation(sstat[:, 1:2], sstat[:, 0:1], AF.Copy, scale=-1.0 / dn))
    ve.wait_ge(c.aE.sem, t2)
    t = c.vE.inc(ve.tensor_scalar(sxc[:], sx[:], sstat[:, 1:2], None, ALU.add))
    ae.wait_ge(c.vE.sem, t)
    t2 = c.aE.inc(ae.activation(sscr[:], sxc[:], AF.Square, accum_out=sstat[:, 2:3]))
    ve.wait_ge(c.aE.sem, t2)
    t = c.vE.inc(ve.tensor_scalar(sstat[:, 3:4], sstat[:, 2:3], 1.0 / dn, EPS, ALU.mult, ALU.add))
    ae.wait_ge(c.vE.sem, t)
    t2 = c.aE.inc(ae.activation(sstat[:, 0:1], sstat[:, 3:4], AF.Sqrt))
    ve.wait_ge(c.aE.sem, t2)
    t = c.vE.inc(ve.reciprocal(sstat[:, 1:2], sstat[:, 0:1]))
    ae.wait_ge(c.vE.sem, t)
    t2 = c.aE.inc(ae.mul(sxln_bf[:], sxc[:], sstat[:, 1:2]))
    return t2



def emit_l0ffn(c: Ctx, x_in, x_out, W, dbg_skip=()):
    nc, te, ve, ae, ge, se = c.nc, c.te, c.ve, c.ae, c.ge, c.se
    E = 16
    NQ = 4
    TQ = TPC // NQ
    NTQ = TQ // 128

    with ExitStack() as ps:
        _u = c.bump()
        sbuf = lambda n, s, d: ps.enter_context(nc.sbuf_tensor(f"{n}_{_u}", s, d))
        psum = lambda n, s, d=F32: ps.enter_context(nc.psum_tensor(f"{n}_{_u}", s, d))

        idb = sbuf("ff_idb", [128, 128], BF16)
        ones1 = sbuf("ff_ones1", [1, 128], F32)
        gw_sb = sbuf("ff_gw", [128, 8 * 16], BF16)
        ebias = sbuf("ff_ebias", [128, 16], F32)
        xlnT = sbuf("ff_xlnT", [128, 8 * TQ], BF16)
        comb_q = sbuf("ff_comb", [128, 8 * 16], F32)
        yacc = sbuf("ff_yacc", [128, NTQ * 1024], F32)
        sx = [sbuf(f"ff_sx{i}", [128, 1024], F32) for i in range(2)]
        sxc = sbuf("ff_sxc", [128, 1024], F32)
        sscr = sbuf("ff_sscr", [128, 1024], F32)
        sstat = sbuf("ff_sstat", [128, 4], F32)
        sxln = sbuf("ff_sxln", [128, 1024], BF16)
        sroute = sbuf("ff_sroute", [128, 64], F32)
        h_sb = sbuf("ff_h", [128, 8 * 512], BF16)
        sg_sb = [sbuf(f"ff_sg{i}", [128, 512], BF16) for i in range(2)]
        tmp_sb = sbuf("ff_tmp", [128, 512], BF16)
        sout = [sbuf(f"ff_sout{i}", [128, 1024], F32) for i in range(2)]
        wg_sb = [sbuf(f"ff_wg{i}", [128, 8 * 1024], BF16) for i in range(2)]
        wu_sb = [sbuf(f"ff_wu{i}", [128, 8 * 1024], BF16) for i in range(2)]
        wd_sb = [sbuf(f"ff_wd{i}", [128, 8 * 1024], BF16) for i in range(2)]

        ptr = [psum("ff_ptr0", [128, 1024], BF16)]
        ptr = ptr + ptr
        pg = psum("ff_pg", [128, 512], F32)
        phg = [psum(f"ff_phg{i}", [128, 512], F32) for i in range(2)]
        phu = [psum(f"ff_phu{i}", [128, 512], F32) for i in range(2)]
        py = [psum(f"ff_py{i}", [128, 512], F32) for i in range(2)]

        c.dW.inc(ge.dma_start(out=idb[:], in_=W["id_bf16"]), 16)
        c.dW.inc(ge.dma_start(out=ones1[:], in_=W["ones1"]), 16)
        c.dW.inc(ge.dma_start(out=gw_sb[:], in_=W["gwT"].rearrange("(k p) e -> p k e", p=128)), 16)
        c.dW.inc(ge.dma_start(out=ebias[:], in_=W["ebias"]), 16)
        const_tick = c.dW.val

        def load_expert(e, b, war_tick):
            if war_tick:
                ge.wait_ge(c.tE.sem, war_tick)
            if e < 0:
                g, u, d_ = W["sgT"], W["suT"], W["sdT"]
            else:
                g, u, d_ = W["wgT"][e], W["wuT"][e], W["wdT"][e]
            cw = c.cnt(f"ffW{b}")
            cw.inc(ge.dma_start(out=wg_sb[b][:], in_=g.rearrange("(k p) i -> p k i", p=128)), 16)
            cw.inc(ge.dma_start(out=wu_sb[b][:], in_=u.rearrange("(k p) i -> p k i", p=128)), 16)
            cw.inc(ge.dma_start(out=wd_sb[b][:], in_=d_.rearrange("(k p) i -> p k i", p=128)), 16)
            return cw.val

        expert_list = [-1] + list(range(E))
        silu_tick = {0: 0, 1: 0}
        tt1_tick = {0: 0, 1: 0}
        hwar_tick = 0
        wticks = {0: 0, 1: 0}
        consumed = {0: 0, 1: 0}
        tr_evac = {0: 0, 1: 0}
        ln_read = {0: 0, 1: 0}
        out_done = {0: 0, 1: 0}
        yev = {0: 0, 1: 0}

        te.wait_ge(c.dW.sem, const_tick)

        for q in range(NQ):
            q0 = q * TQ
            te.wait_ge(c.vE.sem, c.vE.val)
            for i in range(NTQ):
                bb = i % 2
                buf = sx[bb]
                cx = c.cnt(f"ffX{bb}")
                if ln_read[bb]:
                    se.wait_ge(c.vE.sem, ln_read[bb])
                tk = cx.inc(se.dma_start(out=buf[:], in_=x_in[q0 + i * 128: q0 + (i + 1) * 128, :]), 16)
                ve.wait_ge(cx.sem, tk)
                ve.wait_ge(c.tE.sem, c.tE.val)
                at_ln = emit_ln_tile(c, buf, sxc, sscr, sstat, sxln)
                ln_read[bb] = c.vE.val
                te.wait_ge(c.aE.sem, at_ln)
                for k in range(8):
                    if tr_evac[0]:
                        te.wait_ge(c.aE.sem, tr_evac[0])
                    tt = c.tE.inc(te.transpose(ptr[0][:, 0:128], sxln[:, k * 128:(k + 1) * 128], idb[:]))
                    ae.wait_ge(c.tE.sem, tt)
                    tr_evac[0] = c.aE.inc(
                        ae.copy(xlnT[:, k * TQ + i * 128: k * TQ + (i + 1) * 128], ptr[0][:, 0:128]))
            lvl = int(dbg_skip[-1]) if dbg_skip and str(dbg_skip[-1]).isdigit() else 99
            for i in range(NTQ if "a1b" not in dbg_skip else 0):
                te.wait_ge(c.aE.sem, c.aE.val)
                te.wait_ge(c.vE.sem, c.vE.val)
                if lvl < 1:
                    c.vE.inc(ve.memset(comb_q[:, i * 16:(i + 1) * 16], 0.1))
                    continue
                for k in range(8):
                    mm = te.matmul(pg[:, 0:16], xlnT[:, k * TQ + i * 128: k * TQ + (i + 1) * 128],
                                   gw_sb[:, k * 16:(k + 1) * 16], start=(k == 0), stop=(k == 7))
                tt = c.tE.inc(mm)
                ae.wait_ge(c.tE.sem, tt)
                at = c.aE.inc(ae.activation(sroute[:, 0:16], pg[:, 0:16], AF.Sigmoid))
                ve.wait_ge(c.aE.sem, at)
                vse(c, ve.tensor_tensor(sroute[:, 16:32], sroute[:, 0:16], ebias[:], ALU.add))
                vse(c, ve.max(sroute[:, 32:40], sroute[:, 16:32]))
                vse(c, ve.tensor_scalar(sroute[:, 40:56], sroute[:, 16:32], sroute[:, 39:40], None, ALU.is_ge))
                vse(c, ve.tensor_tensor(sroute[:, 40:56], sroute[:, 40:56], sroute[:, 0:16], ALU.mult))
                vse(c, ve.reduce_sum(sroute[:, 56:57], sroute[:, 40:56], AX.X))
                vse(c, ve.reciprocal(sroute[:, 57:58], sroute[:, 56:57]))
                c.vE.inc(ve.tensor_scalar(comb_q[:, i * 16:(i + 1) * 16], sroute[:, 40:56], sroute[:, 57:58], 2.826, ALU.mult, ALU.mult))

            for ei, e in enumerate(expert_list if "a2" not in dbg_skip else []):
                b = ei % 2
                if ei == 0:
                    wticks[0] = load_expert(expert_list[0], 0, consumed[0])
                if ei + 1 < len(expert_list):
                    nb = (ei + 1) % 2
                    wticks[nb] = load_expert(expert_list[ei + 1], nb, consumed[nb])
                te.wait_ge(c.cnt(f"ffW{b}").sem, wticks[b])
                for tb in range(2):
                    tb0 = tb * 512
                    for ic in range(8):
                        par = ic % 2
                        if silu_tick[par]:
                            te.wait_ge(c.aE.sem, silu_tick[par])
                        if tt1_tick[par]:
                            te.wait_ge(c.vE.sem, tt1_tick[par])
                        for k in range(8):
                            mmg = te.matmul(phg[par][:], wg_sb[b][:, k * 1024 + ic * 128: k * 1024 + (ic + 1) * 128],
                                            xlnT[:, k * TQ + tb0: k * TQ + tb0 + 512], start=(k == 0), stop=(k == 7))
                        tg = c.tE.inc(mmg)
                        for k in range(8):
                            mmu = te.matmul(phu[par][:], wu_sb[b][:, k * 1024 + ic * 128: k * 1024 + (ic + 1) * 128],
                                            xlnT[:, k * TQ + tb0: k * TQ + tb0 + 512], start=(k == 0), stop=(k == 7))
                        tt = c.tE.inc(mmu)
                        ae.wait_ge(c.tE.sem, tg)
                        if tt1_tick[par]:
                            ae.wait_ge(c.vE.sem, tt1_tick[par])
                        at = c.aE.inc(ae.activation(sg_sb[par][:], phg[par][:], AF.Silu))
                        silu_tick[par] = at
                        ve.wait_ge(c.aE.sem, at)
                        ve.wait_ge(c.tE.sem, tt)
                        if ic < 2 and hwar_tick:
                            ve.wait_ge(c.tE.sem, hwar_tick)
                        vt = c.vE.inc(ve.tensor_tensor(h_sb[:, ic * 512:(ic + 1) * 512], sg_sb[par][:], phu[par][:], ALU.mult))
                        tt1_tick[par] = vt
                    te.wait_ge(c.vE.sem, tt1_tick[0])
                    te.wait_ge(c.vE.sem, tt1_tick[1])
                    for tt_i in range(4):
                        pb = tt_i % 2
                        for dh in range(2):
                            if yev[pb]:
                                te.wait_ge(c.vE.sem, yev[pb])
                            for ic in range(8):
                                mmd = te.matmul(py[pb][:],
                                                h_sb[:, ic * 512 + tt_i * 128: ic * 512 + tt_i * 128 + 128],
                                                wd_sb[b][:, ic * 1024 + dh * 512: ic * 1024 + (dh + 1) * 512],
                                                start=(ic == 0), stop=(ic == 7))
                            tk = c.tE.inc(mmd)
                            ve.wait_ge(c.tE.sem, tk)
                            col = (tb * 4 + tt_i) * 1024 + dh * 512
                            if ei == 0:
                                yev[pb] = c.vE.inc(ve.tensor_copy(yacc[:, col:col + 512], py[pb][:]))
                            else:
                                cb_col = comb_q[:, (tb * 4 + tt_i) * 16 + e: (tb * 4 + tt_i) * 16 + e + 1]
                                yev[pb] = c.vE.inc(ve.scalar_tensor_tensor(
                                    yacc[:, col:col + 512], py[pb][:], cb_col, yacc[:, col:col + 512], ALU.mult, ALU.add))
                    consumed[b] = c.tE.val
            if "dump" in dbg_skip and q == 0:
                dmp = W["dbg"]
                se.wait_ge(c.vE.sem, c.vE.val)
                se.wait_ge(c.aE.sem, c.aE.val)
                c.dS.inc(ge.dma_start(out=dmp[0:128, :], in_=xlnT[:, 0:4096]), 16)
                c.dS.inc(se.dma_start(out=dmp[128:256, 0:128], in_=comb_q[:]), 16)
                c.dS.inc(se.dma_start(out=dmp[256:384, :], in_=yacc[:, 0:4096]), 16)
                c.dS.inc(ge.dma_start(out=dmp[384:512, :], in_=h_sb[:]), 16)
            se.wait_ge(c.vE.sem, c.vE.val)
            for i in range(NTQ):
                bb = i % 2
                buf, ob = sx[bb], sout[bb]
                cx = c.cnt(f"ffX{bb}")
                cs = c.cnt(f"ffS{bb}")
                if ln_read[bb]:
                    se.wait_ge(c.vE.sem, ln_read[bb])
                tk = cx.inc(se.dma_start(out=buf[:], in_=x_in[q0 + i * 128: q0 + (i + 1) * 128, :]), 16)
                if out_done[bb]:
                    ve.wait_ge(cs.sem, out_done[bb])
                ve.wait_ge(cx.sem, tk)
                vt = c.vE.inc(ve.scalar_tensor_tensor(ob[:], yacc[:, i * 1024:(i + 1) * 1024], 0.5, buf[:], ALU.mult, ALU.add))
                ln_read[bb] = vt
                se.wait_ge(c.vE.sem, vt)
                out_done[bb] = cs.inc(se.dma_start(out=x_out[q0 + i * 128: q0 + (i + 1) * 128, :], in_=ob[:]), 16)
        ge.wait_ge(c.dS.sem, c.dS.val)



def _prep_l0ffn_weights(inputs):
    f = np.float32
    W = {}
    W["gwT"] = np.ascontiguousarray(inputs["L0_ffn_gate_w"].T.astype(f))
    W["ebias"] = np.ascontiguousarray(np.broadcast_to(inputs["L0_ffn_expert_bias"].astype(f), (128, 16)))
    W["wgT"] = np.ascontiguousarray(np.transpose(np.asarray(inputs["L0_ffn_w_gate"], f), (0, 2, 1)))
    W["wuT"] = np.ascontiguousarray(np.transpose(np.asarray(inputs["L0_ffn_w_up"], f), (0, 2, 1)))
    W["wdT"] = np.ascontiguousarray(np.transpose(np.asarray(inputs["L0_ffn_w_down"], f), (0, 2, 1)))
    W["sgT"] = np.ascontiguousarray(inputs["L0_ffn_shared_gate_w"].T.astype(f))
    W["suT"] = np.ascontiguousarray(inputs["L0_ffn_shared_up_w"].T.astype(f))
    W["sdT"] = np.ascontiguousarray(inputs["L0_ffn_shared_down_w"].T.astype(f))
    W["id_bf16"] = np.eye(128, dtype=ml_dtypes.bfloat16)
    W["ones1"] = np.ones((1, 128), dtype=f)
    return W


L0FFN_W_KEYS = ["gwT", "ebias", "wgT", "wuT", "wdT", "sgT", "suT", "sdT", "id_bf16", "ones1"]


def build_l0ffn_program(dbg_skip=()):
    nc = bass.Bass("TRN2", target_bir_lowering=False)
    x_in = nc.dram_tensor("x", [TPC, D], F32, kind="ExternalInput").ap()
    x_out = nc.dram_tensor("xo", [TPC, D], F32, kind="ExternalOutput").ap()
    Wd = {
        "dbg": nc.dram_tensor("dbg", [512, 4096], F32, kind="ExternalOutput").ap(),
        "gwT": nc.dram_tensor("gwT", [D, 16], F32, kind="ExternalInput").ap(),
        "ebias": nc.dram_tensor("ebias", [128, 16], F32, kind="ExternalInput").ap(),
        "wgT": nc.dram_tensor("wgT", [16, D, D], F32, kind="ExternalInput").ap(),
        "wuT": nc.dram_tensor("wuT", [16, D, D], F32, kind="ExternalInput").ap(),
        "wdT": nc.dram_tensor("wdT", [16, D, D], F32, kind="ExternalInput").ap(),
        "sgT": nc.dram_tensor("sgT", [D, D], F32, kind="ExternalInput").ap(),
        "suT": nc.dram_tensor("suT", [D, D], F32, kind="ExternalInput").ap(),
        "sdT": nc.dram_tensor("sdT", [D, D], F32, kind="ExternalInput").ap(),
        "id_bf16": nc.dram_tensor("id_bf16", [128, 128], BF16, kind="ExternalInput").ap(),
        "ones1": nc.dram_tensor("ones1", [1, 128], F32, kind="ExternalInput").ap(),
    }
    with ExitStack() as ctx:
        c = Ctx(nc, ctx)
        block = ctx.enter_context(nc.Block())

        @block.sync
        def _(sync):
            emit_l0ffn(c, x_in, x_out, Wd, dbg_skip=dbg_skip)
    return nc


def run_l0ffn(x_full, W, nc):
    ims = []
    for core in range(NCORES):
        xs = np.ascontiguousarray(x_full[2 * core:2 * core + 2].reshape(TPC, D))
        im = {"x": xs}
        for k in L0FFN_W_KEYS:
            im[k] = W[k]
        ims.append(im)
    res = run_bass_kernel_spmd(nc, ims, core_ids=list(range(NCORES)))
    out = np.empty((B, L, D), np.float32)
    for core in range(NCORES):
        out[2 * core:2 * core + 2] = np.asarray(res.results[core]["xo"]).reshape(2, L, D)
    return out




NHEAD = 32
DSSD = 2048
CONV_DIM = 2304
CHUNK = 256


def emit_mamba(c: Ctx, x_in, x_out, W):
    nc, te, ve, ae, ge, se = c.nc, c.te, c.ve, c.ae, c.ge, c.se
    NOC = CONV_DIM // 128

    with ExitStack() as ps:
        _u = c.bump()
        sbuf = lambda n, s, d: ps.enter_context(nc.sbuf_tensor(f"{n}_{_u}", s, d))
        psum = lambda n, s, d=F32: ps.enter_context(nc.psum_tensor(f"{n}_{_u}", s, d))

        idb = sbuf("mb_idb", [128, 128], BF16)
        idf = sbuf("mb_idf", [128, 128], F32)
        ones1 = sbuf("mb_ones1", [1, 128], F32)
        ones32 = sbuf("mb_ones32", [32, 256], F32)
        convw = sbuf("mb_convw", [128, NOC * 4], F32)
        convb = sbuf("mb_convb", [128, NOC], F32)
        dtb = sbuf("mb_dtb", [128, 32], F32)
        alog = sbuf("mb_alog", [128, 32], F32)
        drep = sbuf("mb_drep", [128, 32], F32)
        negA = sbuf("mb_negA", [128, 32], F32)
        normw = sbuf("mb_normw", [128, DSSD], BF16)
        trmask = sbuf("mb_trmask", [128, 2 * 256], BF16)

        wbuf = [sbuf(f"mb_w{i}", [128, 8 * 2336], BF16) for i in range(2)]

        sx = [sbuf(f"mb_sx{i}", [128, 1024], F32) for i in range(2)]
        sxc = sbuf("mb_sxc", [128, 1024], F32)
        sstat = sbuf("mb_sstat", [128, 8], F32)
        sxln = sbuf("mb_sxln", [128, 1024], BF16)
        xlnT = sbuf("mb_xlnT", [128, 8 * 256], BF16)
        z_sb = sbuf("mb_z", [128, 2 * 2048], BF16)
        xbc = sbuf("mb_xbc", [128, NOC * 259], BF16)
        xbcs = sbuf("mb_xbcs", [128, NOC * 256], BF16)
        x_tm = sbuf("mb_xtm", [128, 2 * 2048], BF16)
        xdt = sbuf("mb_xdt", [128, 2 * 2048], BF16)
        b_tm = sbuf("mb_btm", [128, 2 * 128], BF16)
        dt_tm = sbuf("mb_dttm", [128, 2 * 32], F32)
        dts_tm = sbuf("mb_dtstm", [128, 2 * 32], F32)
        a_tm = sbuf("mb_atm", [128, 2 * 32], F32)
        acs_tm = sbuf("mb_acstm", [128, 2 * 32], F32)
        eacs_tm = sbuf("mb_eacstm", [128, 2 * 32], F32)
        w_tm = sbuf("mb_wtm", [128, 2 * 32], F32)
        sp32 = sbuf("mb_sp32", [128, 4 * 32], F32)
        afm = sbuf("mb_afm", [32, 256], F32)
        acsfm = sbuf("mb_acsfm", [32, 256], F32)
        dtfm = sbuf("mb_dtfm", [32, 256], F32)
        efm = sbuf("mb_efm", [32, 256], F32)
        wfm = sbuf("mb_wfm", [32, 256], F32)
        acs_hg = sbuf("mb_acshg", [1, 2048], F32)
        acs_last = sbuf("mb_acslast", [1, 32], F32)
        cdrep = sbuf("mb_cdrep", [128, 32], F32)
        bct = sbuf("mb_bct", [128, 2 * 256], F32)
        dD = sbuf("mb_dD", [128, 512], F32)
        e_s = sbuf("mb_es", [128, 512], BF16)
        s1_s = sbuf("mb_s1", [128, 512], BF16)
        S_sb = sbuf("mb_S", [128, 2 * 2048], BF16)
        carry = sbuf("mb_carry", [128, 2048], F32)
        carry_bf = sbuf("mb_carrybf", [128, 2048], BF16)
        y_sb = sbuf("mb_y", [128, 2 * 2048], F32)
        ytmp = sbuf("mb_ytmp", [128, 2048], F32)
        dD2 = [dD[:], ytmp[:, 0:512]]
        e_s2 = [e_s[:], sxln[:, 0:512]]
        s1_s2 = [s1_s[:], sxln[:, 512:1024]]

        acs_dram = W["acs_dram"]

        p0 = psum("mb_p0", [128, 512], F32)
        p1 = psum("mb_p1", [128, 512], F32)
        p2 = psum("mb_p2", [128, 512], F32)
        p3 = psum("mb_p3", [128, 512], F32)
        ptb = [psum(f"mb_ptb{i}", [128, 1024], BF16) for i in range(2)]
        yps = [psum("mb_yps0", [128, 512], F32), psum("mb_yps1", [128, 512], F32)]

        for dst, key in [(idb, "id_bf16"), (idf, "id_f32"), (ones1, "ones1"),
                         (dtb, "dtb_rep"), (alog, "alog_rep"), (drep, "d_rep"),
                         (normw, "normw_rep"), (trmask, "trmask"), (convb, "convb_r")]:
            c.dW.inc(ge.dma_start(out=dst[:], in_=W[key]), 16)
        c.dW.inc(ge.dma_start(out=convw[:], in_=W["convw"].rearrange("(o p) j -> p o j", p=128)), 16)
        const_tick = c.dW.val
        ve.wait_ge(c.dW.sem, const_tick)
        vse(c, ve.memset(ones32[:], 1.0))
        t = c.vE.inc(ve.tensor_copy(sp32[:, 0:32], alog[:]))
        ae.wait_ge(c.vE.sem, t)
        t2 = c.aE.inc(ae.activation(sp32[:, 32:64], sp32[:, 0:32], AF.Exp))
        ve.wait_ge(c.aE.sem, t2)
        vse(c, ve.tensor_scalar(negA[:], sp32[:, 32:64], -1.0, None, ALU.mult))
        te.wait_ge(c.dW.sem, const_tick)

        wticks = {0: 0, 1: 0}
        wconsumed = {0: 0, 1: 0}
        ln_read = {0: 0, 1: 0}
        tr_evac = {0: 0, 1: 0}
        ps_ev = {}
        out_done = 0
        acs_row_used = 0

        def load_w(part, b):
            cw = c.cnt(f"mbW{b}")
            if wconsumed[b]:
                ge.wait_ge(c.tE.sem, wconsumed[b])
            if part == 0:
                src = W["inprojT"][:, 0:2048]
                cw.inc(ge.dma_start(out=bass.AP(wbuf[b][:].tensor, 0, [[wbuf[b][:].ap[0][0], 128], [2048, 8], [1, 2048]]),
                                     in_=src.rearrange("(k p) i -> p k i", p=128)), 16)
            elif part == 1:
                src = W["inprojT"][:, 2048:4384]
                cw.inc(ge.dma_start(out=bass.AP(wbuf[b][:].tensor, 0, [[wbuf[b][:].ap[0][0], 128], [2336, 8], [1, 2336]]),
                                     in_=src.rearrange("(k p) i -> p k i", p=128)), 16)
            else:
                src = W["opT"]
                cw.inc(ge.dma_start(out=bass.AP(wbuf[b][:].tensor, 0, [[wbuf[b][:].ap[0][0], 128], [1024, 16], [1, 1024]]),
                                     in_=src.rearrange("(k p) i -> p k i", p=128)), 16)
            return cw.val

        def transpose_bf(src_ap, dst_ap, pb):
            nsrc_p, nsrc_f = src_ap.shape[0], src_ap.shape[1]
            if tr_evac[pb]:
                te.wait_ge(c.aE.sem, tr_evac[pb])
            tt = c.tE.inc(te.transpose(ptb[pb][0:nsrc_f, 0:nsrc_p], src_ap, idb[0:nsrc_p, 0:nsrc_p]))
            ae.wait_ge(c.tE.sem, tt)
            tr_evac[pb] = c.aE.inc(ae.copy(dst_ap, ptb[pb][0:nsrc_f, 0:nsrc_p]))
            return tr_evac[pb]

        ftr_evac = {2: 0, 3: 0}

        def transpose_f32(src_ap, dst_ap, which):
            pp = {2: p2, 3: p3}[which]
            nsrc_p, nsrc_f = src_ap.shape[0], src_ap.shape[1]
            if ftr_evac[which]:
                te.wait_ge(c.aE.sem, ftr_evac[which])
            te.wait_ge(c.vE.sem, c.vE.val)
            tt = c.tE.inc(te.transpose(pp[0:nsrc_f, 0:nsrc_p], src_ap, idf[0:nsrc_p, 0:nsrc_p]))
            ae.wait_ge(c.tE.sem, tt)
            ftr_evac[which] = c.aE.inc(ae.copy(dst_ap, pp[0:nsrc_f, 0:nsrc_p]))
            return ftr_evac[which]

        def mdump(row, ap):
            if "mdbg" not in W:
                return
            ge.wait_ge(c.vE.sem, c.vE.val)
            ge.wait_ge(c.aE.sem, c.aE.val)
            ge.wait_ge(c.tE.sem, c.tE.val)
            ncols = ap.shape[1] if len(ap.shape) > 1 else ap.shape[0]
            c.gE.inc(ge.dma_start(out=W["mdbg"][row * 128:(row) * 128 + ap.shape[0], 0:ncols], in_=ap), 16)
            ge.wait_ge(c.gE.sem, c.gE.val)
            ve.wait_ge(c.gE.sem, c.gE.val)
            ae.wait_ge(c.gE.sem, c.gE.val)
            te.wait_ge(c.gE.sem, c.gE.val)

        NCH = (TPC // CHUNK) if not W.get('dbg_nch') else W['dbg_nch']
        bz, bxd, bop = 0, 1, 0

        wticks[0] = load_w(0, 0)
        wticks[1] = load_w(1, 1)
        parts_loaded = [0, 1]

        for ch in range(NCH):
            t0 = ch * CHUNK
            seq_start = (ch % 8 == 0)

            for i in range(2):
                bb = i % 2
                cx = c.cnt(f"mbX{bb}")
                if ln_read[bb]:
                    se.wait_ge(c.vE.sem, ln_read[bb])
                tk = cx.inc(se.dma_start(out=sx[bb][:], in_=x_in[t0 + i * 128: t0 + (i + 1) * 128, :]), 16)
                ve.wait_ge(cx.sem, tk)
                ve.wait_ge(c.tE.sem, c.tE.val)
                if out_done:
                    ve.wait_ge(c.dS.sem, out_done)
                at_ln = emit_ln_tile(c, sx[bb], sxc, z_sb[:, 0:1024], sstat, sxln)
                ln_read[bb] = c.vE.val
                te.wait_ge(c.aE.sem, at_ln)
                for k in range(8):
                    transpose_bf(sxln[:, k * 128:(k + 1) * 128],
                                 xlnT[:, k * 256 + i * 128: k * 256 + (i + 1) * 128], k % 2)

            bz = parts_loaded.index(0)
            te.wait_ge(c.cnt(f"mbW{bz}").sem, wticks[bz])
            te.wait_ge(c.aE.sem, c.aE.val)
            for i in range(2):
                for oc in range(4):
                    pp = [p0, p1, p2, p3][oc]
                    te.wait_ge(c.aE.sem, c.aE.val)
                    for k in range(8):
                        mm = te.matmul(pp[:], xlnT[:, k * 256 + i * 128: k * 256 + (i + 1) * 128],
                                       wbuf[bz][:, k * 2048 + oc * 512: k * 2048 + oc * 512 + 512],
                                       start=(k == 0), stop=(k == 7))
                    tt = c.tE.inc(mm)
                    ae.wait_ge(c.tE.sem, tt)
                    c.aE.inc(ae.copy(z_sb[:, i * 2048 + oc * 512: i * 2048 + (oc + 1) * 512], pp[:]))
            wconsumed[bz] = c.tE.val
            nextp = (max(parts_loaded) + 1) % 3 if ch * 3 + 2 < NCH * 3 else None
            wticks[bz] = load_w(2, bz)
            parts_loaded[bz] = 2

            bxd = parts_loaded.index(1)
            te.wait_ge(c.cnt(f"mbW{bxd}").sem, wticks[bxd])
            if seq_start:
                hist = c.vE.inc(ve.memset(bass.AP(xbc[:].tensor, 0, [[xbc[:].ap[0][0], 128], [259, NOC], [1, 3]]), 0.0))
            else:
                hist = c.vE.inc(ve.tensor_copy(
                    bass.AP(xbc[:].tensor, 0, [[xbc[:].ap[0][0], 128], [259, NOC], [1, 3]]),
                    bass.AP(xbc[:].tensor, 256, [[xbc[:].ap[0][0], 128], [259, NOC], [1, 3]])))
            for oc in range(NOC):
                pp = [p0, p1][oc % 2]
                te.wait_ge(c.aE.sem, c.aE.val)
                for k in range(8):
                    mm = te.matmul(pp[:, 0:256], wbuf[bxd][:, k * 2336 + oc * 128: k * 2336 + oc * 128 + 128],
                                   xlnT[:, k * 256: (k + 1) * 256], start=(k == 0), stop=(k == 7))
                tt = c.tE.inc(mm)
                ae.wait_ge(c.tE.sem, tt)
                ae.wait_ge(c.vE.sem, hist)
                c.aE.inc(ae.copy(xbc[:, oc * 259 + 3: oc * 259 + 259], pp[:, 0:256]))
            for i in range(2):
                pp = [p2, p3][i]
                te.wait_ge(c.vE.sem, c.vE.val)
                for k in range(8):
                    mm = te.matmul(pp[:, 0:32], xlnT[:, k * 256 + i * 128: k * 256 + (i + 1) * 128],
                                   wbuf[bxd][:, k * 2336 + NOC * 128: k * 2336 + NOC * 128 + 32],
                                   start=(k == 0), stop=(k == 7))
                tt = c.tE.inc(mm)
                ve.wait_ge(c.tE.sem, tt)
                c.vE.inc(ve.tensor_tensor(dt_tm[:, i * 32:(i + 1) * 32], pp[:, 0:32], dtb[:], ALU.add))
            wconsumed[bxd] = c.tE.val
            if ch + 1 < NCH:
                wticks[bxd] = load_w(1, bxd)
            parts_loaded[bxd] = 1

            ve.wait_ge(c.aE.sem, c.aE.val)
            ve.wait_ge(c.tE.sem, c.tE.val)
            at = 0
            for oc in range(NOC):
                dst = ytmp[:, 0:256] if oc % 2 == 0 else ytmp[:, 256:512]
                src0 = xbc[:, oc * 259: oc * 259 + 256]
                ve.tensor_scalar(dst, src0, convw[:, oc * 4: oc * 4 + 1], convb[:, oc: oc + 1], ALU.mult, ALU.add)
                for j in range(1, 3):
                    srcj = xbc[:, oc * 259 + j: oc * 259 + j + 256]
                    ve.scalar_tensor_tensor(dst, srcj, convw[:, oc * 4 + j: oc * 4 + j + 1], dst, ALU.mult, ALU.add)
                cvt = c.vE.inc(ve.scalar_tensor_tensor(dst, xbc[:, oc * 259 + 3: oc * 259 + 3 + 256],
                                                       convw[:, oc * 4 + 3: oc * 4 + 4], dst, ALU.mult, ALU.add))
                ae.wait_ge(c.vE.sem, cvt)
                at = c.aE.inc(ae.activation(xbcs[:, oc * 256:(oc + 1) * 256], dst, AF.Silu))
                if oc >= 1:
                    ve.wait_ge(c.aE.sem, prev_at)
                prev_at = at

            ae.wait_ge(c.vE.sem, c.vE.val)
            t2 = c.aE.inc(ae.activation(sp32[:, 0:64], dt_tm[:], AF.Abs))
            ae.wait_ge(c.aE.sem, t2)
            t2 = c.aE.inc(ae.activation(sp32[:, 64:128], sp32[:, 0:64], AF.Exp, scale=-1.0))
            ae.wait_ge(c.aE.sem, t2)
            t2 = c.aE.inc(ae.activation(sp32[:, 0:64], sp32[:, 64:128], AF.Ln, bias=1.0))
            ve.wait_ge(c.aE.sem, t2)
            vse(c, ve.tensor_scalar(sp32[:, 64:128], dt_tm[:], 0.0, None, ALU.max))
            vse(c, ve.tensor_tensor(dts_tm[:], sp32[:, 64:128], sp32[:, 0:64], ALU.add))
            if ch == 0:
                mdump(0, dts_tm[:])
            vse(c, ve.tensor_tensor(a_tm[:], dts_tm[:],
                                    bass.AP(negA[:].tensor, 0, [[negA[:].ap[0][0], 128], [0, 2], [1, 32]]),
                                    ALU.mult))

            for i in range(2):
                transpose_f32(a_tm[:, i * 32:(i + 1) * 32], afm[:, i * 128:(i + 1) * 128], 2)
                transpose_f32(dts_tm[:, i * 32:(i + 1) * 32], dtfm[:, i * 128:(i + 1) * 128], 3)

            ve.wait_ge(c.aE.sem, c.aE.val)
            vse(c, ve.tensor_tensor_scan(acsfm[:], ones32[:], afm[:], 0.0, ALU.mult, ALU.add))

            se.wait_ge(c.vE.sem, c.vE.val)
            cw_ = c.cnt("mbAcsW")
            tk = cw_.inc(se.dma_start(out=acs_dram, in_=acsfm[:]), 16)
            se.wait_ge(cw_.sem, tk)
            if acs_row_used:
                se.wait_ge(c.tE.sem, acs_row_used)
            cl_ = c.cnt("mbAcsL")
            tk_row = cl_.inc(se.dma_start(out=acs_last[:],
                             in_=bass.AP(acs_dram.tensor, 255, [[8192, 1], [256, 32]]),
                             allow_slow_non_contiguous=True), 16)
            for i in range(2):
                transpose_f32(acsfm[:, i * 128:(i + 1) * 128], acs_tm[:, i * 32:(i + 1) * 32], 2)
            ae.wait_ge(c.aE.sem, c.aE.val)
            t2 = c.aE.inc(ae.activation(eacs_tm[:], acs_tm[:], AF.Exp))

            if ch == 0:
                mdump(1, acs_tm[:])
                mdump(2, xbcs[:, 16 * 256: 18 * 256])
            vse(c, ve.tensor_scalar(efm[:], acsfm[:], acsfm[:, 255:256], None, ALU.subtract))
            ae.wait_ge(c.vE.sem, c.vE.val)
            t2 = c.aE.inc(ae.activation(efm[:], efm[:], AF.Exp, scale=-1.0))
            ve.wait_ge(c.aE.sem, t2)
            vse(c, ve.tensor_tensor(wfm[:], efm[:], dtfm[:], ALU.mult))
            for i in range(2):
                transpose_f32(wfm[:, i * 128:(i + 1) * 128], w_tm[:, i * 32:(i + 1) * 32], 3)

            te.wait_ge(c.cnt("mbAcsL").sem, tk_row)
            te.wait_ge(c.aE.sem, c.aE.val)
            te.wait_ge(c.vE.sem, c.vE.val)
            mm = te.matmul(p0[:, 0:32], ones1[:], acs_last[:], start=True, stop=True)
            tt = c.tE.inc(mm)
            ae.wait_ge(c.tE.sem, tt)
            t2 = c.aE.inc(ae.activation(cdrep[:], p0[:, 0:32], AF.Exp))

            te.wait_ge(c.aE.sem, at)
            for i in range(2):
                for oc in range(16):
                    transpose_bf(xbcs[:, oc * 256 + i * 128: oc * 256 + (i + 1) * 128],
                                 x_tm[:, i * 2048 + oc * 128: i * 2048 + (oc + 1) * 128], oc % 2)
                transpose_bf(xbcs[:, 16 * 256 + i * 128: 16 * 256 + (i + 1) * 128],
                             b_tm[:, i * 128:(i + 1) * 128], i % 2)

            if ch == 0:
                mdump(3, x_tm[:, 0:2048])
                mdump(9, w_tm[:])
            ve.wait_ge(c.aE.sem, c.aE.val)
            for i in range(2):
                vse(c, ve.tensor_tensor(xdt[:, i * 2048:(i + 1) * 2048],
                                        x_tm[:, i * 2048:(i + 1) * 2048],
                                        w_tm[:, i * 32:(i + 1) * 32].broadcast_to((128, 32, 64)), ALU.mult))

            for jt in range(2):
                pp = [p2, p3][jt]
                te.wait_ge(c.vE.sem, c.vE.val)
                te.wait_ge(c.aE.sem, c.aE.val)
                mm = te.matmul(pp[:, 0:256], xbcs[:, 16 * 256 + jt * 128: 16 * 256 + (jt + 1) * 128],
                               xbcs[:, 17 * 256: 18 * 256], start=True, stop=True)
                tt = c.tE.inc(mm)
                ve.wait_ge(c.tE.sem, tt)
                c.vE.inc(ve.tensor_tensor(bct[:, jt * 256:(jt + 1) * 256], pp[:, 0:256],
                                          trmask[:, jt * 256:(jt + 1) * 256], ALU.mult))

            if seq_start:
                vse(c, ve.memset(carry[:], 0.0))
                vse(c, ve.memset(carry_bf[:], 0.0))
            te.wait_ge(c.vE.sem, c.vE.val)
            te.wait_ge(c.aE.sem, c.aE.val)
            yiev = {}
            for i in range(2):
                for ob in range(4):
                    pp = [p0, p1, p2, p3][ob]
                    if i == 1:
                        te.wait_ge(c.vE.sem, yiev[ob])
                    mm = te.matmul(pp[:], xbcs[:, 17 * 256 + i * 128: 17 * 256 + (i + 1) * 128],
                                   carry_bf[:, ob * 512:(ob + 1) * 512], start=True, stop=True)
                    tt = c.tE.inc(mm)
                    ve.wait_ge(c.tE.sem, tt)
                    yiev[ob] = c.vE.inc(ve.tensor_tensor(
                        y_sb[:, i * 2048 + ob * 512: i * 2048 + (ob + 1) * 512], pp[:],
                        eacs_tm[:, i * 32 + ob * 8: i * 32 + (ob + 1) * 8].broadcast_to((128, 8, 64)),
                        ALU.mult))

            if ch == 0:
                mdump(4, y_sb[:, 0:2048])
                mdump(7, bct[:])
            hg_bcast = 0
            for hg in range(4):
                h0 = hg * 8
                ch_ = c.cnt("mbAcsH")
                if hg == 0 and acs_row_used:
                    se.wait_ge(c.tE.sem, acs_row_used)
                if hg >= 1:
                    se.wait_ge(c.tE.sem, hg_bcast)
                tk_hg = ch_.inc(se.dma_start(
                    out=acs_hg[:], in_=bass.AP(acs_dram.tensor, h0 * 256, [[8192, 1], [1, 2048]])), 16)
                te.wait_ge(ch_.sem, tk_hg)
                dsub = {0: c.vE.val, 1: c.vE.val}
                dmin = {0: 0, 1: 0}
                aexp = {0: 0, 1: 0}
                for u, (jt, hp) in enumerate([(j_, h_) for j_ in range(2) for h_ in range(4)]):
                    h = h0 + hp * 2
                    par = u % 2
                    pp = [p0, p1][par]
                    te.wait_ge(c.vE.sem, dsub[par])
                    mm = te.matmul(pp[:], ones1[:], acs_hg[0:1, hp * 512:(hp + 1) * 512], start=True, stop=True)
                    tt = c.tE.inc(mm)
                    hg_bcast = tt
                    ve.wait_ge(c.tE.sem, tt)
                    if aexp[par]:
                        ve.wait_ge(c.aE.sem, aexp[par])
                    ve.tensor_tensor(dD2[par], pp[:],
                                     acs_tm[:, jt * 32 + h: jt * 32 + h + 2].broadcast_to((128, 2, 256)),
                                     ALU.subtract)
                    dsub[par] = c.vE.val + 1
                    dmin[par] = c.vE.inc(ve.tensor_scalar(dD2[par], dD2[par], 0.0, None, ALU.min))
                    ae.wait_ge(c.vE.sem, dmin[par])
                    aexp[par] = c.aE.inc(ae.activation(e_s2[par], dD2[par], AF.Exp))
                    ve.wait_ge(c.aE.sem, aexp[par])
                    ve.tensor_tensor(s1_s2[par], e_s2[par],
                                     dts_tm[:, jt * 32 + h: jt * 32 + h + 2].broadcast_to((128, 2, 256)),
                                     ALU.mult)
                    c.vE.inc(ve.tensor_tensor(
                        S_sb[:, jt * 2048 + hp * 512: jt * 2048 + (hp + 1) * 512], s1_s2[par],
                        bass.AP(bct[:].tensor, jt * 256, [[bct[:].ap[0][0], 128], [0, 2], [1, 256]]),
                        ALU.mult))
                acs_row_used = c.tE.val
                te.wait_ge(c.vE.sem, c.vE.val)
                for it in range(2):
                    for hl in range(8):
                        h = h0 + hl
                        for jt in range(2):
                            mm = te.matmul(yps[it][:, hl * 64:(hl + 1) * 64],
                                           S_sb[:, jt * 2048 + hl * 256 + it * 128: jt * 2048 + hl * 256 + it * 128 + 128],
                                           x_tm[:, jt * 2048 + h * 64: jt * 2048 + (h + 1) * 64],
                                           start=(jt == 0), stop=(jt == 1))
                    tt = c.tE.inc(mm)
                    ve.wait_ge(c.tE.sem, tt)
                    c.vE.inc(ve.tensor_tensor(y_sb[:, it * 2048 + h0 * 64: it * 2048 + h0 * 64 + 512],
                                              y_sb[:, it * 2048 + h0 * 64: it * 2048 + h0 * 64 + 512],
                                              yps[it][:], ALU.add))

            if ch == 0:
                mdump(5, y_sb[:, 0:2048])
            te.wait_ge(c.vE.sem, c.vE.val)
            for h in range(NHEAD):
                pp = [p2, p3][h % 2]
                if h >= 2:
                    te.wait_ge(c.vE.sem, st_ev[h % 2])
                for jt in range(2):
                    mm = te.matmul(pp[:, 0:64], b_tm[:, jt * 128:(jt + 1) * 128],
                                   xdt[:, jt * 2048 + h * 64: jt * 2048 + (h + 1) * 64],
                                   start=(jt == 0), stop=(jt == 1))
                tt = c.tE.inc(mm)
                ve.wait_ge(c.tE.sem, tt)
                ev = c.vE.inc(ve.scalar_tensor_tensor(carry[:, h * 64:(h + 1) * 64],
                                                      carry[:, h * 64:(h + 1) * 64],
                                                      cdrep[:, h: h + 1], pp[:, 0:64],
                                                      ALU.mult, ALU.add))
                if h == 0:
                    st_ev = {}
                st_ev[h % 2] = ev
            ve.wait_ge(c.vE.sem, c.vE.val)
            vse(c, ve.tensor_copy(carry_bf[:], carry[:]))

            if ch == 0:
                mdump(6, carry[:])
            yn = S_sb
            ve.wait_ge(c.aE.sem, c.aE.val)
            ysum = {}
            for it in range(2):
                ve.tensor_tensor(ytmp[:], x_tm[:, it * 2048:(it + 1) * 2048],
                                 drep[:].broadcast_to((128, 32, 64)), ALU.mult)
                ysum[it] = c.vE.inc(ve.tensor_tensor(y_sb[:, it * 2048:(it + 1) * 2048],
                                                     y_sb[:, it * 2048:(it + 1) * 2048], ytmp[:], ALU.add))
            ae.wait_ge(c.tE.sem, c.tE.val)
            sil = {}
            for it in range(2):
                sil[it] = c.aE.inc(ae.activation(xdt[:, it * 2048:(it + 1) * 2048],
                                                 z_sb[:, it * 2048:(it + 1) * 2048], AF.Silu))
            g4 = {}
            for it in range(2):
                ve.wait_ge(c.aE.sem, sil[it])
                g4[it] = c.vE.inc(ve.tensor_tensor(xdt[:, it * 2048:(it + 1) * 2048],
                                                   y_sb[:, it * 2048:(it + 1) * 2048],
                                                   xdt[:, it * 2048:(it + 1) * 2048], ALU.mult))
            sq = {}
            for it in range(2):
                ae.wait_ge(c.vE.sem, g4[it])
                sq[it] = c.aE.inc(ae.activation(y_sb[:, it * 2048:(it + 1) * 2048],
                                                xdt[:, it * 2048:(it + 1) * 2048],
                                                AF.Square, accum_out=sstat[:, it * 4: it * 4 + 1]))
            v6 = {}
            for it in range(2):
                ve.wait_ge(c.aE.sem, sq[it])
                v6[it] = c.vE.inc(ve.tensor_scalar(sstat[:, it * 4 + 1: it * 4 + 2],
                                                   sstat[:, it * 4: it * 4 + 1],
                                                   1.0 / DSSD, EPS, ALU.mult, ALU.add))
            s7 = {}
            for it in range(2):
                ae.wait_ge(c.vE.sem, v6[it])
                s7[it] = c.aE.inc(ae.activation(sstat[:, it * 4 + 2: it * 4 + 3],
                                                sstat[:, it * 4 + 1: it * 4 + 2], AF.Sqrt))
            r8 = {}
            for it in range(2):
                ve.wait_ge(c.aE.sem, s7[it])
                r8[it] = c.vE.inc(ve.reciprocal(sstat[:, it * 4 + 3: it * 4 + 4],
                                                sstat[:, it * 4 + 2: it * 4 + 3]))
            m9 = {}
            for it in range(2):
                ae.wait_ge(c.vE.sem, r8[it])
                m9[it] = c.aE.inc(ae.mul(y_sb[:, it * 2048:(it + 1) * 2048],
                                         xdt[:, it * 2048:(it + 1) * 2048], sstat[:, it * 4 + 3: it * 4 + 4]))
            y10 = {}
            for it in range(2):
                ve.wait_ge(c.aE.sem, m9[it])
                y10[it] = c.vE.inc(ve.tensor_tensor(yn[:, it * 2048:(it + 1) * 2048],
                                                    y_sb[:, it * 2048:(it + 1) * 2048], normw[:], ALU.mult))
            if ch == 0:
                mdump(8, yn[:, 0:2048])
            for it in range(2):
                ynT = xlnT
                te.wait_ge(c.vE.sem, y10[it])
                for k in range(16):
                    transpose_bf(yn[:, it * 2048 + k * 128: it * 2048 + (k + 1) * 128],
                                 ynT[:, k * 128:(k + 1) * 128], k % 2)
                bop = parts_loaded.index(2)
                te.wait_ge(c.cnt(f"mbW{bop}").sem, wticks[bop])
                te.wait_ge(c.aE.sem, c.aE.val)
                bb = it % 2
                cx = c.cnt(f"mbX{bb}")
                if ln_read[bb]:
                    se.wait_ge(c.vE.sem, ln_read[bb])
                tk = cx.inc(se.dma_start(out=sx[bb][:], in_=x_in[t0 + it * 128: t0 + (it + 1) * 128, :]), 16)
                for oc in range(2):
                    pp = [p0, p1][oc]
                    te.wait_ge(c.vE.sem, c.vE.val)
                    for k in range(16):
                        mm = te.matmul(pp[:], ynT[:, k * 128:(k + 1) * 128],
                                       wbuf[bop][:, k * 1024 + oc * 512: k * 1024 + oc * 512 + 512],
                                       start=(k == 0), stop=(k == 15))
                    tt = c.tE.inc(mm)
                    ve.wait_ge(c.tE.sem, tt)
                    ve.wait_ge(cx.sem, tk)
                    if out_done:
                        ve.wait_ge(c.dS.sem, out_done)
                    c.vE.inc(ve.tensor_tensor(sxc[:, oc * 512:(oc + 1) * 512], pp[:],
                                              sx[bb][:, oc * 512:(oc + 1) * 512], ALU.add))
                ln_read[bb] = c.vE.val
                se.wait_ge(c.vE.sem, c.vE.val)
                out_done = c.dS.inc(se.dma_start(out=x_out[t0 + it * 128: t0 + (it + 1) * 128, :], in_=sxc[:]), 16)
                if it == 1:
                    wconsumed[bop] = c.tE.val
                    if ch + 1 < NCH:
                        wticks[bop] = load_w(0, bop)
                        parts_loaded[bop] = 0
            acs_row_used = c.tE.val
        ge.wait_ge(c.dS.sem, c.dS.val)


def _prep_mamba_weights(inputs):
    f = np.float32
    W = {}
    W["inprojT"] = np.ascontiguousarray(np.asarray(inputs["L0_attn_in_proj_w"], f).T)
    W["opT"] = np.ascontiguousarray(np.asarray(inputs["L0_attn_out_proj_w"], f).T)
    W["convw"] = np.ascontiguousarray(np.asarray(inputs["L0_attn_conv_w"], f))
    W["convb_r"] = np.ascontiguousarray(np.asarray(inputs["L0_attn_conv_b"], f).reshape(18, 128).T)
    W["dtb_rep"] = np.ascontiguousarray(np.broadcast_to(np.asarray(inputs["L0_attn_dt_bias"], f), (128, 32)))
    W["alog_rep"] = np.ascontiguousarray(np.broadcast_to(np.asarray(inputs["L0_attn_A_log"], f), (128, 32)))
    W["d_rep"] = np.ascontiguousarray(np.broadcast_to(np.asarray(inputs["L0_attn_D"], f), (128, 32)))
    W["normw_rep"] = np.ascontiguousarray(
        np.broadcast_to(np.asarray(inputs["L0_attn_norm_w"], f), (128, 2048))).astype(ml_dtypes.bfloat16)
    i_idx = np.arange(256)
    m = np.zeros((128, 512), f)
    for jt in range(2):
        m[:, jt * 256:(jt + 1) * 256] = (i_idx[None, :] >= (jt * 128 + np.arange(128))[:, None]).astype(f)
    W["trmask"] = m.astype(ml_dtypes.bfloat16)
    W["id_bf16"] = np.eye(128, dtype=ml_dtypes.bfloat16)
    W["id_f32"] = np.eye(128, dtype=f)
    W["ones1"] = np.ones((1, 128), dtype=f)
    return W


MAMBA_W_KEYS = ["inprojT", "opT", "convw", "convb_r", "dtb_rep", "alog_rep", "d_rep",
                "normw_rep", "trmask", "id_bf16", "id_f32", "ones1"]


def build_mamba_program(dbg_nch=None):
    nc = bass.Bass("TRN2", target_bir_lowering=False)
    x_in = nc.dram_tensor("x", [TPC, D], F32, kind="ExternalInput").ap()
    x_out = nc.dram_tensor("xo", [TPC, D], F32, kind="ExternalOutput").ap()
    Wd = {
        "inprojT": nc.dram_tensor("inprojT", [D, 4384], F32, kind="ExternalInput").ap(),
        "opT": nc.dram_tensor("opT", [2048, 1024], F32, kind="ExternalInput").ap(),
        "convw": nc.dram_tensor("convw", [2304, 4], F32, kind="ExternalInput").ap(),
        "convb_r": nc.dram_tensor("convb_r", [128, 18], F32, kind="ExternalInput").ap(),
        "dtb_rep": nc.dram_tensor("dtb_rep", [128, 32], F32, kind="ExternalInput").ap(),
        "alog_rep": nc.dram_tensor("alog_rep", [128, 32], F32, kind="ExternalInput").ap(),
        "d_rep": nc.dram_tensor("d_rep", [128, 32], F32, kind="ExternalInput").ap(),
        "normw_rep": nc.dram_tensor("normw_rep", [128, 2048], BF16, kind="ExternalInput").ap(),
        "trmask": nc.dram_tensor("trmask", [128, 512], BF16, kind="ExternalInput").ap(),
        "id_bf16": nc.dram_tensor("id_bf16", [128, 128], BF16, kind="ExternalInput").ap(),
        "id_f32": nc.dram_tensor("id_f32", [128, 128], F32, kind="ExternalInput").ap(),
        "ones1": nc.dram_tensor("ones1", [1, 128], F32, kind="ExternalInput").ap(),
        "acs_dram": nc.dram_tensor("acs_dram", [32, 256], F32).ap(),
    }
    if dbg_nch:
        Wd["dbg_nch"] = dbg_nch
        Wd["mdbg"] = nc.dram_tensor("mdbg", [10 * 128, 2048], F32, kind="ExternalOutput").ap()
    with ExitStack() as ctx:
        c = Ctx(nc, ctx)
        block = ctx.enter_context(nc.Block())

        @block.sync
        def _(sync):
            emit_mamba(c, x_in, x_out, Wd)
    return nc


def run_mamba(x_full, W, nc):
    ims = []
    for core in range(NCORES):
        xs = np.ascontiguousarray(x_full[2 * core:2 * core + 2].reshape(TPC, D))
        im = {"x": xs}
        for k in MAMBA_W_KEYS:
            im[k] = W[k]
        ims.append(im)
    res = run_bass_kernel_spmd(nc, ims, core_ids=list(range(NCORES)))
    out = np.empty((B, L, D), np.float32)
    for core in range(NCORES):
        out[2 * core:2 * core + 2] = np.asarray(res.results[core]["xo"]).reshape(2, L, D)
    return out





def emit_l1attn(c: Ctx, x_in, x_out, W):
    nc, te, ve, ae, ge, se = c.nc, c.te, c.ve, c.ae, c.ge, c.se
    H, DH = 16, 64
    NT_SEQ = L // 128
    SHIFT = -8.0

    with ExitStack() as ps:
        _u = c.bump()
        sbuf = lambda n, s, d: ps.enter_context(nc.sbuf_tensor(f"{n}_{_u}", s, d))
        psum = lambda n, s, d=F32: ps.enter_context(nc.psum_tensor(f"{n}_{_u}", s, d))

        idb = sbuf("at_idb", [128, 128], BF16)
        ones1 = sbuf("at_ones1", [1, 128], F32)
        cosh = sbuf("at_cosh", [128, NT_SEQ * 32], F32)
        sinh = sbuf("at_sinh", [128, NT_SEQ * 32], F32)
        wqkv = sbuf("at_wqkv", [128, 8 * 3072], BF16)
        wop = sbuf("at_wop", [128, 8 * 1024], BF16)
        sx = [sbuf(f"at_sx{i}", [128, 1024], F32) for i in range(2)]
        sxc = sbuf("at_sxc", [128, 1024], F32)
        sstat = sbuf("at_sstat", [128, 8], F32)
        sxln = sbuf("at_sxln", [128, 1024], BF16)
        xlnT = sbuf("at_xlnT", [128, 8 * 128], BF16)
        rtmp1 = sbuf("at_rtmp1", [128, 512], F32)
        rtmp2 = sbuf("at_rtmp2", [128, 512], F32)
        q_tm = sbuf("at_qtm", [128, NT_SEQ * 1024], BF16)
        k_rope = sbuf("at_krope", [128, 1024], BF16)
        k_fm = sbuf("at_kfm", [128, 8 * L], BF16)
        v_sb = sbuf("at_vsb", [128, NT_SEQ * (H * 65 + 8)], BF16)
        qfib = sbuf("at_qfib", [128, 8 * 512], BF16)
        ofib = sbuf("at_ofib", [128, 8 * 512], BF16)
        srow2 = [sbuf(f"at_srow{i}", [128, 1024], BF16) for i in range(2)]
        o_ev = rtmp1
        rd = rtmp2

        psA = psum("at_psA", [128, 1024], F32)
        psB = psum("at_psB", [128, 1024], F32)
        pO = psum("at_pO", [128, 512], F32)
        pB_ = psum("at_pB", [128, 512], F32)
        pS2 = [psA, psB]
        pq = [psA[:, 0:512], psA[:, 512:1024]]
        pk = [psB[:, 0:512], psB[:, 512:1024]]
        pv = [pO[:], pB_[:]]
        ptb = [psum(f"at_ptb{i}", [128, 1024], BF16) for i in range(2)]

        for dst, key in [(idb, "id_bf16"), (ones1, "ones1"), (cosh, "cosh"), (sinh, "sinh")]:
            c.dW.inc(ge.dma_start(out=dst[:], in_=W[key]), 16)
        c.dW.inc(ge.dma_start(out=wqkv[:], in_=W["qkvT"].rearrange("(k p) o -> p k o", p=128)), 16)
        c.dW.inc(ge.dma_start(out=wop[:], in_=W["outT"].rearrange("(k p) o -> p k o", p=128)), 16)
        const_tick = c.dW.val
        te.wait_ge(c.dW.sem, const_tick)
        ve.wait_ge(c.dW.sem, const_tick)
        vse(c, ve.memset(sstat[:, 7:8], SHIFT))

        ln_read = {0: 0, 1: 0}
        tr_evac = {0: 0, 1: 0}
        out_done = {0: 0, 1: 0}

        def transpose_bf(src_ap, dst_ap, pb):
            nsrc_p, nsrc_f = src_ap.shape[0], src_ap.shape[1]
            if tr_evac[pb]:
                te.wait_ge(c.aE.sem, tr_evac[pb])
            tt = c.tE.inc(te.transpose(ptb[pb][0:nsrc_f, 0:nsrc_p], src_ap, idb[0:nsrc_p, 0:nsrc_p]))
            ae.wait_ge(c.tE.sem, tt)
            tr_evac[pb] = c.aE.inc(ae.copy(dst_ap, ptb[pb][0:nsrc_f, 0:nsrc_p]))
            return tr_evac[pb]

        def rope(dst_tm, src_psums, i):
            cos8 = bass.AP(cosh[:].tensor, i * 32, [[cosh[:].ap[0][0], 128], [0, 8], [1, 32]])
            sin8 = bass.AP(sinh[:].tensor, i * 32, [[sinh[:].ap[0][0], 128], [0, 8], [1, 32]])
            for half in range(2):
                pp = src_psums[half]
                q1 = bass.AP(pp.tensor, pp.offset, [[pp.ap[0][0], 128], [64, 8], [1, 32]])
                q2 = bass.AP(pp.tensor, pp.offset + 32, [[pp.ap[0][0], 128], [64, 8], [1, 32]])
                vse(c, ve.tensor_tensor(rtmp1[:, 0:256].rearrange("p (h j) -> p h j", h=8), q1, cos8, ALU.mult))
                vse(c, ve.tensor_tensor(rtmp2[:, 0:256].rearrange("p (h j) -> p h j", h=8), q2, sin8, ALU.mult))
                d1 = bass.AP(dst_tm.tensor, dst_tm.offset + half * 512, [[dst_tm.ap[0][0], 128], [64, 8], [1, 32]])
                vse(c, ve.tensor_tensor(d1, rtmp1[:, 0:256].rearrange("p (h j) -> p h j", h=8),
                                        rtmp2[:, 0:256].rearrange("p (h j) -> p h j", h=8), ALU.subtract))
                vse(c, ve.tensor_tensor(rtmp1[:, 0:256].rearrange("p (h j) -> p h j", h=8), q2, cos8, ALU.mult))
                vse(c, ve.tensor_tensor(rtmp2[:, 0:256].rearrange("p (h j) -> p h j", h=8), q1, sin8, ALU.mult))
                d2 = bass.AP(dst_tm.tensor, dst_tm.offset + half * 512 + 32, [[dst_tm.ap[0][0], 128], [64, 8], [1, 32]])
                vse(c, ve.tensor_tensor(d2, rtmp1[:, 0:256].rearrange("p (h j) -> p h j", h=8),
                                        rtmp2[:, 0:256].rearrange("p (h j) -> p h j", h=8), ALU.add))

        for s_ in range(2):
            soff = s_ * L
            vse(c, ve.memset(bass.AP(v_sb[:].tensor, 64, [[v_sb[:].ap[0][0], 128], [1048, NT_SEQ], [65, 16]]), 1.0))
            for i in range(NT_SEQ):
                bb = i % 2
                cx = c.cnt(f"atX{bb}")
                cs0 = c.cnt(f"atS{bb}")
                if ln_read[bb]:
                    se.wait_ge(c.vE.sem, ln_read[bb])
                if out_done[bb]:
                    se.wait_ge(cs0.sem, out_done[bb])
                tk = cx.inc(se.dma_start(out=sx[bb][:], in_=x_in[soff + i * 128: soff + (i + 1) * 128, :]), 16)
                ve.wait_ge(cx.sem, tk)
                ve.wait_ge(c.tE.sem, c.tE.val)
                at_ln = emit_ln_tile(c, sx[bb], sxc, k_rope, sstat, sxln)
                ln_read[bb] = c.vE.val
                te.wait_ge(c.aE.sem, at_ln)
                for k in range(8):
                    transpose_bf(sxln[:, k * 128:(k + 1) * 128], xlnT[:, k * 128:(k + 1) * 128], k % 2)
                te.wait_ge(c.aE.sem, c.aE.val)
                te.wait_ge(c.vE.sem, c.vE.val)
                for half in range(2):
                    for pp, obase in [(pq[half], half * 512), (pk[half], 1024 + half * 512), (pv[half], 2048 + half * 512)]:
                        for k in range(8):
                            mm = te.matmul(pp, xlnT[:, k * 128:(k + 1) * 128],
                                           wqkv[:, k * 3072 + obase: k * 3072 + obase + 512],
                                           start=(k == 0), stop=(k == 7))
                        c.tE.inc(mm)
                tq = c.tE.val
                ve.wait_ge(c.tE.sem, tq)
                ae.wait_ge(c.tE.sem, tq)
                ae.wait_ge(c.vE.sem, c.vE.val)
                for half in range(2):
                    c.aE.inc(ae.copy(
                        bass.AP(v_sb[:].tensor, i * 1048 + half * 8 * 65,
                                [[v_sb[:].ap[0][0], 128], [65, 8], [1, 64]]),
                        pv[half].rearrange("p (h j) -> p h j", h=8)))
                rope(q_tm[:, i * 1024:(i + 1) * 1024], pq, i)
                rope(k_rope[:], pk, i)
                ve.wait_ge(c.vE.sem, c.vE.val)
                te.wait_ge(c.vE.sem, c.vE.val)
                for k in range(8):
                    transpose_bf(k_rope[:, k * 128:(k + 1) * 128], k_fm[:, k * L + i * 128: k * L + (i + 1) * 128], k % 2)
            oev_tick = 0
            for ib in range(4):
                ib0 = ib * 512
                te.wait_ge(c.aE.sem, c.aE.val)
                for ti in range(4):
                    for k in range(8):
                        transpose_bf(q_tm[:, (ib * 4 + ti) * 1024 + k * 128: (ib * 4 + ti) * 1024 + (k + 1) * 128],
                                     qfib[:, k * 512 + ti * 128: k * 512 + (ti + 1) * 128], k % 2)
                te.wait_ge(c.aE.sem, c.aE.val)
                for h in range(H):
                    kc, ko = h // 2, (h % 2) * 64
                    NP = NT_SEQ // 2
                    exp_tick = {0: 0, 1: 0}
                    pv_tick = {0: 0, 1: 0}
                    if h == 0:
                        te.wait_ge(c.vE.sem, c.vE.val)
                        te.wait_ge(c.aE.sem, c.aE.val)
                    def emit_pv(pt):
                        par = pt % 2
                        te.wait_ge(c.aE.sem, exp_tick[par])
                        for half in range(2):
                            mt = pt * 2 + half
                            mm2 = te.matmul(pO[0:65, 0:512], v_sb[:, mt * 1048 + h * 65: mt * 1048 + (h + 1) * 65],
                                            srow2[par][:, half * 512:(half + 1) * 512],
                                            start=(mt == 0), stop=(mt == NT_SEQ - 1))
                        pv_tick[par] = c.tE.inc(mm2)
                    for pt in range(NP):
                        par = pt % 2
                        pp = pS2[par]
                        if exp_tick[par]:
                            te.wait_ge(c.aE.sem, exp_tick[par])
                        if pt == 0:
                            te.wait_ge(c.aE.sem, oev_tick if (oev_tick and h > 0) else 0)
                        for half in range(2):
                            mt = pt * 2 + half
                            mm = te.matmul(pp[:, half * 512:(half + 1) * 512],
                                           k_fm[ko:ko + 64, kc * L + mt * 128: kc * L + (mt + 1) * 128],
                                           qfib[ko:ko + 64, kc * 512:(kc + 1) * 512], start=True, stop=True)
                        ts_ = c.tE.inc(mm)
                        ae.wait_ge(c.tE.sem, ts_)
                        if pv_tick[par]:
                            ae.wait_ge(c.tE.sem, pv_tick[par])
                        at2 = c.aE.inc(ae.activation(srow2[par][:], pp[:], AF.Exp, bias=sstat[:, 7:8], scale=0.125))
                        exp_tick[par] = at2
                        if pt >= 1:
                            emit_pv(pt - 1)
                    emit_pv(NP - 1)
                    pv_used = pv_tick[(NP - 1) % 2]
                    ve.wait_ge(c.tE.sem, pv_used)
                    vse(c, ve.reciprocal(rd[0:1, 0:512], pO[64:65, 0:512]))
                    ae.wait_ge(c.tE.sem, pv_used)
                    at3 = c.aE.inc(ae.copy(o_ev[0:64, :], pO[0:64, 0:512]))
                    oev_tick = at3
                    te.wait_ge(c.vE.sem, c.vE.val)
                    mm3 = te.matmul(pB_[0:64, 0:512], ones1[0:1, 0:64], rd[0:1, 0:512], start=True, stop=True)
                    tb3 = c.tE.inc(mm3)
                    ve.wait_ge(c.tE.sem, tb3)
                    ve.wait_ge(c.aE.sem, at3)
                    c.vE.inc(ve.tensor_tensor(ofib[ko:ko + 64, kc * 512:(kc + 1) * 512], o_ev[0:64, :], pB_[0:64, 0:512], ALU.mult))
                te.wait_ge(c.vE.sem, c.vE.val)
                for ti in range(4):
                    gt = ib * 4 + ti
                    bb = gt % 2
                    cx = c.cnt(f"atX{bb}")
                    cs = c.cnt(f"atS{bb}")
                    if ln_read[bb]:
                        se.wait_ge(c.vE.sem, ln_read[bb])
                    if out_done[bb]:
                        se.wait_ge(cs.sem, out_done[bb])
                    tk = cx.inc(se.dma_start(out=sx[bb][:], in_=x_in[soff + gt * 128: soff + (gt + 1) * 128, :]), 16)
                    for oc in range(2):
                        pp = [pO, pB_][oc]
                        te.wait_ge(c.vE.sem, c.vE.val)
                        for k in range(8):
                            mm = te.matmul(pp[:], ofib[:, k * 512 + ti * 128: k * 512 + (ti + 1) * 128],
                                           wop[:, k * 1024 + oc * 512: k * 1024 + (oc + 1) * 512],
                                           start=(k == 0), stop=(k == 7))
                        tt = c.tE.inc(mm)
                        ve.wait_ge(c.tE.sem, tt)
                        ve.wait_ge(cx.sem, tk)
                        if out_done[bb]:
                            ve.wait_ge(cs.sem, out_done[bb])
                        c.vE.inc(ve.tensor_tensor(sx[bb][:, oc * 512:(oc + 1) * 512], pp[:],
                                                  sx[bb][:, oc * 512:(oc + 1) * 512], ALU.add))
                    ln_read[bb] = c.vE.val
                    se.wait_ge(c.vE.sem, c.vE.val)
                    out_done[bb] = cs.inc(se.dma_start(out=x_out[soff + gt * 128: soff + (gt + 1) * 128, :], in_=sx[bb][:]), 16)
        ge.wait_ge(c.cnt("atS0").sem, c.cnt("atS0").val)
        ge.wait_ge(c.cnt("atS1").sem, c.cnt("atS1").val)


def _prep_l1attn_weights(inputs):
    f = np.float32
    W = {}
    W["qkvT"] = np.ascontiguousarray(np.asarray(inputs["L1_attn_w_qkv"], f).T)
    W["outT"] = np.ascontiguousarray(np.asarray(inputs["L1_attn_w_out"], f).T)
    half = 32
    inv_freq = 1.0 / (10000.0 ** (np.arange(0, half, dtype=np.float64) / half))
    ang = np.arange(L, dtype=np.float64)[:, None] * inv_freq[None, :]
    cos = np.cos(ang).astype(f)
    sin = np.sin(ang).astype(f)
    W["cosh"] = np.ascontiguousarray(cos.reshape(16, 128, 32).transpose(1, 0, 2).reshape(128, 512))
    W["sinh"] = np.ascontiguousarray(sin.reshape(16, 128, 32).transpose(1, 0, 2).reshape(128, 512))
    W["id_bf16"] = np.eye(128, dtype=ml_dtypes.bfloat16)
    W["ones1"] = np.ones((1, 128), dtype=f)
    return W


L1ATTN_W_KEYS = ["qkvT", "outT", "cosh", "sinh", "id_bf16", "ones1"]


def build_l1attn_program():
    nc = bass.Bass("TRN2", target_bir_lowering=False)
    x_in = nc.dram_tensor("x", [TPC, D], F32, kind="ExternalInput").ap()
    x_out = nc.dram_tensor("xo", [TPC, D], F32, kind="ExternalOutput").ap()
    Wd = {
        "qkvT": nc.dram_tensor("qkvT", [D, 3072], F32, kind="ExternalInput").ap(),
        "outT": nc.dram_tensor("outT", [D, D], F32, kind="ExternalInput").ap(),
        "cosh": nc.dram_tensor("cosh", [128, 512], F32, kind="ExternalInput").ap(),
        "sinh": nc.dram_tensor("sinh", [128, 512], F32, kind="ExternalInput").ap(),
        "id_bf16": nc.dram_tensor("id_bf16", [128, 128], BF16, kind="ExternalInput").ap(),
        "ones1": nc.dram_tensor("ones1", [1, 128], F32, kind="ExternalInput").ap(),
    }
    with ExitStack() as ctx:
        c = Ctx(nc, ctx)
        block = ctx.enter_context(nc.Block())

        @block.sync
        def _(sync):
            emit_l1attn(c, x_in, x_out, Wd)
    return nc


def run_l1attn(x_full, W, nc):
    ims = []
    for core in range(NCORES):
        xs = np.ascontiguousarray(x_full[2 * core:2 * core + 2].reshape(TPC, D))
        im = {"x": xs}
        for k in L1ATTN_W_KEYS:
            im[k] = W[k]
        ims.append(im)
    res = run_bass_kernel_spmd(nc, ims, core_ids=list(range(NCORES)))
    out = np.empty((B, L, D), np.float32)
    for core in range(NCORES):
        out[2 * core:2 * core + 2] = np.asarray(res.results[core]["xo"]).reshape(2, L, D)
    return out





def emit_ln_gb(c: Ctx, src, sxc, sscr, sstat, g_rep, b_rep, out_bf, dn=D):
    ve, ae = c.ve, c.ae
    t = c.vE.inc(ve.reduce_sum(sstat[:, 0:1], src[:], AX.X))
    ae.wait_ge(c.vE.sem, t)
    t2 = c.aE.inc(ae.activation(sstat[:, 1:2], sstat[:, 0:1], AF.Copy, scale=-1.0 / dn))
    ve.wait_ge(c.aE.sem, t2)
    t = c.vE.inc(ve.tensor_scalar(sxc[:], src[:], sstat[:, 1:2], None, ALU.add))
    ae.wait_ge(c.vE.sem, t)
    t2 = c.aE.inc(ae.activation(sscr[:], sxc[:], AF.Square, accum_out=sstat[:, 2:3]))
    ve.wait_ge(c.aE.sem, t2)
    t = c.vE.inc(ve.tensor_scalar(sstat[:, 3:4], sstat[:, 2:3], 1.0 / dn, EPS, ALU.mult, ALU.add))
    ae.wait_ge(c.vE.sem, t)
    t2 = c.aE.inc(ae.activation(sstat[:, 0:1], sstat[:, 3:4], AF.Sqrt))
    ve.wait_ge(c.aE.sem, t2)
    t = c.vE.inc(ve.reciprocal(sstat[:, 1:2], sstat[:, 0:1]))
    ae.wait_ge(c.vE.sem, t)
    t2 = c.aE.inc(ae.mul(sscr[:], sxc[:], sstat[:, 1:2]))
    ve.wait_ge(c.aE.sem, t2)
    vse(c, ve.tensor_tensor(sxc[:], sscr[:], g_rep[:], ALU.mult))
    t = c.vE.inc(ve.tensor_tensor(out_bf, sxc[:], b_rep[:], ALU.add))
    return t


def emit_l1ffn(c: Ctx, x_in, x_out, W):
    nc, te, ve, ae, ge, se = c.nc, c.te, c.ve, c.ae, c.ge, c.se
    HI = 4
    RG = [list(range(NCORES))]

    with ExitStack() as ps:
        _u = c.bump()
        sbuf = lambda n, s, d: ps.enter_context(nc.sbuf_tensor(f"{n}_{_u}", s, d))
        psum = lambda n, s, d=F32: ps.enter_context(nc.psum_tensor(f"{n}_{_u}", s, d))

        idb = sbuf("lf_idb", [128, 128], BF16)
        idf = sbuf("lf_idf", [128, 128], F32)
        ones1 = sbuf("lf_ones1", [1, 128], F32)
        onescol_bf = sbuf("lf_onescol", [128, 1], BF16)
        ones81 = sbuf("lf_ones81", [8, 1], F32)
        blockmask = sbuf("lf_bmask", [128, 128], BF16)
        eramp = sbuf("lf_eramp", [128, 8], F32)
        selmat = sbuf("lf_selmat", [64, 8], F32)
        premask = sbuf("lf_premask", [64, 8], F32)
        p2idx = sbuf("lf_p2idx", [128, 32], U32)
        invsel = sbuf("lf_invsel", [128, 32], U32)
        p6idx = sbuf("lf_p6idx", [128, 32], U32)
        idsb = sbuf("lf_idsb", [128, 128], F32)
        inb_rep = sbuf("lf_inb", [128, 3072], BF16)
        outb_rep = sbuf("lf_outb", [128, 1024], BF16)
        ln1g = sbuf("lf_ln1g", [128, 1024], F32)
        ln1b = sbuf("lf_ln1b", [128, 1024], F32)
        ln2g = sbuf("lf_ln2g", [128, 1024], F32)
        ln2b = sbuf("lf_ln2b", [128, 1024], F32)
        b1r = sbuf("lf_b1r", [128, 32], F32)
        b2rep = sbuf("lf_b2rep", [128, 1024], BF16)
        wg_sb = sbuf("lf_wg", [128, 8 * 8], BF16)
        wA = sbuf("lf_wA", [128, 8 * 3072], BF16)
        wB = sbuf("lf_wB", [128, 8 * 1024], BF16)
        sx = [sbuf(f"lf_sx{i}", [128, 1024], F32) for i in range(2)]
        sxc = sbuf("lf_sxc", [128, 1024], F32)
        sscr = sbuf("lf_sscr", [128, 1024], F32)
        sstat = sbuf("lf_sstat", [128, 16], F32)
        sbf = sbuf("lf_sbf", [128, 1024], BF16)
        xgt = sbuf("lf_xgt", [128, 1024], BF16)
        xgT = sbuf("lf_xgT", [128, 8 * 128], BF16)
        qkv_sb = sbuf("lf_qkv", [128, 3072], BF16)
        qfm = sbuf("lf_qfm", [128, 1024], BF16)
        kfm = sbuf("lf_kfm", [128, 1024], BF16)
        es = sbuf("lf_es", [128, 512], BF16)
        esm = sbuf("lf_esm", [128, 512], BF16)
        rd1 = sbuf("lf_rd1", [1, 512], F32)
        aout = sbuf("lf_aout", [128, 8 * 128], BF16)
        y1pre = sbuf("lf_y1pre", [128, 1024], F32)
        oev = y1pre
        g_sb = sbuf("lf_gsb", [128, 8], F32)
        eg = sbuf("lf_eg", [128, 8], F32)
        route = sbuf("lf_route", [128, 3 * 32], F32)
        slu = sbuf("lf_slu", [128, 32], U32)
        sl2u = sbuf("lf_sl2u", [128, 32], U32)
        rscr = sbuf("lf_rscr", [128, 16], F32)
        ohT = sbuf("lf_ohT", [8, TPC], F32)
        pfxT = sbuf("lf_pfxT", [8, TPC], F32)
        allcnt = sbuf("lf_allcnt", [64, 16], F32)
        offlm = sbuf("lf_offlm", [16, 8], F32)
        off_tile = sbuf("lf_offtile", [128, 8], F32)
        ptile = sbuf("lf_ptile", [128, 8], F32)
        invu = sbuf("lf_invu", [128, 256], U32)
        dispfm = sbuf("lf_dispfm", [128, 8 * 2048], BF16)
        hstg = [sbuf(f"lf_hstg{i}", [128, 512], BF16) for i in range(2)]
        gbuf = sbuf("lf_gbuf", [128, 1024], BF16)
        y1t = sbuf("lf_y1t", [128, 1024], BF16)
        zero_sb = sbuf("lf_zero", [128, 256], F32)

        p0 = psum("lf_p0", [128, 512], F32)
        p1 = psum("lf_p1", [128, 512], F32)
        p2 = psum("lf_p2", [128, 512], F32)
        p3 = psum("lf_p3", [128, 512], F32)
        p4 = psum("lf_p4", [128, 512], F32)
        p5 = psum("lf_p5", [128, 512], F32)
        ptb = [psum(f"lf_ptb{i}", [128, 1024], BF16) for i in range(2)]

        DR = W

        for dst, key in [(idb, "id_bf16"), (idf, "id_f32"), (ones1, "ones1"),
                         (blockmask, "blockmask"), (eramp, "eramp_rep"),
                         (selmat, "selmat"), (premask, "premask"),
                         (p2idx, "p2idx"), (p6idx, "p6idx"), (idsb, "ids_base"),
                         (invsel, "invsel"),
                         (inb_rep, "inb_rep"), (outb_rep, "outb_rep"),
                         (ln1g, "ln1g_rep"), (ln1b, "ln1b_rep"),
                         (ln2g, "ln2g_rep"), (ln2b, "ln2b_rep"),
                         (b1r, "b1r"), (b2rep, "b2_rep")]:
            c.dW.inc(ge.dma_start(out=dst[:], in_=DR[key]), 16)
        c.dW.inc(ge.dma_start(out=wg_sb[:], in_=DR["wgT"].rearrange("(k p) e -> p k e", p=128)), 16)
        c.dW.inc(ge.dma_start(out=wA[:], in_=DR["inwT"].rearrange("(k p) o -> p k o", p=128)), 16)
        c.dW.inc(ge.dma_start(out=wB[:], in_=DR["outwT"].rearrange("(k p) o -> p k o", p=128)), 16)
        const_tick = c.dW.val
        te.wait_ge(c.dW.sem, const_tick)
        ve.wait_ge(c.dW.sem, const_tick)
        ae.wait_ge(c.dW.sem, const_tick)
        vse(c, ve.memset(onescol_bf[:], 1.0))
        vse(c, ve.memset(ones81[:], 1.0))
        vse(c, ve.memset(zero_sb[:], 0.0))
        vse(c, ve.memset(sstat[:, 7:8], -8.0))

        ln_read = {0: 0, 1: 0}
        tr_evac = {0: 0, 1: 0}
        if not hasattr(c, "r_bnd"):
            c.r_bnd = ge.alloc_register("lf_bnd")
            ge.reg_mov(c.r_bnd, 32767)
        r_bnd = c.r_bnd

        def transpose_bf(src_ap, dst_ap, pb):
            nsrc_p, nsrc_f = src_ap.shape[0], src_ap.shape[1]
            if tr_evac[pb]:
                te.wait_ge(c.aE.sem, tr_evac[pb])
            tt = c.tE.inc(te.transpose(ptb[pb][0:nsrc_f, 0:nsrc_p], src_ap, idb[0:nsrc_p, 0:nsrc_p]))
            ae.wait_ge(c.tE.sem, tt)
            tr_evac[pb] = c.aE.inc(ae.copy(dst_ap, ptb[pb][0:nsrc_f, 0:nsrc_p]))
            return tr_evac[pb]

        ftr_ev = {4: 0, 5: 0}

        def transpose_f32(src_ap, dst_ap, which):
            pp = {4: p4, 5: p5}[which]
            nsrc_p, nsrc_f = src_ap.shape[0], src_ap.shape[1]
            te.wait_ge(c.aE.sem, c.aE.val)
            te.wait_ge(c.vE.sem, c.vE.val)
            tt = c.tE.inc(te.transpose(pp[0:nsrc_f, 0:nsrc_p], src_ap, idf[0:nsrc_p, 0:nsrc_p]))
            ae.wait_ge(c.tE.sem, tt)
            ftr_ev[which] = c.aE.inc(ae.copy(dst_ap, pp[0:nsrc_f, 0:nsrc_p]))
            return ftr_ev[which]

        cs1 = c.cnt("lfP1S")
        sbf_done = {}
        for j in range(NT):
            bb = j % 2
            cx = c.cnt(f"lfX{bb}")
            if ln_read[bb]:
                se.wait_ge(c.vE.sem, ln_read[bb])
            tk = cx.inc(se.dma_start(out=sx[bb][:], in_=x_in[j * 128:(j + 1) * 128, :]), 16)
            ve.wait_ge(cx.sem, tk)
            if j >= 1:
                ve.wait_ge(cs1.sem, sbf_done[j - 1])
            at_ln = emit_ln_tile(c, sx[bb], sxc, sscr, sstat, sbf)
            ln_read[bb] = c.vE.val
            se.wait_ge(c.aE.sem, at_ln)
            sbf_done[j] = cs1.inc(se.dma_start(out=DR["my_xln"][j * 128:(j + 1) * 128, :], in_=sbf[:]), 16)
            if j in (7, 15, 23):
                q = j // 8
                ge.wait_ge(cs1.sem, sbf_done[j])
                if q == 0:
                    ccxga = c.cnt("ccXgA")
                ccxa_t = ccxga.inc(ge.collective_compute(
                    "AllGather", ALU.bypass, replica_groups=RG,
                    ins=[DR["my_xln"][q * 1024:(q + 1) * 1024, :]],
                    outs=[DR["xg_all"][q * 8192:(q + 1) * 8192, :]]))
        ge.wait_ge(cs1.sem, cs1.val)
        ccxa_t = ccxga.inc(ge.collective_compute(
            "AllGather", ALU.bypass, replica_groups=RG,
            ins=[DR["my_xln"][3072:4096, :]], outs=[DR["xg_all"][24576:32768, :]]))
        ge.wait_ge(ccxga.sem, ccxa_t)

        cy1 = c.cnt("lfY1S")
        cg2 = c.cnt("lfP2G")
        y1_done = {}
        for j in range(NT):
            ge.wait_ge(c.tE.sem, c.tE.val)
            ge.wait_ge(c.vE.sem, c.vE.val)
            gk = cg2.inc(ge.indirect_dma_start(
                xgt[:], None, DR["xg_all"][:], bass.IndirectOffsetOnAxis(p2idx[:, j:j + 1], 0),
                bounds_check=r_bnd, oob_is_err=False), 16)
            te.wait_ge(cg2.sem, gk)
            if "dbg5" in DR and j == 0:
                cdbg0 = c.cnt("lfDbg0")
                cdbg0.inc(ge.dma_start(out=DR["dbg5"][0:128, 0:1024], in_=xgt[:]), 16)
                ge.wait_ge(cdbg0.sem, cdbg0.val)
            te.wait_ge(c.aE.sem, c.aE.val)
            for k in range(8):
                transpose_bf(xgt[:, k * 128:(k + 1) * 128], xgT[:, k * 128:(k + 1) * 128], k % 2)
            te.wait_ge(c.aE.sem, c.aE.val)
            te.wait_ge(c.vE.sem, c.vE.val)
            for ob in range(6):
                pp = [p0, p1, p2, p3, p4, p5][ob]
                for k in range(8):
                    mm = te.matmul(pp[:], xgT[:, k * 128:(k + 1) * 128],
                                   wA[:, k * 3072 + ob * 512: k * 3072 + (ob + 1) * 512],
                                   start=(k == 0), stop=(k == 7))
                tq = c.tE.inc(mm)
                ve.wait_ge(c.tE.sem, tq)
                c.vE.inc(ve.tensor_tensor(qkv_sb[:, ob * 512:(ob + 1) * 512], pp[:],
                                          inb_rep[:, ob * 512:(ob + 1) * 512], ALU.add))
            if "dbg5" in DR and j == 0:
                ve.wait_ge(c.vE.sem, c.vE.val)
                cdbg0 = c.cnt("lfDbg0")
                ge.wait_ge(c.vE.sem, c.vE.val)
                cdbg0.inc(ge.dma_start(out=DR["dbg5"][128:256, 0:2048], in_=qkv_sb[:, 0:2048]), 16)
                ge.wait_ge(cdbg0.sem, cdbg0.val)
            te.wait_ge(c.vE.sem, c.vE.val)
            for h in range(HI):
                for cc_ in range(2):
                    transpose_bf(qkv_sb[:, h * 256 + cc_ * 128: h * 256 + (cc_ + 1) * 128],
                                 qfm[:, h * 256 + cc_ * 128: h * 256 + (cc_ + 1) * 128], 0)
                    transpose_bf(qkv_sb[:, 1024 + h * 256 + cc_ * 128: 1024 + h * 256 + (cc_ + 1) * 128],
                                 kfm[:, h * 256 + cc_ * 128: h * 256 + (cc_ + 1) * 128], 1)
            te.wait_ge(c.aE.sem, c.aE.val)
            te.wait_ge(c.vE.sem, c.vE.val)
            for h in range(HI):
                for cc_ in range(2):
                    mm = te.matmul(p0[:, h * 128:(h + 1) * 128],
                                   kfm[:, h * 256 + cc_ * 128: h * 256 + (cc_ + 1) * 128],
                                   qfm[:, h * 256 + cc_ * 128: h * 256 + (cc_ + 1) * 128],
                                   start=(cc_ == 0), stop=(cc_ == 1))
            ts_ = c.tE.inc(mm)
            ae.wait_ge(c.tE.sem, ts_)
            ae.wait_ge(c.vE.sem, c.vE.val)
            at2 = c.aE.inc(ae.activation(es[:], p0[:], AF.Exp, bias=sstat[:, 7:8], scale=1.0 / 16.0))
            ve.wait_ge(c.aE.sem, at2)
            vse(c, ve.tensor_tensor(
                esm[:].rearrange("p (h t) -> p h t", h=4),
                es[:].rearrange("p (h t) -> p h t", h=4),
                bass.AP(blockmask[:].tensor, 0, [[blockmask[:].ap[0][0], 128], [0, 4], [1, 128]]),
                ALU.mult))
            te.wait_ge(c.vE.sem, c.vE.val)
            for h in range(HI):
                pp = [p2, p4][h // 2]
                for hd in range(2):
                    mm = te.matmul(pp[:, (h % 2) * 256 + hd * 128: (h % 2) * 256 + (hd + 1) * 128],
                                   qkv_sb[:, 2048 + h * 256 + hd * 128: 2048 + h * 256 + (hd + 1) * 128],
                                   esm[:, h * 128:(h + 1) * 128], start=True, stop=True)
                mmd = te.matmul(p3[0:1, h * 128:(h + 1) * 128], onescol_bf[:],
                                esm[:, h * 128:(h + 1) * 128], start=True, stop=True)
            td = c.tE.inc(mmd)
            ve.wait_ge(c.tE.sem, td)
            vse(c, ve.reciprocal(rd1[0:1, 0:512], p3[0:1, 0:512]))
            te.wait_ge(c.vE.sem, c.vE.val)
            for b_ in range(4):
                mm = te.matmul(p3[:, b_ * 128:(b_ + 1) * 128], ones1[0:1, 0:128],
                               rd1[0:1, (b_ // 2) * 128:(b_ // 2 + 1) * 128], start=True, stop=True)
                mm = te.matmul(p5[:, b_ * 128:(b_ + 1) * 128], ones1[0:1, 0:128],
                               rd1[0:1, 256 + (b_ // 2) * 128: 256 + (b_ // 2 + 1) * 128], start=True, stop=True)
            tb3 = c.tE.inc(mm)
            ae.wait_ge(c.tE.sem, td)
            ae.wait_ge(c.vE.sem, c.vE.val)
            at3 = c.aE.inc(ae.copy(oev[:, 0:512], p2[:]))
            at3 = c.aE.inc(ae.copy(oev[:, 512:1024], p4[:]))
            ve.wait_ge(c.aE.sem, at3)
            ve.wait_ge(c.tE.sem, tb3)
            c.vE.inc(ve.tensor_tensor(aout[:, 0:512], oev[:, 0:512], p3[:], ALU.mult))
            c.vE.inc(ve.tensor_tensor(aout[:, 512:1024], oev[:, 512:1024], p5[:], ALU.mult))
            te.wait_ge(c.vE.sem, c.vE.val)
            for oc in range(2):
                pp = [p0, p1][oc]
                for k in range(8):
                    mm = te.matmul(pp[:], aout[:, k * 128:(k + 1) * 128],
                                   wB[:, k * 1024 + oc * 512: k * 1024 + (oc + 1) * 512],
                                   start=(k == 0), stop=(k == 7))
                to = c.tE.inc(mm)
                ve.wait_ge(c.tE.sem, to)
                vse(c, ve.tensor_tensor(y1pre[:, oc * 512:(oc + 1) * 512], pp[:],
                                        outb_rep[:, oc * 512:(oc + 1) * 512], ALU.add))
            vse(c, ve.tensor_tensor(y1pre[:], y1pre[:], xgt[:], ALU.add))
            if j >= 1:
                ve.wait_ge(cy1.sem, y1_done[j - 1])
            lt = emit_ln_gb(c, y1pre, sxc, sscr, sstat, ln1g, ln1b, sbf[:])
            se.wait_ge(c.vE.sem, lt)
            y1_done[j] = cy1.inc(se.dma_start(out=DR["my_y1"][j * 128:(j + 1) * 128, :], in_=sbf[:]), 16)
            if j in (7, 15, 23):
                q = j // 8
                ge.wait_ge(cy1.sem, y1_done[j])
                if q == 0:
                    ccy1a = c.cnt("ccY1a")
                ccy1a.inc(ge.collective_compute(
                    "AllGather", ALU.bypass, replica_groups=RG,
                    ins=[DR["my_y1"][q * 1024:(q + 1) * 1024, :]],
                    outs=[DR["y1g"][q * 8192:(q + 1) * 8192, :]]))
            te.wait_ge(c.vE.sem, lt)
            for k in range(8):
                transpose_bf(sbf[:, k * 128:(k + 1) * 128], xgT[:, k * 128:(k + 1) * 128], k % 2)
            te.wait_ge(c.aE.sem, c.aE.val)
            for k in range(8):
                mm = te.matmul(p0[:, 0:8], xgT[:, k * 128:(k + 1) * 128],
                               wg_sb[:, k * 8:(k + 1) * 8], start=(k == 0), stop=(k == 7))
            tg = c.tE.inc(mm)
            ae.wait_ge(c.tE.sem, tg)
            at4 = c.aE.inc(ae.copy(g_sb[:], p0[:, 0:8]))
            ve.wait_ge(c.aE.sem, at4)
            vse(c, ve.reduce_max(rscr[:, 0:1], g_sb[:], AX.X))
            vse(c, ve.tensor_scalar(rscr[:, 1:2], rscr[:, 0:1], -1.0, None, ALU.mult))
            ae.wait_ge(c.vE.sem, c.vE.val)
            at5 = c.aE.inc(ae.activation(eg[:], g_sb[:], AF.Exp, bias=rscr[:, 1:2], accum_out=rscr[:, 2:3]))
            ve.wait_ge(c.aE.sem, at5)
            vse(c, ve.reciprocal(route[:, 32 + j: 32 + j + 1], rscr[:, 2:3]))
            vse(c, ve.tensor_scalar(rscr[:, 3:11], g_sb[:], rscr[:, 0:1], None, ALU.is_ge))
            vse(c, ve.tensor_scalar(sstat[:, 8:16], rscr[:, 3:11], -99.0, 99.0, ALU.mult, ALU.add))
            vse(c, ve.tensor_tensor(rscr[:, 3:11], rscr[:, 3:11], eramp[:], ALU.mult))
            vse(c, ve.tensor_tensor(rscr[:, 3:11], rscr[:, 3:11], sstat[:, 8:16], ALU.add))
            vse(c, ve.tensor_reduce(route[:, j:j + 1], rscr[:, 3:11], AX.X, ALU.min))
            vse(c, ve.tensor_scalar(rscr[:, 3:11], eramp[:], route[:, j:j + 1], None, ALU.is_equal))
            transpose_f32(rscr[:, 3:11], ohT[:, j * 128:(j + 1) * 128], 4)

        ve.wait_ge(c.aE.sem, c.aE.val)
        for l in range(16):
            vse(c, ve.tensor_tensor_scan(
                bass.AP(pfxT[:].tensor, l, [[pfxT[:].ap[0][0], 8], [16, 256]]),
                bass.AP(ones81[:].tensor, 0, [[ones81[:].ap[0][0], 8], [0, 256]]),
                bass.AP(ohT[:].tensor, l, [[ohT[:].ap[0][0], 8], [16, 256]]),
                0.0, ALU.mult, ALU.add))
        cw2 = c.cnt("lfCnt")
        se.wait_ge(c.vE.sem, c.vE.val)
        ct = cw2.inc(se.dma_start(out=DR["cnt_d"][:], in_=pfxT[:, TPC - 16: TPC]), 16)
        ge.wait_ge(cw2.sem, ct)
        cc2 = c.cc.inc(ge.collective_compute(
            "AllGather", ALU.bypass, replica_groups=RG,
            ins=[DR["cnt_d"][:]], outs=[DR["cntg"][:]]))
        ge.wait_ge(c.cc.sem, cc2)
        ct2 = cw2.inc(ge.dma_start(out=allcnt[:], in_=DR["cntg"][:]), 16)
        te.wait_ge(cw2.sem, ct2)
        te.wait_ge(c.vE.sem, c.vE.val)
        te.wait_ge(c.aE.sem, c.aE.val)
        mm = te.matmul(p4[0:8, 0:16], selmat[:], allcnt[:], start=True, stop=True)
        c.tE.inc(mm)
        mm = te.matmul(p5[0:8, 0:16], premask[:], allcnt[:], start=True, stop=True)
        t5_ = c.tE.inc(mm)
        ve.wait_ge(c.tE.sem, t5_)
        vse(c, ve.tensor_copy(rscr[0:8, 0:16], p4[0:8, 0:16]))
        vse(c, ve.tensor_tensor_scan(
            sstat[0:8, 0:16] if False else sscr[0:8, 0:16],
            bass.AP(ones81[:].tensor, 0, [[ones81[:].ap[0][0], 8], [0, 16]]),
            rscr[0:8, 0:16], 0.0, ALU.mult, ALU.add))
        vse(c, ve.tensor_tensor(sscr[0:8, 0:16], sscr[0:8, 0:16], rscr[0:8, 0:16], ALU.subtract))
        vse(c, ve.tensor_tensor(sscr[0:8, 0:16], sscr[0:8, 0:16], p5[0:8, 0:16], ALU.add))
        vse(c, ve.tensor_scalar(sscr[0:8, 0:16], sscr[0:8, 0:16], -1.0, None, ALU.add))
        transpose_f32(sscr[0:8, 0:16], offlm[:], 5)
        cw3 = c.cnt("lfOff")
        se.wait_ge(c.aE.sem, c.aE.val)
        o1 = cw3.inc(se.dma_start(out=DR["off_d"][:], in_=offlm[:]), 16)
        se.wait_ge(cw3.sem, o1)
        for r8 in range(8):
            cw3.inc(se.dma_start(out=off_tile[r8 * 16:(r8 + 1) * 16, :], in_=DR["off_d"][:]), 16)
        ve.wait_ge(cw3.sem, cw3.val)

        cz = c.cnt("lfZ")
        vz = vse(c, ve.memset(sscr[:], 0.0))
        se.wait_ge(c.vE.sem, vz)
        zt = cz.inc(se.dma_start(out=DR["inv2_p"].rearrange("(p q) x -> p (q x)", p=128), in_=sscr[:]), 16)
        ge.wait_ge(cz.sem, zt)
        if "dbg3" in DR:
            cdbg3 = c.cnt("lfDbg3")
            cdbg3.inc(ge.dma_start(out=DR["dbg3"][128:256, 0:1024],
                                   in_=DR["inv2_p"].rearrange("(p q) x -> p (q x)", p=128)), 16)
            ge.wait_ge(cdbg3.sem, cdbg3.val)
        ge.wait_ge(cy1.sem, cy1.val)
        ccy1a_t = ccy1a.inc(ge.collective_compute(
            "AllGather", ALU.bypass, replica_groups=RG,
            ins=[DR["my_y1"][3072:4096, :]], outs=[DR["y1g"][24576:32768, :]]))
        for j in range(NT):
            transpose_f32(pfxT[:, j * 128:(j + 1) * 128], ptile[:], 4)
            ve.wait_ge(c.aE.sem, c.aE.val)
            vse(c, ve.tensor_tensor(ptile[:], ptile[:], off_tile[:], ALU.add))
            vse(c, ve.tensor_scalar(rscr[:, 3:11], eramp[:], route[:, j:j + 1], None, ALU.is_equal))
            vse(c, ve.tensor_tensor(rscr[:, 3:11], rscr[:, 3:11], ptile[:], ALU.mult))
            vse(c, ve.reduce_sum(rscr[:, 0:1], rscr[:, 3:11], AX.X))
            vse(c, ve.tensor_scalar(rscr[:, 1:2], rscr[:, 0:1], 4096.0, None, ALU.is_lt))
            vse(c, ve.tensor_scalar(rscr[:, 2:3], route[:, j:j + 1], 4096.0, None, ALU.mult))
            vse(c, ve.tensor_tensor(rscr[:, 2:3], rscr[:, 2:3], rscr[:, 0:1], ALU.add))
            vse(c, ve.tensor_scalar(rscr[:, 11:12], rscr[:, 1:2], -1e6, 1e6, ALU.mult, ALU.add))
            vse(c, ve.tensor_tensor(route[:, 64 + j: 64 + j + 1], rscr[:, 2:3], rscr[:, 11:12], ALU.add))
            vse(c, ve.tensor_copy(slu[:, j:j + 1], route[:, 64 + j: 64 + j + 1]))
            vse(c, ve.tensor_copy(sl2u[:, j:j + 1], rscr[:, 2:3]))
            vse(c, ve.tensor_copy(invu[:, 255:256], rscr[:, 11:12]))
            vse(c, ve.tensor_scalar(invu[:, 254:255], sl2u[:, j:j + 1], 127, None, ALU.bitwise_and))
            vse(c, ve.tensor_scalar(invu[:, 254:255], invu[:, 254:255], 8, None, ALU.logical_shift_left))
            vse(c, ve.tensor_scalar(sl2u[:, j:j + 1], sl2u[:, j:j + 1], 7, None, ALU.logical_shift_right))
            vse(c, ve.tensor_tensor(sl2u[:, j:j + 1], sl2u[:, j:j + 1], invu[:, 254:255], ALU.add))
            vse(c, ve.tensor_tensor(sl2u[:, j:j + 1], sl2u[:, j:j + 1], invu[:, 255:256], ALU.add))
            ge.wait_ge(c.vE.sem, c.vE.val)
            c.gE.inc(ge.indirect_dma_start(
                DR["inv2_p"][:], bass.IndirectOffsetOnAxis(sl2u[:, j:j + 1], 0),
                idsb[:, 4 * j: 4 * j + 4], None, bounds_check=r_bnd, oob_is_err=False), 16)
        ge.wait_ge(c.gE.sem, c.gE.val)
        if "dbg3" in DR:
            cdbg2 = c.cnt("lfDbg2")
            cdbg2.inc(ge.dma_start(out=DR["dbg3"][0:128, 0:1024],
                                   in_=DR["inv2_p"].rearrange("(p q) x -> p (q x)", p=128)), 16)
            ge.wait_ge(cdbg2.sem, cdbg2.val)
        cc4 = c.cc.inc(ge.collective_compute(
            "AllGather", ALU.bypass, replica_groups=RG,
            ins=[DR["inv2_p"].rearrange("(p q) x -> p (q x)", p=128)],
            outs=[DR["inv2_all"].rearrange("(p q) x -> p (q x)", p=1024)]))
        ge.wait_ge(c.cc.sem, cc4)
        if "dbg3" in DR:
            cdbg4 = c.cnt("lfDbg4")
            cdbg4.inc(ge.dma_start(out=DR["dbg3"][256:384, 0:1024],
                                   in_=DR["inv2_all"][0:32768, :].rearrange("(p q) x -> p (q x)", p=128)), 16)
            cdbg4.inc(ge.dma_start(out=DR["dbg3"][384:512, 0:1024],
                                   in_=DR["inv2_all"][3 * 32768:4 * 32768, :].rearrange("(p q) x -> p (q x)", p=128)), 16)
            ge.wait_ge(cdbg4.sem, cdbg4.val)
        cw4 = c.cnt("lfInv")
        invs, invs2 = y1pre, sxc
        se.wait_ge(c.cc.sem, cc4)
        i0 = cw4.inc(se.dma_start(out=invs[:], in_=DR["inv2_all"][0:32768, :].rearrange("(p q) x -> p (q x)", p=128)), 16)
        ve.wait_ge(cw4.sem, i0)
        for r in range(1, 8):
            ik = cw4.inc(se.dma_start(out=invs2[:], in_=DR["inv2_all"][r * 32768:(r + 1) * 32768, :].rearrange("(p q) x -> p (q x)", p=128)), 16)
            ve.wait_ge(cw4.sem, ik)
            vse(c, ve.tensor_tensor(invs[:], invs[:], invs2[:], ALU.add))
            if r < 7:
                se.wait_ge(c.vE.sem, c.vE.val)
        cw5 = c.cnt("lfInvW")
        se.wait_ge(c.vE.sem, c.vE.val)
        iw = cw5.inc(se.dma_start(out=DR["inv_sum_d"].rearrange("(p q) x -> p (q x)", p=128), in_=invs[:]), 16)
        ge.wait_ge(cw5.sem, iw)

        ge.wait_ge(ccy1a.sem, ccy1a_t)
        ge.wait_ge(c.vE.sem, c.vE.val)
        cgd = c.cnt("lfDisp")
        cwE = c.cnt("lfWE")
        chS = c.cnt("lfHS")
        wticks = {0: 0, 1: 0}
        wcons = {0: 0, 1: 0}
        hdone = {0: 0, 1: 0}

        def load_w1(oc, b):
            if wcons[b]:
                ge.wait_ge(c.tE.sem, wcons[b])
            cwb = c.cnt(f"lfW1{b}")
            cwb.inc(ge.dma_start(
                out=bass.AP(wB[:].tensor, b * 4096, [[wB[:].ap[0][0], 128], [512, 8], [1, 128]]),
                in_=DR["w1c"][:, oc * 128:(oc + 1) * 128].rearrange("(k p) f -> p k f", p=128)), 16)
            return cwb.val

        for half in range(2):
            for j in range(16):
                jj = half * 16 + j
                gi = cgd.inc(ge.indirect_dma_start(
                    oev[:, 0:4], None, DR["inv_sum_d"][:],
                    bass.IndirectOffsetOnAxis(invsel[:, jj:jj + 1], 0),
                    bounds_check=r_bnd, oob_is_err=False), 16)
                ve.wait_ge(cgd.sem, gi)
                vt_ = c.vE.inc(ve.tensor_copy(invu[:, jj:jj + 1], oev[:, 0:1]))
                ge.wait_ge(c.vE.sem, vt_)
                gk = cgd.inc(ge.indirect_dma_start(
                    gbuf[:], None, DR["y1g"][:],
                    bass.IndirectOffsetOnAxis(invu[:, jj:jj + 1], 0),
                    bounds_check=r_bnd, oob_is_err=False), 16)
                te.wait_ge(cgd.sem, gk)
                for k in range(8):
                    transpose_bf(gbuf[:, k * 128:(k + 1) * 128],
                                 dispfm[:, k * 2048 + j * 128: k * 2048 + (j + 1) * 128], k % 2)
                ge.wait_ge(c.tE.sem, c.tE.val)
            wticks[0] = load_w1(0, 0)
            wticks[1] = load_w1(1, 1)
            for oc in range(32):
                b = oc % 2
                te.wait_ge(c.cnt(f"lfW1{b}").sem, wticks[b])
                for tb in range(4):
                    pp = [p0, p1][tb % 2]
                    te.wait_ge(c.vE.sem, c.vE.val)
                    for k in range(8):
                        mm = te.matmul(pp[:], wB[:, b * 4096 + k * 512: b * 4096 + k * 512 + 128],
                                       dispfm[:, k * 2048 + tb * 512: k * 2048 + (tb + 1) * 512],
                                       start=(k == 0), stop=(k == 7))
                    th = c.tE.inc(mm)
                    ve.wait_ge(c.tE.sem, th)
                    hb_ = hstg[tb % 2]
                    if hdone[tb % 2]:
                        ve.wait_ge(chS.sem, hdone[tb % 2])
                    at6 = c.vE.inc(ve.tensor_scalar(hb_[:], pp[:], b1r[:, oc:oc + 1], None, ALU.add))
                    se.wait_ge(c.vE.sem, at6)
                    hdone[tb % 2] = chS.inc(se.dma_start(
                        out=DR["h_d"][oc, :, half * 2048 + tb * 512: half * 2048 + (tb + 1) * 512], in_=hb_[:]), 16)
                wcons[b] = c.tE.val
                if oc + 2 < 32:
                    wticks[b] = load_w1(oc + 2, b)
        se.wait_ge(chS.sem, chS.val)
        cyS = c.cnt("lfYS")
        chL = c.cnt("lfHL")
        for dh in range(2):
            ge.wait_ge(c.tE.sem, c.tE.val)
            wt = cwE.inc(ge.dma_start(
                out=bass.AP(wA[:].tensor, 0, [[wA[:].ap[0][0], 128], [512, 32], [1, 512]]),
                in_=DR["w2c"][:, dh * 512:(dh + 1) * 512].rearrange("(k p) d -> p k d", p=128)), 16)
            te.wait_ge(cwE.sem, wt)
            hload = {0: 0, 1: 0}
            hcons = {0: c.tE.val, 1: c.tE.val}
            for tt in range(32):
                hb2 = tt % 2
                hbuf_cols = hb2 * 4096
                if tt < 2:
                    se.wait_ge(c.tE.sem, hcons[hb2])
                    hload[hb2] = chL.inc(se.dma_start(
                        out=bass.AP(dispfm[:].tensor, hbuf_cols, [[dispfm[:].ap[0][0], 128], [128, 32], [1, 128]]),
                        in_=bass.AP(DR["h_d"].tensor, tt * 128, [[4096, 128], [128 * 4096, 32], [1, 128]])), 16)
                te.wait_ge(chL.sem, hload[hb2])
                if tt + 2 < 32:
                    nb = (tt + 2) % 2
                    se.wait_ge(c.tE.sem, hcons[nb] if hcons[nb] else 0)
                te.wait_ge(c.vE.sem, c.vE.val)
                pp = [p2, p3][tt % 2]
                for k in range(32):
                    mm = te.matmul(pp[:], dispfm[:, hbuf_cols + k * 128: hbuf_cols + (k + 1) * 128],
                                   wA[:, k * 512:(k + 1) * 512], start=(k == 0), stop=(k == 31))
                ty = c.tE.inc(mm)
                hcons[hb2] = ty
                if tt + 2 < 32:
                    nb2 = (tt + 2) % 2
                    se.wait_ge(c.tE.sem, hcons[nb2])
                    hload[nb2] = chL.inc(se.dma_start(
                        out=bass.AP(dispfm[:].tensor, nb2 * 4096, [[dispfm[:].ap[0][0], 128], [128, 32], [1, 128]]),
                        in_=bass.AP(DR["h_d"].tensor, (tt + 2) * 128, [[4096, 128], [128 * 4096, 32], [1, 128]])), 16)
                ve.wait_ge(c.tE.sem, ty)
                if tt >= 2:
                    ve.wait_ge(cyS.sem, ystore[tt % 2])
                vt = c.vE.inc(ve.tensor_tensor(hstg[tt % 2][:], pp[:], b2rep[:, dh * 512:(dh + 1) * 512], ALU.add))
                se.wait_ge(c.vE.sem, vt)
                if tt == 0 and dh == 0:
                    ystore = {0: 0, 1: 0}
                ystore[tt % 2] = cyS.inc(se.dma_start(
                    out=[DR["yexp_dA"], DR["yexp_dB"]][dh][tt * 128:(tt + 1) * 128, :], in_=hstg[tt % 2][:]), 16)
            if dh == 0:
                ge.wait_ge(cyS.sem, cyS.val)
                ccyea = c.cnt("ccYeA")
                ccye_t = ccyea.inc(ge.collective_compute(
                    "AllGather", ALU.bypass, replica_groups=RG,
                    ins=[DR["yexp_dA"][:]], outs=[DR["yexp_cA"][:]]))
        ge.wait_ge(cyS.sem, cyS.val)
        cc5 = c.cc.inc(ge.collective_compute(
            "AllGather", ALU.bypass, replica_groups=RG,
            ins=[DR["yexp_dB"][:]], outs=[DR["yexp_cB"][:]]))
        ge.wait_ge(c.cc.sem, cc5)
        ge.wait_ge(ccyea.sem, ccye_t)

        cg5 = c.cnt("lfP5G")
        cy2 = c.cnt("lfY2S")
        cyl = c.cnt("lfY1L")
        y2_done = {}
        for j in range(NT):
            ve.wait_ge(c.tE.sem, c.tE.val)
            vm = vse(c, ve.memset(gbuf[:], 0.0))
            ge.wait_ge(c.vE.sem, vm)
            cg5.inc(ge.indirect_dma_start(
                gbuf[:, 0:512], None, DR["yexp_cA"][:], bass.IndirectOffsetOnAxis(slu[:, j:j + 1], 0),
                bounds_check=r_bnd, oob_is_err=False), 16)
            gk = cg5.inc(ge.indirect_dma_start(
                gbuf[:, 512:1024], None, DR["yexp_cB"][:], bass.IndirectOffsetOnAxis(slu[:, j:j + 1], 0),
                bounds_check=r_bnd, oob_is_err=False), 16)
            yk = cyl.inc(se.dma_start(out=y1t[:], in_=DR["my_y1"][j * 128:(j + 1) * 128, :]), 16)
            ve.wait_ge(cg5.sem, gk)
            ve.wait_ge(cyl.sem, yk)
            vse(c, ve.scalar_tensor_tensor(y1pre[:], gbuf[:], route[:, 32 + j: 32 + j + 1], y1t[:], ALU.mult, ALU.add))
            if j >= 1:
                ve.wait_ge(cy2.sem, y2_done[j - 1])
            lt = emit_ln_gb(c, y1pre, sxc, sscr, sstat, ln2g, ln2b, sbf[:])
            se.wait_ge(c.vE.sem, lt)
            y2_done[j] = cy2.inc(se.dma_start(out=DR["y2_d"][j * 128:(j + 1) * 128, :], in_=sbf[:]), 16)
            if j in (7, 15, 23):
                q = j // 8
                ge.wait_ge(cy2.sem, y2_done[j])
                if q == 0:
                    ccy2a = c.cnt("ccY2a")
                    ccy2b = c.cnt("ccY2b")
                cq = ccy2a if q < 2 else ccy2b
                _t = cq.inc(ge.collective_compute(
                    "AllGather", ALU.bypass, replica_groups=RG,
                    ins=[DR["y2_d"][q * 1024:(q + 1) * 1024, :]],
                    outs=[DR["y2_all"][q * 8192:(q + 1) * 8192, :]]))
                if q == 1:
                    ccY1 = _t
        ge.wait_ge(cy2.sem, cy2.val)
        cc6 = ccy2b.inc(ge.collective_compute(
            "AllGather", ALU.bypass, replica_groups=RG,
            ins=[DR["y2_d"][3072:4096, :]], outs=[DR["y2_all"][24576:32768, :]]))

        if "dbg1" in DR:
            cdbg = c.cnt("lfDbg")
            ge.wait_ge(c.vE.sem, c.vE.val)
            ge.wait_ge(c.aE.sem, c.aE.val)
            cdbg.inc(ge.dma_start(out=DR["dbg1"][:], in_=DR["my_y1"][:]), 16)
            cdbg.inc(ge.dma_start(out=DR["dbg2"][0:128, 0:96], in_=route[:]), 16)
            cdbg.inc(ge.dma_start(out=DR["dbg2"][128:256, 0:8], in_=off_tile[:]), 16)
            cdbg.inc(ge.dma_start(out=DR["dbg2"][256:264, 0:2048], in_=pfxT[:, 0:2048]), 16)
            cdbg.inc(ge.dma_start(out=DR["dbg2"][264:392, 0:256], in_=invu[:]), 16)
            cdbg.inc(ge.dma_start(out=DR["dbg3"][512:4096 + 512 - 512, :], in_=DR["yexp_d"][0:3584, :]), 16)
            cdbg.inc(ge.dma_start(out=DR["dbg4"][:], in_=DR["y2_d"][:]), 16)
            ge.wait_ge(cdbg.sem, cdbg.val)
        cg6 = c.cnt("lfP6G")
        cs6 = c.cnt("lfP6S")
        x6_done = {}
        p6order = [j for j in range(NT) if (j % 16) % 2 == 0] + [j for j in range(NT) if (j % 16) % 2 == 1]
        ge.wait_ge(ccy2a.sem, ccY1)
        for i, j in enumerate(p6order):
            bb = i % 2
            gb6 = dispfm[:, bb * 1024:(bb + 1) * 1024]
            cx = c.cnt(f"lfX{bb}")
            if i == 16:
                ge.wait_ge(ccy2b.sem, cc6)
            if i >= 2:
                se.wait_ge(cs6.sem, x6_done[i - 2])
                ge.wait_ge(c.vE.sem, c.vE.val)
            tk = cx.inc(se.dma_start(out=sx[bb][:], in_=x_in[j * 128:(j + 1) * 128, :]), 16)
            gk = cg6.inc(ge.indirect_dma_start(
                gb6, None, DR["y2_all"][:], bass.IndirectOffsetOnAxis(p6idx[:, j:j + 1], 0),
                bounds_check=r_bnd, oob_is_err=False), 16)
            ve.wait_ge(cx.sem, tk)
            ve.wait_ge(cg6.sem, gk)
            vt = c.vE.inc(ve.scalar_tensor_tensor(sx[bb][:], gb6, 0.5, sx[bb][:], ALU.mult, ALU.add))
            se.wait_ge(c.vE.sem, vt)
            x6_done[i] = cs6.inc(se.dma_start(out=x_out[j * 128:(j + 1) * 128, :], in_=sx[bb][:]), 16)
        ge.wait_ge(cs6.sem, cs6.val)


def _prep_l1ffn_weights(inputs, core):
    f = np.float32
    bf = ml_dtypes.bfloat16
    W = {}
    W["inwT"] = np.ascontiguousarray(np.asarray(inputs["L1_ffn_in_proj_w"], f).T)
    W["outwT"] = np.ascontiguousarray(np.asarray(inputs["L1_ffn_out_proj_w"], f).T)
    W["wgT"] = np.ascontiguousarray(np.asarray(inputs["L1_ffn_wg"], f))
    W["w1c"] = np.ascontiguousarray(np.asarray(inputs["L1_ffn_w1"], f)[core])
    W["w2c"] = np.ascontiguousarray(np.asarray(inputs["L1_ffn_w2"], f)[core])
    b1 = np.asarray(inputs["L1_ffn_b1"], f)[core]
    W["b1r"] = np.ascontiguousarray(b1.reshape(32, 128).T)
    W["b2_rep"] = np.ascontiguousarray(np.broadcast_to(np.asarray(inputs["L1_ffn_b2"], f)[core], (128, 1024))).astype(bf)
    W["inb_rep"] = np.ascontiguousarray(np.broadcast_to(np.asarray(inputs["L1_ffn_in_proj_b"], f), (128, 3072))).astype(bf)
    W["outb_rep"] = np.ascontiguousarray(np.broadcast_to(np.asarray(inputs["L1_ffn_out_proj_b"], f), (128, 1024))).astype(bf)
    for nm, key in [("ln1g_rep", "L1_ffn_ln1_g"), ("ln1b_rep", "L1_ffn_ln1_b"),
                    ("ln2g_rep", "L1_ffn_ln2_g"), ("ln2b_rep", "L1_ffn_ln2_b")]:
        W[nm] = np.ascontiguousarray(np.broadcast_to(np.asarray(inputs[key], f), (128, 1024)))
    W["id_bf16"] = np.eye(128, dtype=bf)
    W["id_f32"] = np.eye(128, dtype=f)
    W["ones1"] = np.ones((1, 128), f)
    t_ = np.arange(128)
    W["blockmask"] = ((t_[:, None] // 16) == (t_[None, :] // 16)).astype(bf)
    W["eramp_rep"] = np.ascontiguousarray(np.broadcast_to(np.arange(8, dtype=f), (128, 8)))
    ce = np.arange(64)
    W["selmat"] = np.eye(8, dtype=f)[ce % 8]
    W["premask"] = (np.eye(8, dtype=f)[ce % 8]) * ((ce // 8) < core)[:, None]
    def glob(l, n):
        return (l // 2) * 4096 + (l % 2) * 2048 + n
    p2 = np.zeros((128, 32), np.uint32)
    for p in range(128):
        n_sub, l = p // 16, p % 16
        for j in range(32):
            p2[p, j] = glob(l, 256 * core + j * 8 + n_sub)
    p2 = ((p2 % 4096) // 1024) * 8192 + (p2 // 4096) * 1024 + (p2 % 1024)
    W["p2idx"] = p2
    p6 = np.zeros((128, 32), np.uint32)
    for j in range(32):
        b_loc = j // 16
        l = 2 * core + b_loc
        for p in range(128):
            n = (j % 16) * 128 + p
            r = n // 256
            p6[p, j] = r * 4096 + (n % 256) * 16 + l
    p6 = ((p6 % 4096) // 1024) * 8192 + (p6 // 4096) * 1024 + (p6 % 1024)
    W["p6idx"] = p6
    ids = np.zeros((128, 128), f)
    for j in range(32):
        loc = j * 128 + np.arange(128)
        for q in range(4):
            ids[:, 4 * j + q] = (loc // 1024) * 8192 + core * 1024 + (loc % 1024)
    W["ids_base"] = ids
    isel = np.zeros((128, 32), np.uint32)
    for j in range(32):
        isel[:, j] = np.arange(128) * 256 + core * 32 + j
    W["invsel"] = isel
    return W


L1FFN_W_KEYS = ["inwT", "outwT", "wgT", "w1c", "w2c", "b1r", "b2_rep", "inb_rep",
                "outb_rep", "ln1g_rep", "ln1b_rep", "ln2g_rep", "ln2b_rep",
                "id_bf16", "id_f32", "ones1", "blockmask", "eramp_rep",
                "selmat", "premask", "p2idx", "p6idx", "ids_base", "invsel"]


def build_l1ffn_program(dbg=False):
    nc = bass.Bass("TRN2", target_bir_lowering=False)
    x_in = nc.dram_tensor("x", [TPC, D], F32, kind="ExternalInput").ap()
    x_out = nc.dram_tensor("xo", [TPC, D], F32, kind="ExternalOutput").ap()
    Wd = {
        "inwT": nc.dram_tensor("inwT", [D, 3072], F32, kind="ExternalInput").ap(),
        "outwT": nc.dram_tensor("outwT", [D, D], F32, kind="ExternalInput").ap(),
        "wgT": nc.dram_tensor("wgT", [D, 8], F32, kind="ExternalInput").ap(),
        "w1c": nc.dram_tensor("w1c", [D, 4096], F32, kind="ExternalInput").ap(),
        "w2c": nc.dram_tensor("w2c", [4096, D], F32, kind="ExternalInput").ap(),
        "b1r": nc.dram_tensor("b1r", [128, 32], F32, kind="ExternalInput").ap(),
        "b2_rep": nc.dram_tensor("b2_rep", [128, 1024], BF16, kind="ExternalInput").ap(),
        "inb_rep": nc.dram_tensor("inb_rep", [128, 3072], BF16, kind="ExternalInput").ap(),
        "outb_rep": nc.dram_tensor("outb_rep", [128, 1024], BF16, kind="ExternalInput").ap(),
        "ln1g_rep": nc.dram_tensor("ln1g_rep", [128, 1024], F32, kind="ExternalInput").ap(),
        "ln1b_rep": nc.dram_tensor("ln1b_rep", [128, 1024], F32, kind="ExternalInput").ap(),
        "ln2g_rep": nc.dram_tensor("ln2g_rep", [128, 1024], F32, kind="ExternalInput").ap(),
        "ln2b_rep": nc.dram_tensor("ln2b_rep", [128, 1024], F32, kind="ExternalInput").ap(),
        "id_bf16": nc.dram_tensor("id_bf16", [128, 128], BF16, kind="ExternalInput").ap(),
        "id_f32": nc.dram_tensor("id_f32", [128, 128], F32, kind="ExternalInput").ap(),
        "ones1": nc.dram_tensor("ones1", [1, 128], F32, kind="ExternalInput").ap(),
        "blockmask": nc.dram_tensor("blockmask", [128, 128], BF16, kind="ExternalInput").ap(),
        "eramp_rep": nc.dram_tensor("eramp_rep", [128, 8], F32, kind="ExternalInput").ap(),
        "selmat": nc.dram_tensor("selmat", [64, 8], F32, kind="ExternalInput").ap(),
        "premask": nc.dram_tensor("premask", [64, 8], F32, kind="ExternalInput").ap(),
        "p2idx": nc.dram_tensor("p2idx", [128, 32], U32, kind="ExternalInput").ap(),
        "p6idx": nc.dram_tensor("p6idx", [128, 32], U32, kind="ExternalInput").ap(),
        "ids_base": nc.dram_tensor("ids_base", [128, 128], F32, kind="ExternalInput").ap(),
        "invsel": nc.dram_tensor("invsel", [128, 32], U32, kind="ExternalInput").ap(),
        "my_xln": nc.dram_tensor("my_xln", [TPC, D], BF16).ap(),
        "xg_all": nc.dram_tensor("xg_all", [8 * TPC, D], BF16).ap(),
        "my_y1": nc.dram_tensor("my_y1", [TPC, D], BF16).ap(),
        "y1g": nc.dram_tensor("y1g", [8 * TPC, D], BF16).ap(),
        "cnt_d": nc.dram_tensor("cnt_d", [8, 16], F32).ap(),
        "cntg": nc.dram_tensor("cntg", [64, 16], F32).ap(),
        "off_d": nc.dram_tensor("off_d", [16, 8], F32).ap(),
        "inv2_p": nc.dram_tensor("inv2_p", [32768, 4], F32).ap(),
        "inv2_all": nc.dram_tensor("inv2_all", [8 * 32768, 4], F32).ap(),
        "inv_sum_d": nc.dram_tensor("inv_sum_d", [32768, 4], F32).ap(),
        "h_d": nc.dram_tensor("h_d", [32, 128, 4096], BF16).ap(),
        "yexp_dA": nc.dram_tensor("yexp_dA", [4096, 512], BF16).ap(),
        "yexp_dB": nc.dram_tensor("yexp_dB", [4096, 512], BF16).ap(),
        "yexp_cA": nc.dram_tensor("yexp_cA", [8 * 4096, 512], BF16).ap(),
        "yexp_cB": nc.dram_tensor("yexp_cB", [8 * 4096, 512], BF16).ap(),
        "y2_d": nc.dram_tensor("y2_d", [TPC, D], BF16).ap(),
        "y2_all": nc.dram_tensor("y2_all", [8 * TPC, D], BF16).ap(),
    }
    if dbg:
        Wd["dbg1"] = nc.dram_tensor("dbg1", [TPC, D], F32, kind="ExternalOutput").ap()
        Wd["dbg2"] = nc.dram_tensor("dbg2", [392, 2048], F32, kind="ExternalOutput").ap()
        Wd["dbg3"] = nc.dram_tensor("dbg3", [4096, D], F32, kind="ExternalOutput").ap()
        Wd["dbg4"] = nc.dram_tensor("dbg4", [TPC, D], F32, kind="ExternalOutput").ap()
        Wd["dbg5"] = nc.dram_tensor("dbg5", [256, 2048], F32, kind="ExternalOutput").ap()
    with ExitStack() as ctx:
        c = Ctx(nc, ctx)
        block = ctx.enter_context(nc.Block())

        @block.sync
        def _(sync):
            emit_l1ffn(c, x_in, x_out, Wd)
    return nc


def run_l1ffn(x_full, Ws, nc):
    ims = []
    for core in range(NCORES):
        xs = np.ascontiguousarray(x_full[2 * core:2 * core + 2].reshape(TPC, D))
        im = {"x": xs}
        for k in L1FFN_W_KEYS:
            im[k] = Ws[core][k]
        ims.append(im)
    res = run_bass_kernel_spmd(nc, ims, core_ids=list(range(NCORES)))
    out = np.empty((B, L, D), np.float32)
    for core in range(NCORES):
        out[2 * core:2 * core + 2] = np.asarray(res.results[core]["xo"]).reshape(2, L, D)
    return out





def phase_barrier(c):
    cnts = [c.dW, c.dX, c.dS, c.tE, c.vE, c.aE, c.gE, c.cc]
    if hasattr(c, "_cnts"):
        cnts += list(c._cnts.values())
    for cn in cnts:
        c.ge.wait_ge(cn.sem, cn.val)
    t = c.gE.inc(c.ge.nop())
    for eng in (c.te, c.ve, c.ae, c.se):
        eng.wait_ge(c.gE.sem, t)


def build_fused_program():
    nc = bass.Bass("TRN2", target_bir_lowering=False)
    x0 = nc.dram_tensor("x", [TPC, D], F32, kind="ExternalInput").ap()
    xo = nc.dram_tensor("xo", [TPC, D], F32, kind="ExternalOutput").ap()
    xA = nc.dram_tensor("xA", [TPC, D], F32).ap()
    xB = nc.dram_tensor("xB", [TPC, D], F32).ap()
    xC = nc.dram_tensor("xC", [TPC, D], F32).ap()
    xD = nc.dram_tensor("xD", [TPC, D], F32).ap()
    xE = nc.dram_tensor("xE", [TPC, D], F32).ap()

    WA = {
        "gwT": nc.dram_tensor("gwT", [D, 16], F32, kind="ExternalInput").ap(),
        "ebias": nc.dram_tensor("ebias", [128, 16], F32, kind="ExternalInput").ap(),
        "wgT": nc.dram_tensor("wgT", [16, D, D], F32, kind="ExternalInput").ap(),
        "wuT": nc.dram_tensor("wuT", [16, D, D], F32, kind="ExternalInput").ap(),
        "wdT": nc.dram_tensor("wdT", [16, D, D], F32, kind="ExternalInput").ap(),
        "sgT": nc.dram_tensor("sgT", [D, D], F32, kind="ExternalInput").ap(),
        "suT": nc.dram_tensor("suT", [D, D], F32, kind="ExternalInput").ap(),
        "sdT": nc.dram_tensor("sdT", [D, D], F32, kind="ExternalInput").ap(),
        "id_bf16": nc.dram_tensor("id_bf16", [128, 128], BF16, kind="ExternalInput").ap(),
        "ones1": nc.dram_tensor("ones1", [1, 128], F32, kind="ExternalInput").ap(),
    }
    WB = {
        "inprojT": nc.dram_tensor("inprojT", [D, 4384], F32, kind="ExternalInput").ap(),
        "opT": nc.dram_tensor("opT", [2048, 1024], F32, kind="ExternalInput").ap(),
        "convw": nc.dram_tensor("convw", [2304, 4], F32, kind="ExternalInput").ap(),
        "convb_r": nc.dram_tensor("convb_r", [128, 18], F32, kind="ExternalInput").ap(),
        "dtb_rep": nc.dram_tensor("dtb_rep", [128, 32], F32, kind="ExternalInput").ap(),
        "alog_rep": nc.dram_tensor("alog_rep", [128, 32], F32, kind="ExternalInput").ap(),
        "d_rep": nc.dram_tensor("d_rep", [128, 32], F32, kind="ExternalInput").ap(),
        "normw_rep": nc.dram_tensor("normw_rep", [128, 2048], BF16, kind="ExternalInput").ap(),
        "trmask": nc.dram_tensor("trmask", [128, 512], BF16, kind="ExternalInput").ap(),
        "id_bf16": WA["id_bf16"], "id_f32": nc.dram_tensor("id_f32", [128, 128], F32, kind="ExternalInput").ap(),
        "ones1": WA["ones1"],
        "acs_dram": nc.dram_tensor("acs_dram", [32, 256], F32).ap(),
    }
    WC = {
        "qkvT": nc.dram_tensor("qkvT", [D, 3072], F32, kind="ExternalInput").ap(),
        "outT": nc.dram_tensor("outT", [D, D], F32, kind="ExternalInput").ap(),
        "cosh": nc.dram_tensor("cosh", [128, 512], F32, kind="ExternalInput").ap(),
        "sinh": nc.dram_tensor("sinh", [128, 512], F32, kind="ExternalInput").ap(),
        "id_bf16": WA["id_bf16"], "ones1": WA["ones1"],
    }
    WD = {
        "inwT": nc.dram_tensor("inwT", [D, 3072], F32, kind="ExternalInput").ap(),
        "outwT": nc.dram_tensor("outwT", [D, D], F32, kind="ExternalInput").ap(),
        "wgTl1": nc.dram_tensor("wgTl1", [D, 8], F32, kind="ExternalInput").ap(),
        "w1c": nc.dram_tensor("w1c", [D, 4096], F32, kind="ExternalInput").ap(),
        "w2c": nc.dram_tensor("w2c", [4096, D], F32, kind="ExternalInput").ap(),
        "b1r": nc.dram_tensor("b1r", [128, 32], F32, kind="ExternalInput").ap(),
        "b2_rep": nc.dram_tensor("b2_rep", [128, 1024], BF16, kind="ExternalInput").ap(),
        "inb_rep": nc.dram_tensor("inb_rep", [128, 3072], BF16, kind="ExternalInput").ap(),
        "outb_rep": nc.dram_tensor("outb_rep", [128, 1024], BF16, kind="ExternalInput").ap(),
        "ln1g_rep": nc.dram_tensor("ln1g_rep", [128, 1024], F32, kind="ExternalInput").ap(),
        "ln1b_rep": nc.dram_tensor("ln1b_rep", [128, 1024], F32, kind="ExternalInput").ap(),
        "ln2g_rep": nc.dram_tensor("ln2g_rep", [128, 1024], F32, kind="ExternalInput").ap(),
        "ln2b_rep": nc.dram_tensor("ln2b_rep", [128, 1024], F32, kind="ExternalInput").ap(),
        "id_bf16": WA["id_bf16"], "id_f32": WB["id_f32"], "ones1": WA["ones1"],
        "blockmask": nc.dram_tensor("blockmask", [128, 128], BF16, kind="ExternalInput").ap(),
        "eramp_rep": nc.dram_tensor("eramp_rep", [128, 8], F32, kind="ExternalInput").ap(),
        "selmat": nc.dram_tensor("selmat", [64, 8], F32, kind="ExternalInput").ap(),
        "premask": nc.dram_tensor("premask", [64, 8], F32, kind="ExternalInput").ap(),
        "p2idx": nc.dram_tensor("p2idx", [128, 32], U32, kind="ExternalInput").ap(),
        "p6idx": nc.dram_tensor("p6idx", [128, 32], U32, kind="ExternalInput").ap(),
        "ids_base": nc.dram_tensor("ids_base", [128, 128], F32, kind="ExternalInput").ap(),
        "invsel": nc.dram_tensor("invsel", [128, 32], U32, kind="ExternalInput").ap(),
        "my_xln": nc.dram_tensor("my_xln", [TPC, D], BF16).ap(),
        "xg_all": nc.dram_tensor("xg_all", [8 * TPC, D], BF16).ap(),
        "my_y1": nc.dram_tensor("my_y1", [TPC, D], BF16).ap(),
        "y1g": nc.dram_tensor("y1g", [8 * TPC, D], BF16).ap(),
        "cnt_d": nc.dram_tensor("cnt_d", [8, 16], F32).ap(),
        "cntg": nc.dram_tensor("cntg", [64, 16], F32).ap(),
        "off_d": nc.dram_tensor("off_d", [16, 8], F32).ap(),
        "inv2_p": nc.dram_tensor("inv2_p", [32768, 4], F32).ap(),
        "inv2_all": nc.dram_tensor("inv2_all", [8 * 32768, 4], F32).ap(),
        "inv_sum_d": nc.dram_tensor("inv_sum_d", [32768, 4], F32).ap(),
        "h_d": nc.dram_tensor("h_d", [32, 128, 4096], BF16).ap(),
        "yexp_dA": nc.dram_tensor("yexp_dA", [4096, 512], BF16).ap(),
        "yexp_dB": nc.dram_tensor("yexp_dB", [4096, 512], BF16).ap(),
        "yexp_cA": nc.dram_tensor("yexp_cA", [8 * 4096, 512], BF16).ap(),
        "yexp_cB": nc.dram_tensor("yexp_cB", [8 * 4096, 512], BF16).ap(),
        "y2_d": nc.dram_tensor("y2_d", [TPC, D], BF16).ap(),
        "y2_all": nc.dram_tensor("y2_all", [8 * TPC, D], BF16).ap(),
    }
    WD["wgT"] = WD["wgTl1"]

    with ExitStack() as ctx:
        c = Ctx(nc, ctx)
        block = ctx.enter_context(nc.Block())

        @block.sync
        def _(sync):
            emit_l0ffn(c, x0, xA, WA)
            phase_barrier(c)
            emit_mamba(c, xA, xB, WB)
            phase_barrier(c)
            emit_l0ffn(c, xB, xC, WA)
            phase_barrier(c)
            emit_l1ffn(c, xC, xD, WD)
            phase_barrier(c)
            emit_l1attn(c, xD, xE, WC)
            phase_barrier(c)
            emit_l1ffn(c, xE, xo, WD)
    return nc


FUSED_IN_KEYS_A = L0FFN_W_KEYS
FUSED_IN_KEYS_B = [k for k in MAMBA_W_KEYS if k not in ("id_bf16", "ones1")]
FUSED_IN_KEYS_C = [k for k in L1ATTN_W_KEYS if k not in ("id_bf16", "ones1")]
FUSED_IN_KEYS_D = [k for k in L1FFN_W_KEYS if k not in ("id_bf16", "id_f32", "ones1", "wgT")]


def kernel(**inputs):
    x = np.ascontiguousarray(np.asarray(inputs["L0_attn_u"], np.float32))
    Wff0 = _prep_l0ffn_weights(inputs)
    Wmb = _prep_mamba_weights(inputs)
    Wat = _prep_l1attn_weights(inputs)
    Wlf = [_prep_l1ffn_weights(inputs, core) for core in range(NCORES)]
    nc = build_fused_program()
    ims = []
    for core in range(NCORES):
        im = {"x": np.ascontiguousarray(x[2 * core:2 * core + 2].reshape(TPC, D))}
        for k in FUSED_IN_KEYS_A:
            im[k] = Wff0[k]
        for k in FUSED_IN_KEYS_B:
            im[k] = Wmb[k]
        for k in FUSED_IN_KEYS_C:
            im[k] = Wat[k]
        for k in FUSED_IN_KEYS_D:
            im[k] = Wlf[core][k]
        im["wgTl1"] = Wlf[core]["wgT"]
        ims.append(im)
    res = run_bass_kernel_spmd(nc, ims, core_ids=list(range(NCORES)))
    out = np.empty((B, L, D), np.float32)
    for core in range(NCORES):
        out[2 * core:2 * core + 2] = np.asarray(res.results[core]["xo"]).reshape(2, L, D)
    return out
